# version 07
# baseline (speedup 1.0000x reference)
import numpy as np
from contextlib import ExitStack

import concourse.bass as bass
import concourse.mybir as mybir
from concourse.bass_utils import run_bass_kernel_spmd

F32 = mybir.dt.float32
BF16 = mybir.dt.bfloat16
AF = mybir.ActivationFunctionType
ALU = mybir.AluOpType
AX = mybir.AxisListType

D = 1024
HF = 4096
NT = 16
NTOK = NT * 128
LN_EPS = 1e-5
CTX_EPS = 1e-6
ABS_EPS = 1e-6

ENGINES = ("sync", "gpsimd", "vector", "scalar", "tensor")


class Builder:
    def __init__(self):
        nc = bass.Bass("TRN2", target_bir_lowering=False)
        self.nc = nc
        self.es = ExitStack()
        self.cnt = {}
        self.items = None
        self.ps_cons = {}
        self.ps_prod = {}
        self.misc = {}
        self.x_free = {}

        dt = nc.dram_tensor
        self.x_d = dt("x", [NTOK, D], F32, kind="ExternalInput").ap()
        self.xprev_d = dt("xprev", [NTOK, D], F32, kind="ExternalInput").ap()
        self.mc_d = dt("mc", [128, 1], F32, kind="ExternalInput").ap()
        self.mb_d = dt("mb", [128, 1], F32, kind="ExternalInput").ap()
        self.wq_d = dt("wq", [D, D], F32, kind="ExternalInput").ap()
        self.wk_d = dt("wk", [D, D], F32, kind="ExternalInput").ap()
        self.wv_d = dt("wv", [D, D], F32, kind="ExternalInput").ap()
        self.wo_d = dt("wo", [D, D], F32, kind="ExternalInput").ap()
        self.bo_d = dt("bo", [1, D], F32, kind="ExternalInput").ap()
        self.wfc_d = dt("wfc", [HF, D], F32, kind="ExternalInput").ap()
        self.lnw_d = dt("lnw", [1, HF], F32, kind="ExternalInput").ap()
        self.lnb_d = dt("lnb", [1, HF], F32, kind="ExternalInput").ap()
        self.wproj_d = dt("wproj", [D, HF], F32, kind="ExternalInput").ap()
        self.out_d = dt("out", [NTOK, D], F32, kind="ExternalOutput").ap()
        self.x2sp_d = dt("x2spill", [NTOK, D], F32).ap()

        names = ["dx", "dw", "dq0", "dq1", "dx0", "dx1", "dx2",
                 "dsp0", "dsp1", "dout0", "dout1", "dout2", "v", "s", "t", "g"]
        self.sems = {nm: self.es.enter_context(nc.semaphore(nm)) for nm in names}
        for nm in names:
            self.cnt[nm] = 0

        self.psA = self._alloc(self.es, "psA", [128, 512], F32, psum=True)
        self.psB = self._alloc(self.es, "psB", [128, 512], F32, psum=True)
        self.psW = self._alloc(self.es, "psW", [128, 1024], F32, psum=True)
        self.ps0 = self._alloc(self.es, "ps0", [128, 512], F32, psum=True)
        self.ps1 = self._alloc(self.es, "ps1", [128, 512], F32, psum=True)
        self.psT = self._alloc(self.es, "psT", [128, 1024], BF16, psum=True)
        self.psS = self._alloc(self.es, "psS", [128, 512], F32, psum=True)
        self.s_pack_started = set()

        sb = lambda n, s, d=F32: self._alloc(self.es, n, s, d)
        self.ident = sb("ident", [128, 128], BF16)
        self.ident_f = sb("ident_f", [32, 32])
        self.ones_row_b = sb("ones_row_b", [1, 128], BF16)
        self.zrow_b = sb("zrow_b", [1, 512], BF16)
        self.ones_col_f = sb("ones_col_f", [128, 1])
        self.ones_row_f = sb("ones_row_f", [1, 128])
        self.half_sc = sb("half_sc", [1, 1])
        self.mc_sb = sb("mc_sb", [128, 1])
        self.mb_sb = sb("mb_sb", [128, 1])
        self.XR = 3
        self.x_ring = sb("x_ring", [128, self.XR, D])
        self.xln = sb("xln", [128, 2, D], BF16)
        self.xlnT = sb("xlnT", [128, 2, 8, 128], BF16)
        self.y_att = sb("y_att", [128, 2, D])
        self.ln2tmp = sb("ln2tmp", [128, 2, D])
        self.sq_scr = sb("sq_scr", [128, D], BF16)
        self.st = {}
        for nm in ["sx1", "xx1", "mu1", "r1", "tm1",
                   "sx2", "xx2", "mu2", "r2", "tm2",
                   "sx3", "xx3", "mu3", "r3", "tm3",
                   "sx4", "xx4", "mu4", "r4", "tm4",
                   "sxh", "xxh", "muh", "rh", "tmh"]:
            self.st[nm] = sb("st_" + nm, [128, NT])
        self.gam_fc = sb("gam_fc", [128, 1])
        self.gam_pj = sb("gam_pj", [128, 1])
        self.gam_scr = sb("gam_scr", [128, 1])
        self.thr_fc = sb("thr_fc", [128, 1])
        self.thr_fcn = sb("thr_fcn", [128, 1])
        self.thr_pj = sb("thr_pj", [128, 1])
        self.thr_pjn = sb("thr_pjn", [128, 1])

        self.att_scope = ExitStack()
        sba = lambda n, s, d=BF16: self._alloc(self.att_scope, n, s, d)
        self.wq_sb = sba("wq_sb", [128, 8, D])
        self.wk_sb = sba("wk_sb", [128, 8, D])
        self.wv_sb = sba("wv_sb", [128, 8, D])
        self.wo_sb = sba("wo_sb", [128, 8, D])

        self.a_scope = ExitStack()
        sbA = lambda n, s, d=F32: self._alloc(self.a_scope, n, s, d)
        self.bo_row = sbA("bo_row", [1, D])
        self.bo_bc = sbA("bo_bc", [128, D])
        self.mask_g = sbA("mask_g", [128, 4, 256], BF16)
        self.mask_0 = sbA("mask_0", [128, 4, 256], BF16)
        self.qfull = sbA("qfull", [128, 2, D], BF16)
        self.qT = sbA("qT", [128, 2, 8, 128], BF16)
        self.expq = sbA("expq", [128, 768])
        self.zq = sbA("zq", [128, 2, 12])
        self.zqr = sbA("zqr", [128, 2, 12])
        self.kgl = sbA("kgl", [128, 2, 768], BF16)
        self.elu1 = sbA("elu1", [128, 768])
        self.elu2 = sbA("elu2", [128, 768], BF16)
        self.vhat = sbA("vhat", [128, 2, 12, 65], BF16)
        self.kloc = sbA("kloc", [128, 256], BF16)
        self.klocT = sbA("klocT", [128, 2, NT * 128 + 128], BF16)
        self.vloc = sbA("vloc", [128, NT + 1, 4, 65], BF16)
        self.expl = sbA("expl", [128, 4, 256], BF16)
        self.explT = sbA("explT", [128, 8, 128], BF16)
        self.attnl = sbA("attnl", [128, 256], BF16)
        self.zl = sbA("zl", [128, 4])
        self.zlr = sbA("zlr", [128, 4])
        self.ctxp = sbA("ctxp", [128, 2, 2, 192], BF16)
        self.zsn = sbA("zsn", [128, 2, 3])
        self.zsr = sbA("zsr", [128, 2, 3])
        self.attnT = sbA("attnT", [128, 2, 8, 128], BF16)
        self.x2t = sbA("x2t", [128, 2, D])
        self.stp = {}
        for nm in ["sx1p", "xx1p", "mu1p", "r1p", "tm1p"]:
            self.stp[nm] = sbA("st_" + nm, [128, NT])

        self.q_scope = ExitStack()
        sbQ = lambda n, s, d=F32: self._alloc(self.q_scope, n, s, d, side="right")
        self.wg_ring = sbQ("wg_ring", [128, 2, 1024])

    def _alloc(self, scope, name, shape, dtt, psum=False, side=None):
        f = self.nc.psum_tensor if psum else self.nc.sbuf_tensor
        kw = {"side": side} if side else {}
        return scope.enter_context(f(name, shape, dtt, **kw))

    def begin(self):
        self.items = {e: [] for e in ENGINES}

    def op(self, eng, fn, inc=None, inc16=None):
        incs = []
        ret = None
        if inc is not None:
            for nm in ([inc] if isinstance(inc, str) else inc):
                self.cnt[nm] += 1
                incs.append((nm, 1))
                if ret is None:
                    ret = self.cnt[nm]
        if inc16 is not None:
            self.cnt[inc16] += 16
            incs.append((inc16, 16))
            if ret is None:
                ret = self.cnt[inc16]
        self.items[eng].append(("op", fn, incs))
        return ret

    def wait(self, eng, sem, val):
        if val:
            self.items[eng].append(("wait", sem, val))

    def waits(self, eng, lst):
        for sem, val in lst:
            self.wait(eng, sem, val)

    def flush(self):
        items = self.items
        sems = self.sems
        with self.nc.Block() as blk:
            def mk(e):
                def body(E):
                    for it in items[e]:
                        if it[0] == "wait":
                            E.wait_ge(sems[it[1]], it[2])
                        else:
                            inst = it[1](E)
                            for nm, k in it[2]:
                                inst.then_inc(sems[nm], k)
                return body
            for e in ENGINES:
                if items[e]:
                    getattr(blk, e)(mk(e))
        self.items = None

    def pe_acquire(self, name):
        for sem, val in self.ps_cons.pop(name, []):
            self.wait("tensor", sem, val)

    def pe_done(self, *names):
        for name in names:
            self.ps_prod[name] = ("t", self.cnt["t"])

    def rd_wait(self, eng, name):
        p = self.ps_prod.get(name)
        if p:
            self.wait(eng, p[0], p[1])

    def rd_done(self, name, sem="v", val=None):
        if val is None:
            val = self.cnt[sem]
        self.ps_cons.setdefault(name, []).append((sem, val))

    def ln_stats(self, src_ap, st, pfx, col, width):
        self.op("vector", lambda E: E.reduce_sum(st["sx" + pfx][:, col:col + 1], src_ap, AX.X), inc="v")
        cv = self.cnt["v"]
        self.op("scalar", lambda E: E.activation(self.sq_scr[:, 0:width], src_ap, AF.Square,
                                                 accum_out=st["xx" + pfx][:, col:col + 1]), inc="s")
        return cv, self.cnt["s"]

    def ln_stats_vec(self, src_ap, st, pfx, col, scratch):
        self.op("vector", lambda E: E.reduce_sum(st["sx" + pfx][:, col:col + 1], src_ap, AX.X), inc="v")
        self.op("vector", lambda E: E.scalar_tensor_tensor(
            scratch, src_ap, 1.0, src_ap, ALU.mult, ALU.mult,
            accum_out=st["xx" + pfx][:, col:col + 1]), inc="v")
        return self.cnt["v"], None

    def ln_finalize(self, st, pfx, col, nelem, cv, cs):
        mus = st["mu" + pfx]; rs = st["r" + pfx]
        sxs = st["sx" + pfx]; xxs = st["xx" + pfx]; tms = st["tm" + pfx]
        c = slice(col, col + 1)
        self.wait("vector", "v", cv)
        if cs:
            self.wait("vector", "s", cs)
        self.op("vector", lambda E: E.tensor_scalar_mul(mus[:, c], sxs[:, c], 1.0 / nelem), inc="v")
        c1 = self.cnt["v"]
        self.wait("vector", "v", c1)
        self.op("vector", lambda E: E.tensor_mul(tms[:, c], mus[:, c], mus[:, c]), inc="v")
        self.wait("vector", "v", self.cnt["v"])
        self.op("vector", lambda E: E.scalar_tensor_tensor(
            tms[:, c], xxs[:, c], 1.0 / nelem, tms[:, c], ALU.mult, ALU.subtract), inc="v")
        self.wait("vector", "v", self.cnt["v"])
        self.op("vector", lambda E: E.tensor_scalar_add(tms[:, c], tms[:, c], LN_EPS), inc="v")
        c4 = self.cnt["v"]
        self.wait("scalar", "v", c4)
        self.op("scalar", lambda E: E.activation(rs[:, c], tms[:, c], AF.Ln), inc="s")
        self.wait("scalar", "s", self.cnt["s"])
        self.op("scalar", lambda E: E.activation(rs[:, c], rs[:, c], AF.Exp, scale=-0.5), inc="s")
        return c1, self.cnt["s"]

    def transpose8(self, src_getter, waitlist, n=8):
        self.pe_acquire("psT")
        self.waits("tensor", waitlist)
        for cc in range(n):
            self.op("tensor",
                    lambda E, cc=cc: E.transpose(self.psT[:, cc * 128:(cc + 1) * 128],
                                                 src_getter(cc), self.ident[:, :]),
                    inc="t" if cc == n - 1 else None)
        self.pe_done("psT")
        return self.cnt["t"]

    def evac_psT(self, dst_ap, n=8, pre_waits=()):
        self.rd_wait("vector", "psT")
        self.waits("vector", pre_waits)
        self.op("vector", lambda E: E.tensor_copy(
            dst_ap, self.psT[:, 0:n * 128].rearrange("p (c f) -> p c f", c=n)), inc="v")
        self.rd_done("psT")
        return self.cnt["v"]

    def plan_setup(self):
        self.begin()
        for w_sb, w_d in [(self.wq_sb, self.wq_d), (self.wo_sb, self.wo_d)]:
            self.op("gpsimd", lambda E, a=w_sb, b=w_d: E.dma_start(
                a[:, :, :], b.rearrange("(c p) f -> p c f", p=128)), inc16="dw")
        for w_sb, w_d in [(self.wk_sb, self.wk_d), (self.wv_sb, self.wv_d)]:
            self.op("gpsimd", lambda E, a=w_sb, b=w_d: E.dma_start(
                a[:, :, 0:768], b[:, 256:1024].rearrange("(c p) f -> p c f", p=128)), inc16="dw")
            self.op("gpsimd", lambda E, a=w_sb, b=w_d: E.dma_start(
                a[:, :, 768:1024], b[:, 0:256].rearrange("(c p) f -> p c f", p=128)), inc16="dw")
        self.op("sync", lambda E: E.dma_start(self.mc_sb[:, :], self.mc_d), inc16="dx")
        self.op("sync", lambda E: E.dma_start(self.mb_sb[:, :], self.mb_d), inc16="dx")
        self.op("sync", lambda E: E.dma_start(self.bo_row[:, :], self.bo_d), inc16="dx")
        cdx = self.cnt["dx"]

        g = "gpsimd"
        self.op(g, lambda E: E.memset(self.ident[:, :], 0.0), inc="g")
        self.wait(g, "g", self.cnt["g"])
        self.op(g, lambda E: E.affine_select(out=self.ident[:, :], in_=self.ident[:, :],
                                             compare_op=ALU.not_equal, fill=1.0, base=0,
                                             pattern=[[-1, 128]], channel_multiplier=1))
        self.op(g, lambda E: E.memset(self.ident_f[:, :], 0.0), inc="g")
        self.wait(g, "g", self.cnt["g"])
        self.op(g, lambda E: E.affine_select(out=self.ident_f[:, :], in_=self.ident_f[:, :],
                                             compare_op=ALU.not_equal, fill=1.0, base=0,
                                             pattern=[[-1, 32]], channel_multiplier=1))
        self.op(g, lambda E: E.memset(self.ones_col_f[:, :], 1.0))
        self.op(g, lambda E: E.memset(self.ones_row_f[:, :], 1.0))
        self.op(g, lambda E: E.memset(self.half_sc[:, :], 0.5))
        self.op(g, lambda E: E.memset(self.ones_row_b[:, :], 1.0))
        self.op(g, lambda E: E.memset(self.zrow_b[:, :], 0.0))
        self.op(g, lambda E: E.memset(self.vloc[:, :, :, :], 1.0))
        self.op(g, lambda E: E.memset(self.vhat[:, :, :, :], 1.0))
        self.op(g, lambda E: E.memset(self.mask_g[:, :, :], 1.0), inc="g")
        self.wait(g, "g", self.cnt["g"])
        for h in range(4):
            self.op(g, lambda E, h=h: E.affine_select(
                out=self.mask_g[:, h, 128:256], in_=self.mask_g[:, h, 128:256],
                compare_op=ALU.is_ge, fill=0.0, base=0,
                pattern=[[-1, 128]], channel_multiplier=1),
                inc="g" if h == 3 else None)
        cg = self.cnt["g"]

        self.wait("vector", "dx", cdx)
        self.wait("vector", "g", cg)
        self.op("vector", lambda E: E.tensor_copy(self.mask_0[:, :, :], self.mask_g[:, :, :]), inc="v")
        self.wait("vector", "v", self.cnt["v"])
        for h in range(4):
            self.op("vector", lambda E, h=h: E.tensor_scalar_mul(
                self.mask_0[:, h, 0:128], self.mask_0[:, h, 0:128], self.mc_sb[:, 0:1]),
                inc="v" if h == 3 else None)

        self.wait("tensor", "dx", cdx)
        self.wait("tensor", "g", cg)
        for half in range(2):
            self.pe_acquire("psS")
            self.op("tensor", lambda E, half=half: E.matmul(
                self.psS[:, 0:512], self.ones_row_f[:, :],
                self.bo_row[:, half * 512:(half + 1) * 512], start=True, stop=True), inc="t")
            self.pe_done("psS")
            self.rd_wait("vector", "psS")
            self.op("vector", lambda E, half=half: E.tensor_copy(
                self.bo_bc[:, half * 512:(half + 1) * 512], self.psS[:, 0:512]), inc="v")
            self.rd_done("psS")
        self.flush()

    def init_gamma(self):
        tiles = []
        for i in range(32):
            tiles.append(("fc", self.wfc_d[i * 128:(i + 1) * 128, :]))
        for i in range(8):
            for j in range(4):
                tiles.append(("pj", self.wproj_d[i * 128:(i + 1) * 128, j * 1024:(j + 1) * 1024]))
        self.gamma_tiles = tiles
        self.gamma_i = 0
        self.gamma_loads = []
        self.gamma_red = []

    def plan_gamma(self, n):
        for _ in range(n):
            i = self.gamma_i
            if i >= len(self.gamma_tiles):
                return
            kind, ap = self.gamma_tiles[i]
            buf = i % 2
            if i >= 2:
                self.wait("scalar", "v", self.gamma_red[i - 2])
            self.op("scalar", lambda E, ap=ap, buf=buf: E.dma_start(self.wg_ring[:, buf, :], ap), inc16="dq%d" % buf)
            self.gamma_loads.append(self.cnt["dq%d" % buf])
            self.wait("vector", "dq%d" % buf, self.gamma_loads[i])
            tgt = self.gam_fc if kind == "fc" else self.gam_pj
            first = (i == 0) or (kind == "pj" and self.gamma_tiles[i - 1][0] == "fc")
            if first:
                self.op("vector", lambda E, buf=buf, tgt=tgt: E.tensor_reduce(
                    tgt[:, :], self.wg_ring[:, buf, :], AX.X, ALU.add,
                    apply_absolute_value=True), inc="v")
            else:
                self.wait("vector", "v", self.misc.get("gam_add", 0))
                self.op("vector", lambda E, buf=buf: E.tensor_reduce(
                    self.gam_scr[:, :], self.wg_ring[:, buf, :], AX.X, ALU.add,
                    apply_absolute_value=True), inc="v")
                self.wait("vector", "v", self.cnt["v"])
                self.op("vector", lambda E, tgt=tgt: E.tensor_add(tgt[:, :], tgt[:, :], self.gam_scr[:, :]), inc="v")
                self.misc["gam_add"] = self.cnt["v"]
            self.gamma_red.append(self.cnt["v"])
            self.gamma_i = i + 1

    def plan_chunk_S(self, par, r0, waitlist):
        self.waits("tensor", waitlist)
        for nm in ("psA", "psB"):
            for sem, val in self.ps_cons.pop(nm, []):
                self.wait("tensor", sem, val)
        for h in range(4, 16):
            pack = self.psA if h < 10 else self.psB
            s = (h - 4) % 6
            p0 = (s % 2) * 64
            c0 = (s // 2) * 65
            self.op("tensor", lambda E, pack=pack, p0=p0, c0=c0, par=par, r0=r0, h=h: E.matmul(
                pack[p0:p0 + 64, c0:c0 + 65],
                self.kgl[r0:r0 + 64, par, (h - 4) * 64:(h - 3) * 64],
                self.vhat[r0:r0 + 64, par, h - 4, :],
                start=False, stop=False, skip_group_check=True),
                inc="t" if h == 15 else None)
        self.pe_done("psA", "psB")
        return self.cnt["t"]

    def plan_snapshot(self, ring):
        self.wait("vector", "t", self.misc.get(("ctx_rd", ring), 0))
        for ipk, pack in enumerate((self.psA, self.psB)):
            pname = "psA" if ipk == 0 else "psB"
            self.rd_wait("vector", pname)
            self.op("vector", lambda E, pack=pack, ipk=ipk: E.tensor_scalar_add(
                self.zsn[:, ipk, :],
                pack[:, 0:195].rearrange("p (s e) -> p s e", s=3)[:, :, 64], CTX_EPS), inc="v")
            self.wait("vector", "v", self.cnt["v"])
            self.op("vector", lambda E, ipk=ipk: E.reciprocal(
                self.zsr[:, ipk, :], self.zsn[:, ipk, :]), inc="v")
            self.wait("vector", "v", self.cnt["v"])
            self.op("vector", lambda E, pack=pack, ipk=ipk, ring=ring: E.tensor_tensor(
                self.ctxp[:, ring, ipk, :].rearrange("p (s e) -> p s e", s=3),
                pack[:, 0:195].rearrange("p (s e) -> p s e", s=3)[:, :, 0:64],
                self.zsr[:, ipk, :].unsqueeze(2).broadcast_to([128, 3, 64]),
                ALU.mult), inc="v")
            self.rd_done(pname)
        return self.cnt["v"]

    def plan_prefix(self, ntiles=NT):
        self.begin()
        for pack in (self.psA, self.psB):
            self.op("tensor", lambda E, pack=pack: E.matmul(
                pack[:, 0:512], self.ones_row_b[:, :], self.zrow_b[:, :],
                start=True, stop=False, skip_group_check=True), inc="t")
        self.pe_done("psA", "psB")
        import os as _os
        if _os.environ.get("KDBG_PIPEP"):
            for p in range(ntiles):
                self._prefix_head(p)
                if p > 0:
                    self._prefix_tail(p - 1)
            self._prefix_tail(ntiles - 1)
        else:
            for p in range(ntiles):
                self._prefix_head(p)
                self._prefix_tail(p)
        csn = self.plan_snapshot(1)
        self.misc[("ctx_ready", 1)] = csn
        self.flush()

    def _prefix_head(self, p):
        par = p % 2
        xb = p % self.XR
        if p >= self.XR:
            sem, val = self.x_free.get(p - self.XR, (None, 0))
            if sem:
                self.wait("sync", sem, val)
        dsem = "dx%d" % xb
        self.op("sync", lambda E, p=p, xb=xb: E.dma_start(
            self.x_ring[:, xb, :], self.xprev_d[p * 128:(p + 1) * 128, :]), inc16=dsem)
        cl = self.cnt[dsem]

        self.wait("vector", dsem, cl)
        self.wait("scalar", dsem, cl)
        cv, cs = self.ln_stats(self.x_ring[:, xb, :], self.stp, "1p", p, D)
        cmu, cr = self.ln_finalize(self.stp, "1p", p, D, cv, cs)
        self.wait("vector", "v", cmu)
        self.wait("vector", "s", cr)
        self.wait("vector", "t", self.misc.get(("xlnT_done", par), 0))
        self.op("vector", lambda E, xb=xb, par=par, p=p: E.tensor_scalar(
            self.xln[:, par, :], self.x_ring[:, xb, :],
            self.stp["mu1p"][:, p:p + 1], self.stp["r1p"][:, p:p + 1],
            ALU.subtract, ALU.mult), inc="v")
        capp = self.cnt["v"]
        self.x_free[p] = ("v", capp)

        ctr = self.transpose8(lambda cc, par=par: self.xln[:, par, cc * 128:(cc + 1) * 128],
                              [("v", capp)])
        self.misc[("xlnT_done", par)] = ctr
        cxT = self.evac_psT(self.xlnT[:, par, :, :],
                            pre_waits=[("t", self.misc.get(("xlnT_rd", par), 0))])

        self.wait("tensor", "v", cxT)
        self.wait("tensor", "dw", 96)
        self.pe_acquire("ps0")
        self.pe_acquire("ps1")
        for cc in range(8):
            self.op("tensor", lambda E, par=par, cc=cc: E.matmul(
                self.ps0[:, 0:512], self.xlnT[:, par, cc, :], self.wk_sb[:, cc, 0:512],
                start=(cc == 0), stop=(cc == 7)))
            self.op("tensor", lambda E, par=par, cc=cc: E.matmul(
                self.ps1[:, 0:256], self.xlnT[:, par, cc, :], self.wk_sb[:, cc, 512:768],
                start=(cc == 0), stop=(cc == 7)),
                inc="t" if cc == 7 else None)
        self.pe_done("ps0", "ps1")
        self.pe_acquire("psW")
        for cc in range(8):
            self.op("tensor", lambda E, par=par, cc=cc: E.matmul(
                self.psW[:, 0:512], self.xlnT[:, par, cc, :], self.wv_sb[:, cc, 0:512],
                start=(cc == 0), stop=(cc == 7)))
            self.op("tensor", lambda E, par=par, cc=cc: E.matmul(
                self.psW[:, 512:768], self.xlnT[:, par, cc, :], self.wv_sb[:, cc, 512:768],
                start=(cc == 0), stop=(cc == 7)),
                inc="t" if cc == 7 else None)
        self.pe_done("psW")
        self.misc[("xlnT_rd", par)] = self.cnt["t"]

        self.rd_wait("scalar", "ps0")
        self.rd_wait("scalar", "ps1")
        self.wait("scalar", "v", self.misc.get(("elu2_free", par), 0))
        self.op("scalar", lambda E: E.activation(self.elu1[:, 0:512], self.ps0[:, 0:512], AF.Relu, scale=-1.0))
        self.op("scalar", lambda E: E.activation(self.elu1[:, 512:768], self.ps1[:, 0:256], AF.Relu, scale=-1.0), inc="s")
        ce1 = self.cnt["s"]
        self.wait("scalar", "s", ce1)
        self.op("scalar", lambda E: E.activation(self.elu2[:, :], self.elu1[:, :], AF.Exp,
                                                 scale=-1.0, bias=self.mb_sb[:, 0:1]), inc="s")
        ce2 = self.cnt["s"]
        semv = self.misc.get(("kv_free", par))
        if semv:
            self.wait("vector", semv[0], semv[1])
        self.wait("vector", "s", ce2)
        self.op("vector", lambda E, par=par: E.tensor_scalar(
            self.kgl[:, par, 0:512], self.ps0[:, 0:512], 0.0, self.mc_sb[:, 0:1], ALU.max, ALU.mult))
        self.op("vector", lambda E, par=par: E.tensor_scalar(
            self.kgl[:, par, 512:768], self.ps1[:, 0:256], 0.0, self.mc_sb[:, 0:1], ALU.max, ALU.mult), inc="v")
        ckm = self.cnt["v"]
        self.rd_done("ps0")
        self.rd_done("ps1")
        self.wait("vector", "v", ckm)
        self.op("vector", lambda E, par=par: E.tensor_add(
            self.kgl[:, par, :], self.kgl[:, par, :], self.elu2[:, :]), inc="v")
        ckg = self.cnt["v"]
        self.misc[("elu2_free", par)] = ckg

        self.rd_wait("vector", "psW")
        self.op("vector", lambda E, par=par: E.tensor_copy(
            self.vhat[:, par, :, 0:64],
            self.psW[:, 0:768].rearrange("p (h e) -> p h e", h=12)), inc="v")
        cvh = self.cnt["v"]
        self.rd_done("psW")
        self.misc[("pfx_kv", p)] = (ckg, cvh)

        if p == NT - 1:
            self.pe_acquire("ps1")
            self.pe_acquire("psW")
            self.wait("tensor", "v", cxT)
            for cc in range(8):
                self.op("tensor", lambda E, par=par, cc=cc: E.matmul(
                    self.ps1[:, 256:512], self.xlnT[:, par, cc, :], self.wk_sb[:, cc, 768:1024],
                    start=(cc == 0), stop=(cc == 7)))
                self.op("tensor", lambda E, par=par, cc=cc: E.matmul(
                    self.psW[:, 768:1024], self.xlnT[:, par, cc, :], self.wv_sb[:, cc, 768:1024],
                    start=(cc == 0), stop=(cc == 7)),
                    inc="t" if cc == 7 else None)
            self.pe_done("ps1", "psW")
            self.misc[("xlnT_rd", par)] = self.cnt["t"]
            self.rd_wait("vector", "ps1")
            self.op("vector", lambda E: E.tensor_copy(self.kloc[:, :], self.ps1[:, 256:512]), inc="v")
            ckl = self.cnt["v"]
            self.rd_done("ps1")
            ctr = self.transpose8(lambda cc: self.kloc[:, cc * 128:(cc + 1) * 128],
                                  [("v", ckl)], n=2)
            self.rd_wait("vector", "psT")
            self.op("vector", lambda E: E.tensor_copy(self.klocT[:, 0, 0:128], self.psT[:, 0:128]))
            self.op("vector", lambda E: E.tensor_copy(self.klocT[:, 1, 0:128], self.psT[:, 128:256]), inc="v")
            self.rd_done("psT")
            self.rd_wait("vector", "psW")
            self.op("vector", lambda E: E.tensor_copy(
                self.vloc[:, 0, :, 0:64],
                self.psW[:, 768:1024].rearrange("p (h e) -> p h e", h=4)), inc="v")
            chv = self.cnt["v"]
            self.rd_done("psW")
            self.misc["halo_ready"] = chv

    def _prefix_tail(self, p):
        par = p % 2
        ckg, cvh = self.misc[("pfx_kv", p)]
        for half in range(2):
            self.plan_chunk_S(par, half * 64, [("v", ckg), ("v", cvh)])
        self.misc[("kv_free", par)] = ("t", self.cnt["t"])
        self.plan_gamma(2)

    def plan_phaseA(self, ntiles=NT):
        import os as _os
        self.begin()
        if _os.environ.get("KDBG_PIPEA"):
            for t in range(ntiles):
                self._a_head1(t)
                if t > 0:
                    self._a_tail1(t - 1)
                self._a_head2(t)
                if t > 0:
                    self._a_tail2(t - 1)
                self._a_head3(t)
                if t > 0:
                    self._a_tail3(t - 1)
            self._a_tail1(ntiles - 1)
            self._a_tail2(ntiles - 1)
            self._a_tail3(ntiles - 1)
        else:
            for t in range(ntiles):
                self._a_head1(t)
                self._a_head2(t)
                self._a_head3(t)
                self._a_tail1(t)
                self._a_tail2(t)
                self._a_tail3(t)
        self.flush()

    def _a_head1(self, t):
        par = t % 2
        xb = t % self.XR
        if t < self.XR:
            pv = t + self.XR * ((NT - 1 - t) // self.XR)
            sem, val = self.x_free.get(pv, (None, 0))
            if sem:
                self.wait("sync", sem, val)
        else:
            self.wait("sync", "g", self.misc.get(("xdone", t - self.XR), 0))
        dsem = "dx%d" % xb
        self.op("sync", lambda E, t=t, xb=xb: E.dma_start(
            self.x_ring[:, xb, :], self.x_d[t * 128:(t + 1) * 128, :]), inc16=dsem)
        cl = self.cnt[dsem]

        self.wait("vector", dsem, cl)
        self.wait("scalar", dsem, cl)
        cv, cs = self.ln_stats(self.x_ring[:, xb, :], self.st, "1", t, D)
        cmu, cr = self.ln_finalize(self.st, "1", t, D, cv, cs)
        self.wait("vector", "v", cmu)
        self.wait("vector", "s", cr)
        self.wait("vector", "t", self.misc.get(("xlnT_done", par), 0))
        self.op("vector", lambda E, xb=xb, par=par, t=t: E.tensor_scalar(
            self.xln[:, par, :], self.x_ring[:, xb, :],
            self.st["mu1"][:, t:t + 1], self.st["r1"][:, t:t + 1],
            ALU.subtract, ALU.mult), inc="v")
        capp = self.cnt["v"]

        ctr = self.transpose8(lambda cc, par=par: self.xln[:, par, cc * 128:(cc + 1) * 128],
                              [("v", capp)])
        self.misc[("xlnT_done", par)] = ctr
        cxT = self.evac_psT(self.xlnT[:, par, :, :],
                            pre_waits=[("t", self.misc.get(("xlnT_rd", par), 0))])
        self.misc[("a_cxT", t)] = cxT

        self.wait("tensor", "v", cxT)
        self.wait("tensor", "dw", 96)
        self.pe_acquire("psW")
        for cc in range(8):
            self.op("tensor", lambda E, par=par, cc=cc: E.matmul(
                self.psW[:, 0:512], self.xlnT[:, par, cc, :], self.wq_sb[:, cc, 0:512],
                start=(cc == 0), stop=(cc == 7)))
            self.op("tensor", lambda E, par=par, cc=cc: E.matmul(
                self.psW[:, 512:1024], self.xlnT[:, par, cc, :], self.wq_sb[:, cc, 512:1024],
                start=(cc == 0), stop=(cc == 7)),
                inc="t" if cc == 7 else None)
        self.pe_done("psW")

        self.rd_wait("scalar", "psW")
        self.wait("scalar", "v", self.misc.get(("expq_free", par), 0))
        self.op("scalar", lambda E: E.activation(self.expq[:, :], self.psW[:, 256:1024], AF.Exp), inc="s")
        cexq = self.cnt["s"]
        self.rd_done("psW", "s", cexq)
        self.wait("vector", "s", cexq)
        self.op("vector", lambda E, par=par: E.tensor_reduce(
            self.zq[:, par, :], self.expq[:, :].rearrange("p (h e) -> p h e", h=12),
            AX.X, ALU.add), inc="v")
        self.wait("vector", "v", self.cnt["v"])
        self.op("vector", lambda E, par=par: E.reciprocal(self.zqr[:, par, :], self.zq[:, par, :]), inc="v")
        czr = self.cnt["v"]
        self.rd_wait("vector", "psW")
        self.wait("vector", "t", self.misc.get(("qT_done", par), 0))
        self.op("vector", lambda E, par=par: E.tensor_scalar_mul(
            self.qfull[:, par, 0:256], self.psW[:, 0:256], 0.125), inc="v")
        clq = self.cnt["v"]
        self.rd_done("psW")
        self.wait("vector", "v", czr)
        self.op("vector", lambda E, par=par: E.tensor_tensor(
            self.qfull[:, par, 256:1024].rearrange("p (h e) -> p h e", h=12),
            self.expq[:, :].rearrange("p (h e) -> p h e", h=12),
            self.zqr[:, par, :].unsqueeze(2).broadcast_to([128, 12, 64]),
            ALU.mult), inc="v")
        cqf = self.cnt["v"]
        self.misc[("expq_free", par)] = cqf
        self.misc[("a_qf", t)] = (cqf, clq)

    def _a_tail1(self, t):
        par = t % 2
        cqT = self.misc[("a_cqT", t)]
        ckT = self.misc[("a_ckT", t)]
        cvl = self.misc[("a_cvl", t)]
        self.pe_acquire("ps0")
        self.pe_acquire("ps1")
        self.wait("tensor", "v", cqT)
        self.wait("tensor", "v", ckT)
        if t == 0:
            self.wait("tensor", "v", self.misc.get("halo_ready", 0))
        for h in range(4):
            tgt = self.ps0 if h < 2 else self.ps1
            coff = (h % 2) * 256
            p0 = (h % 2) * 64
            if p0 == 0:
                self.op("tensor", lambda E, h=h, tgt=tgt, coff=coff, par=par, t=t: E.matmul(
                    tgt[:, coff:coff + 256],
                    self.qT[0:64, par, h // 2, :],
                    self.klocT[0:64, h // 2, t * 128:t * 128 + 256],
                    start=True, stop=True))
            else:
                for mh in range(2):
                    self.op("tensor", lambda E, h=h, tgt=tgt, coff=coff, par=par, t=t, mh=mh: E.matmul(
                        tgt[mh * 64:(mh + 1) * 64, coff:coff + 256],
                        self.qT[64:128, par, h // 2, mh * 64:(mh + 1) * 64],
                        self.klocT[64:128, h // 2, t * 128:t * 128 + 256],
                        start=True, stop=True),
                        inc="t" if (h == 3 and mh == 1) else None)
        self.pe_done("ps0", "ps1")

        self.rd_wait("scalar", "ps0")
        self.rd_wait("scalar", "ps1")
        self.wait("scalar", "t", self.misc.get("expl_rd", 0))
        self.op("scalar", lambda E: E.activation(
            self.expl[:, 0:2, :].rearrange("p h e -> p (h e)"), self.ps0[:, :], AF.Exp))
        self.op("scalar", lambda E: E.activation(
            self.expl[:, 2:4, :].rearrange("p h e -> p (h e)"), self.ps1[:, :], AF.Exp), inc="s")
        cexl = self.cnt["s"]
        self.rd_done("ps0", "s", cexl)
        self.rd_done("ps1", "s", cexl)
        self.wait("vector", "s", cexl)
        msk = self.mask_0 if t == 0 else self.mask_g
        self.op("vector", lambda E, msk=msk: E.tensor_tensor(
            self.expl[:, :, :], self.expl[:, :, :], msk[:, :, :], ALU.mult), inc="v")
        cmsk = self.cnt["v"]

        ctr = self.transpose8(lambda cc: self.expl[:, cc // 2, (cc % 2) * 128:(cc % 2) * 128 + 128],
                              [("v", cmsk)])
        self.misc["expl_rd"] = ctr
        ceT = self.evac_psT(self.explT[:, :, :],
                            pre_waits=[("t", self.misc.get("explT_rd", 0))])

        self.pe_acquire("psS")
        self.wait("tensor", "v", ceT)
        self.wait("tensor", "v", cvl)
        for h in range(4):
            for half in range(2):
                self.op("tensor", lambda E, h=h, half=half, t=t: E.matmul(
                    self.psS[:, h * 65:h * 65 + 65],
                    self.explT[:, 2 * h + half, :],
                    self.vloc[:, t + half, h, :],
                    start=(half == 0), stop=(half == 1)),
                    inc="t" if (h == 3 and half == 1) else None)
        self.pe_done("psS")
        self.misc["explT_rd"] = self.cnt["t"]

        self.rd_wait("vector", "psS")
        self.op("vector", lambda E: E.tensor_copy(
            self.zl[:, :],
            self.psS[:, 0:260].rearrange("p (h e) -> p h e", h=4)[:, :, 64]), inc="v")
        self.wait("vector", "v", self.cnt["v"])
        self.op("vector", lambda E: E.reciprocal(self.zlr[:, :], self.zl[:, :]), inc="v")
        self.wait("vector", "v", self.cnt["v"])
        self.wait("vector", "t", self.misc.get("attnl_rd", 0))
        self.op("vector", lambda E: E.tensor_tensor(
            self.attnl[:, :].rearrange("p (h e) -> p h e", h=4),
            self.psS[:, 0:260].rearrange("p (h e) -> p h e", h=4)[:, :, 0:64],
            self.zlr[:, :].unsqueeze(2).broadcast_to([128, 4, 64]),
            ALU.mult), inc="v")
        cloc = self.cnt["v"]
        self.rd_done("psS")

        ctr = self.transpose8(lambda cc: self.attnl[:, cc * 128:(cc + 1) * 128],
                              [("v", cloc)], n=2)
        self.misc["attnl_rd"] = ctr
        self.rd_wait("vector", "psT")
        self.wait("vector", "t", self.misc.get(("wo_done", par), 0))
        self.op("vector", lambda E, par=par: E.tensor_copy(self.attnT[:, par, 0, :], self.psT[:, 0:128]))
        self.op("vector", lambda E, par=par: E.tensor_copy(self.attnT[:, par, 1, :], self.psT[:, 128:256]), inc="v")
        catl = self.cnt["v"]
        self.rd_done("psT")
        self.misc[("a_catl", t)] = catl

    def _a_head2(self, t):
        par = t % 2
        cqf, clq = self.misc[("a_qf", t)]
        ctr = self.transpose8(lambda cc, par=par: self.qfull[:, par, cc * 128:(cc + 1) * 128],
                              [("v", cqf), ("v", clq)])
        self.misc[("qT_done", par)] = ctr
        cqT = self.evac_psT(self.qT[:, par, :, :],
                            pre_waits=[("t", self.misc.get(("qT_rd", par), 0))])
        self.misc[("a_cqT", t)] = cqT

        cxT = self.misc[("a_cxT", t)]
        self.pe_acquire("psW")
        self.wait("tensor", "v", cxT)
        for cc in range(8):
            self.op("tensor", lambda E, par=par, cc=cc: E.matmul(
                self.psW[:, 0:512], self.xlnT[:, par, cc, :], self.wk_sb[:, cc, 0:512],
                start=(cc == 0), stop=(cc == 7)))
            self.op("tensor", lambda E, par=par, cc=cc: E.matmul(
                self.psW[:, 512:1024], self.xlnT[:, par, cc, :], self.wk_sb[:, cc, 512:1024],
                start=(cc == 0), stop=(cc == 7)),
                inc="t" if cc == 7 else None)
        self.pe_done("psW")

        self.rd_wait("scalar", "psW")
        self.wait("scalar", "v", self.misc.get(("elu2_free", par), 0))
        self.op("scalar", lambda E: E.activation(self.elu1[:, 0:512], self.psW[:, 0:512], AF.Relu, scale=-1.0))
        self.op("scalar", lambda E: E.activation(self.elu1[:, 512:768], self.psW[:, 512:768], AF.Relu, scale=-1.0), inc="s")
        ce1 = self.cnt["s"]
        self.wait("scalar", "s", ce1)
        self.op("scalar", lambda E: E.activation(self.elu2[:, :], self.elu1[:, :], AF.Exp, scale=-1.0), inc="s")
        ce2 = self.cnt["s"]
        semv = self.misc.get(("kv_free", par))
        if semv:
            self.wait("vector", semv[0], semv[1])
        self.wait("vector", "s", ce2)
        self.op("vector", lambda E, par=par: E.scalar_tensor_tensor(
            self.kgl[:, par, 0:512], self.psW[:, 0:512], 0.0, self.elu2[:, 0:512], ALU.max, ALU.add))
        self.op("vector", lambda E, par=par: E.scalar_tensor_tensor(
            self.kgl[:, par, 512:768], self.psW[:, 512:768], 0.0, self.elu2[:, 512:768], ALU.max, ALU.add), inc="v")
        ckg = self.cnt["v"]
        self.misc[("elu2_free", par)] = ckg
        self.wait("vector", "t", self.misc.get("kloc_rd", 0))
        self.op("vector", lambda E: E.tensor_copy(self.kloc[:, :], self.psW[:, 768:1024]), inc="v")
        ckl = self.cnt["v"]
        self.rd_done("psW")
        ctr = self.transpose8(lambda cc: self.kloc[:, cc * 128:(cc + 1) * 128],
                              [("v", ckl)], n=2)
        self.misc["kloc_rd"] = ctr
        self.rd_wait("vector", "psT")
        col = 128 + t * 128
        self.op("vector", lambda E, col=col: E.tensor_copy(self.klocT[:, 0, col:col + 128], self.psT[:, 0:128]))
        self.op("vector", lambda E, col=col: E.tensor_copy(self.klocT[:, 1, col:col + 128], self.psT[:, 128:256]), inc="v")
        ckT = self.cnt["v"]
        self.rd_done("psT")
        self.misc[("a_ckT", t)] = ckT
        self.misc[("a_ckg", t)] = ckg

    def _a_tail2(self, t):
        par = t % 2
        cqT = self.misc[("a_cqT", t)]
        ckg = self.misc[("a_ckg", t)]
        cvh = self.misc[("a_cvh", t)]
        catg = []
        for half in range(2):
            u = 2 * t + half
            ring = u % 2
            prev = 1 - ring
            r0 = half * 64
            self.pe_acquire("psS")
            self.wait("tensor", "v", self.misc.get(("ctx_ready", prev), 0))
            self.wait("tensor", "v", cqT)
            for h in range(4, 16):
                ipk = 0 if h < 10 else 1
                s = (h - 4) % 6
                pr = (h - 4) // 2
                self.op("tensor", lambda E, h=h, ipk=ipk, s=s, pr=pr, prev=prev, par=par, r0=r0: E.matmul(
                    self.psS[((h - 4) % 2) * 64:((h - 4) % 2) * 64 + 64, pr * 64:pr * 64 + 64],
                    self.ctxp[(s % 2) * 64:(s % 2) * 64 + 64, prev, ipk, (s // 2) * 64:(s // 2) * 64 + 64],
                    self.qT[(h % 2) * 64:(h % 2) * 64 + 64, par, h // 2, r0:r0 + 64],
                    start=True, stop=True),
                    inc="t" if h == 15 else None)
            self.pe_done("psS")
            self.misc[("ctx_rd", prev)] = self.cnt["t"]
            self.rd_wait("vector", "psS")
            self.op("vector", lambda E, par=par, r0=r0: E.tensor_copy(
                self.attnT[:, par, 2:8, r0:r0 + 64],
                self.psS[:, 0:384].rearrange("p (pr e) -> p pr e", pr=6)), inc="v")
            catg.append(self.cnt["v"])
            self.rd_done("psS")

            self.plan_chunk_S(par, r0, [("v", ckg), ("v", cvh)])
            csn = self.plan_snapshot(ring)
            self.misc[("ctx_ready", ring)] = csn
        self.misc[("kv_free", par)] = ("t", self.cnt["t"])
        self.misc[("qT_rd", par)] = self.cnt["t"]
        self.misc[("a_catg", t)] = catg

    def _a_head3(self, t):
        par = t % 2
        cxT = self.misc[("a_cxT", t)]
        self.pe_acquire("psW")
        for cc in range(8):
            self.op("tensor", lambda E, par=par, cc=cc: E.matmul(
                self.psW[:, 0:512], self.xlnT[:, par, cc, :], self.wv_sb[:, cc, 0:512],
                start=(cc == 0), stop=(cc == 7)))
            self.op("tensor", lambda E, par=par, cc=cc: E.matmul(
                self.psW[:, 512:1024], self.xlnT[:, par, cc, :], self.wv_sb[:, cc, 512:1024],
                start=(cc == 0), stop=(cc == 7)),
                inc="t" if cc == 7 else None)
        self.pe_done("psW")
        self.misc[("xlnT_rd", par)] = self.cnt["t"]
        self.rd_wait("vector", "psW")
        semv = self.misc.get(("kv_free", par))
        if semv:
            self.wait("vector", semv[0], semv[1])
        self.op("vector", lambda E, par=par: E.tensor_copy(
            self.vhat[:, par, :, 0:64],
            self.psW[:, 0:768].rearrange("p (h e) -> p h e", h=12)), inc="v")
        cvh = self.cnt["v"]
        self.op("vector", lambda E, t=t: E.tensor_copy(
            self.vloc[:, t + 1, :, 0:64],
            self.psW[:, 768:1024].rearrange("p (h e) -> p h e", h=4)), inc="v")
        cvl = self.cnt["v"]
        self.rd_done("psW")
        self.misc[("a_cvh", t)] = cvh
        self.misc[("a_cvl", t)] = cvl

    def _a_tail3(self, t):
        par = t % 2
        xb = t % self.XR
        catl = self.misc[("a_catl", t)]
        catg = self.misc[("a_catg", t)]
        self.pe_acquire("ps0")
        self.pe_acquire("ps1")
        self.wait("tensor", "v", catl)
        for cv_ in catg:
            self.wait("tensor", "v", cv_)
        for cc in range(8):
            self.op("tensor", lambda E, par=par, cc=cc: E.matmul(
                self.ps0[:, 0:512], self.attnT[:, par, cc, :], self.wo_sb[:, cc, 0:512],
                start=(cc == 0), stop=(cc == 7)))
            self.op("tensor", lambda E, par=par, cc=cc: E.matmul(
                self.ps1[:, 0:512], self.attnT[:, par, cc, :], self.wo_sb[:, cc, 512:1024],
                start=(cc == 0), stop=(cc == 7)),
                inc="t" if cc == 7 else None)
        self.pe_done("ps0", "ps1")
        self.misc[("wo_done", par)] = self.cnt["t"]

        self.rd_wait("vector", "ps0")
        self.rd_wait("vector", "ps1")
        self.wait("vector", "s", self.misc.get(("yatt_sfree", par), 0))
        self.op("vector", lambda E, par=par: E.tensor_add(
            self.y_att[:, par, 0:512], self.ps0[:, 0:512], self.bo_bc[:, 0:512]))
        self.op("vector", lambda E, par=par: E.tensor_add(
            self.y_att[:, par, 512:1024], self.ps1[:, 0:512], self.bo_bc[:, 512:1024]), inc="v")
        cy = self.cnt["v"]
        self.rd_done("ps0")
        self.rd_done("ps1")
        self.wait("vector", "v", cy)
        self.wait("scalar", "v", cy)
        cv2, cs2 = self.ln_stats(self.y_att[:, par, :], self.st, "2", t, D)
        self.misc[("yatt_sfree", par)] = cs2
        cmu2, cr2 = self.ln_finalize(self.st, "2", t, D, cv2, cs2)
        self.wait("vector", "v", cmu2)
        self.wait("vector", "s", cr2)
        self.wait("vector", "g", self.misc.get(("ln2t_free", par), 0))
        self.op("vector", lambda E, par=par, t=t: E.tensor_scalar(
            self.ln2tmp[:, par, :], self.y_att[:, par, :],
            self.st["mu2"][:, t:t + 1], self.st["r2"][:, t:t + 1],
            ALU.subtract, ALU.mult), inc="v")
        cln2 = self.cnt["v"]
        self.wait("gpsimd", "v", cln2)
        self.wait("gpsimd", "dsp%d" % par, self.misc.get(("x2_spilled", par), 0))
        sv = self.misc.get(("x2_stats", par))
        if sv:
            self.wait("gpsimd", "v", sv[0])
            self.wait("gpsimd", "s", sv[1])
        self.op("gpsimd", lambda E, par=par, xb=xb: E.tensor_add(
            self.x2t[:, par, :], self.x_ring[:, xb, :], self.ln2tmp[:, par, :]), inc="g")
        cx2 = self.cnt["g"]
        self.misc[("xdone", t)] = cx2
        self.misc[("ln2t_free", par)] = cx2

        self.wait("sync", "g", cx2)
        self.op("sync", lambda E, par=par, t=t: E.dma_start(
            self.x2sp_d[t * 128:(t + 1) * 128, :], self.x2t[:, par, :]), inc16="dsp%d" % par)
        self.misc[("x2_spilled", par)] = self.cnt["dsp%d" % par]
        self.wait("vector", "g", cx2)
        self.wait("scalar", "g", cx2)
        cv3, cs3 = self.ln_stats(self.x2t[:, par, :], self.st, "3", t, D)
        self.misc[("x2_stats", par)] = (cv3, cs3)
        self.ln_finalize(self.st, "3", t, D, cv3, cs3)

        self.plan_gamma(2)

    def plan_quant(self):
        self.a_scope.close()
        self.att_scope.close()
        sb = lambda n, s, d=F32: self._alloc(self.es, n, s, d)
        sbQ = lambda n, s, d=F32: self._alloc(self.q_scope, n, s, d, side="right")
        self.wfcT = sb("wfcT", [128, 8, HF], BF16)
        self.wprojT = sb("wprojT", [128, 32, D], BF16)
        self.c_bc = sb("c_bc", [128, D])
        self.lnwcol = sb("lnwcol", [128, 32])
        self.sg1 = sbQ("sg1", [128, 2, 1024], BF16)
        self.sg2 = sbQ("sg2", [128, 2, 1024], BF16)
        self.wq2 = sbQ("wq2", [128, 2, 1024], BF16)
        self.lnw_sb = sbQ("lnw_sb", [32, 128])
        self.lnb_sb = sbQ("lnb_sb", [32, 128])
        self.c_row = sbQ("c_row", [1, D])
        self.lnbcol = sbQ("lnbcol", [128, 32], BF16)
        self.tval = sbQ("tval", [1, 2])

        self.begin()
        for eng in ENGINES:
            self.wait(eng, "dsp0", self.cnt["dsp0"])
            self.wait(eng, "dsp1", self.cnt["dsp1"])
        self.plan_gamma(len(self.gamma_tiles) - self.gamma_i)
        cv_gam = self.gamma_red[-1]

        self.op("sync", lambda E: E.dma_start(
            self.lnw_sb[:, :], self.lnw_d.rearrange("o (r c) -> (o r) c", c=128)), inc16="dx")
        self.op("sync", lambda E: E.dma_start(
            self.lnb_sb[:, :], self.lnb_d.rearrange("o (r c) -> (o r) c", c=128)), inc16="dx")
        crow = self.cnt["dx"]

        self.pe_acquire("psS")
        self.wait("tensor", "v", cv_gam)
        self.op("tensor", lambda E: E.matmul(self.psS[0:1, 0:1], self.gam_fc[:, :], self.ones_col_f[:, :],
                                             start=True, stop=True))
        self.op("tensor", lambda E: E.matmul(self.psS[0:1, 1:2], self.gam_pj[:, :], self.ones_col_f[:, :],
                                             start=True, stop=True), inc="t")
        self.pe_done("psS")
        self.rd_wait("vector", "psS")
        self.op("vector", lambda E: E.tensor_scalar(
            self.tval[:, 0:2], self.psS[0:1, 0:2], 1.0 / (HF * D), ABS_EPS, ALU.mult, ALU.add), inc="v")
        self.rd_done("psS")
        self.wait("vector", "v", self.cnt["v"])
        self.op("vector", lambda E: E.tensor_scalar_mul(self.tval[:, 0:2], self.tval[:, 0:2], 0.5), inc="v")
        ctv = self.cnt["v"]
        self.pe_acquire("psS")
        self.wait("tensor", "v", ctv)
        self.op("tensor", lambda E: E.matmul(self.psS[:, 0:1], self.ones_row_f[:, :], self.tval[:, 0:1],
                                             start=True, stop=True))
        self.op("tensor", lambda E: E.matmul(self.psS[:, 1:2], self.ones_row_f[:, :], self.tval[:, 1:2],
                                             start=True, stop=True), inc="t")
        self.pe_done("psS")
        self.rd_wait("vector", "psS")
        self.op("vector", lambda E: E.tensor_copy(self.thr_fc[:, :], self.psS[:, 0:1]))
        self.op("vector", lambda E: E.tensor_copy(self.thr_pj[:, :], self.psS[:, 1:2]), inc="v")
        self.rd_done("psS")
        self.wait("vector", "v", self.cnt["v"])
        self.op("vector", lambda E: E.tensor_scalar_mul(self.thr_fcn[:, :], self.thr_fc[:, :], -1.0))
        self.op("vector", lambda E: E.tensor_scalar_mul(self.thr_pjn[:, :], self.thr_pj[:, :], -1.0), inc="v")
        cthr = self.cnt["v"]

        self.pe_acquire("psW")
        self.wait("tensor", "dx", crow)
        self.op("tensor", lambda E: E.transpose(self.psW[:, 0:32], self.lnw_sb[:, :], self.ident_f[:, :]))
        self.op("tensor", lambda E: E.transpose(self.psW[:, 32:64], self.lnb_sb[:, :], self.ident_f[:, :]), inc="t")
        self.pe_done("psW")
        self.rd_wait("vector", "psW")
        self.op("vector", lambda E: E.tensor_scalar_mul(self.lnwcol[:, :], self.psW[:, 0:32], 0.5))
        self.op("vector", lambda E: E.tensor_scalar_mul(self.lnbcol[:, :], self.psW[:, 32:64], 0.5), inc="v")
        ccol = self.cnt["v"]
        self.rd_done("psW")

        qlast = {("add", 0): 0, ("add", 1): 0, ("tr", 0): 0, ("tr", 1): 0}

        def quant_tile(idx, src_ap, thr_p, thr_n, dst_fn):
            buf = idx % 2
            self.op("scalar", lambda E, src_ap=src_ap, buf=buf: E.dma_start(
                self.wg_ring[:, buf, :], src_ap), inc16="dq%d" % buf)
            cld = self.cnt["dq%d" % buf]
            self.wait("scalar", "dq%d" % buf, cld)
            self.wait("scalar", "v", qlast[("add", buf)])
            if idx == 0:
                self.wait("scalar", "v", cthr)
            self.op("scalar", lambda E, buf=buf, thr_n=thr_n: E.activation(
                self.sg1[:, buf, :], self.wg_ring[:, buf, :], AF.Sign, bias=thr_n[:, 0:1]))
            self.op("scalar", lambda E, buf=buf, thr_p=thr_p: E.activation(
                self.sg2[:, buf, :], self.wg_ring[:, buf, :], AF.Sign, bias=thr_p[:, 0:1]), inc="s")
            csg = self.cnt["s"]
            self.wait("vector", "s", csg)
            self.wait("vector", "t", qlast[("tr", buf)])
            self.op("vector", lambda E, buf=buf: E.tensor_add(
                self.wq2[:, buf, :], self.sg1[:, buf, :], self.sg2[:, buf, :]), inc="v")
            cwq = self.cnt["v"]
            qlast[("add", buf)] = cwq
            if buf == 0:
                ctr = self.transpose8(lambda cc, buf=buf: self.wq2[:, buf, cc * 128:(cc + 1) * 128], [("v", cwq)])
                qlast[("tr", buf)] = ctr
                self.evac_psT(dst_fn())
            else:
                pb = self.psB[:, :].bitcast(BF16)
                self.pe_acquire("psB")
                self.wait("tensor", "v", cwq)
                for cc in range(8):
                    self.op("tensor", lambda E, cc=cc, buf=buf, pb=pb: E.transpose(
                        pb[:, cc * 128:(cc + 1) * 128],
                        self.wq2[:, buf, cc * 128:(cc + 1) * 128], self.ident[:, :]),
                        inc="t" if cc == 7 else None)
                self.pe_done("psB")
                qlast[("tr", buf)] = self.cnt["t"]
                self.rd_wait("vector", "psB")
                self.op("vector", lambda E, pb=pb: E.tensor_copy(
                    dst_fn(), pb.rearrange("p (c f) -> p c f", c=8)), inc="v")
                self.rd_done("psB")
            return self.cnt["v"]

        for i in range(32):
            quant_tile(i, self.wfc_d[i * 128:(i + 1) * 128, :], self.thr_fc, self.thr_fcn,
                       (lambda i: lambda: self.wfcT[:, :, i * 128:(i + 1) * 128])(i))
        idx = 32
        for i in range(8):
            for j in range(4):
                quant_tile(idx, self.wproj_d[i * 128:(i + 1) * 128, j * 1024:(j + 1) * 1024],
                           self.thr_pj, self.thr_pjn,
                           (lambda i, j: lambda: self.wprojT[:, j * 8:(j + 1) * 8, i * 128:(i + 1) * 128])(i, j))
                idx += 1
        cwT = self.cnt["v"]

        self.pe_acquire("psW")
        self.wait("tensor", "v", cwT)
        self.wait("tensor", "v", ccol)
        for dhalf in range(2):
            for cc in range(32):
                self.op("tensor", lambda E, cc=cc, dhalf=dhalf: E.matmul(
                    self.psW[0:1, dhalf * 512:(dhalf + 1) * 512],
                    self.lnbcol[:, cc:cc + 1],
                    self.wprojT[:, cc, dhalf * 512:(dhalf + 1) * 512],
                    start=(cc == 0), stop=(cc == 31)),
                    inc="t" if (dhalf == 1 and cc == 31) else None)
        self.pe_done("psW")
        self.rd_wait("vector", "psW")
        self.op("vector", lambda E: E.tensor_copy(self.c_row[:, :], self.psW[0:1, :]), inc="v")
        c_row_done = self.cnt["v"]
        self.rd_done("psW")

        cfold = None
        self.wait("vector", "t", self.cnt["t"])
        for cc in range(32):
            self.op("vector", lambda E, cc=cc: E.tensor_scalar_mul(
                self.wprojT[:, cc, :], self.wprojT[:, cc, :], self.lnwcol[:, cc:cc + 1]),
                inc="v" if cc == 31 else None)
        cfold = self.cnt["v"]
        self.misc["wprojT_ready"] = cfold

        for half in range(2):
            self.pe_acquire("psS")
            self.wait("tensor", "v", c_row_done)
            self.op("tensor", lambda E, half=half: E.matmul(
                self.psS[:, 0:512], self.ones_row_f[:, :],
                self.c_row[:, half * 512:(half + 1) * 512], start=True, stop=True), inc="t")
            self.pe_done("psS")
            self.rd_wait("vector", "psS")
            self.op("vector", lambda E, half=half: E.tensor_copy(
                self.c_bc[:, half * 512:(half + 1) * 512], self.psS[:, 0:512]), inc="v")
            self.rd_done("psS")
        self.misc["c_bc_ready"] = self.cnt["v"]
        self.flush()

    def plan_phaseB(self):
        self.q_scope.close()
        self.h_sb = self._alloc(self.es, "h_sb", [128, 2, HF], BF16, side="right")
        self.hsums = self._alloc(self.es, "hsums", [128, 2, 4], F32, side="right")
        self.hsq4 = self._alloc(self.es, "hsq4", [128, 2, 4], F32, side="right")
        self.hlnT = self._alloc(self.es, "hlnT", [128, 32, 128], BF16, side="right")
        self.begin()
        self.ln4_pend = {}
        import os as _os
        if _os.environ.get("KDBG_PIPEB"):
            for t in range(NT):
                self._b_head(t)
                if t > 0:
                    self._b_tail(t - 1)
            self._b_tail(NT - 1)
        else:
            for t in range(NT):
                self._b_head(t)
                self._b_tail(t)
        cv4, cs4 = self.ln4_pend.pop(NT - 1)
        cmu4, cr4 = self.ln_finalize(self.st, "4", NT - 1, D, cv4, cs4)
        self._plan_ln4_apply(NT - 1, cmu4, cr4)
        for sl in range(self.XR):
            self.wait("sync", "dout%d" % sl, self.cnt["dout%d" % sl])
        self.flush()

    def _b_head(self, t):
        par = t % 2
        xb = t % self.XR
        dsem = "dx%d" % xb
        self.wait("sync", "dsp%d" % (t % 2), 16 * (t // 2 + 1))
        if t >= self.XR:
            self.wait("sync", "dout%d" % xb, self.misc.get(("out_tile", t - self.XR), 0))
        self.op("sync", lambda E, t=t, xb=xb: E.dma_start(
            self.x_ring[:, xb, :], self.x2sp_d[t * 128:(t + 1) * 128, :]), inc16=dsem)
        cl = self.cnt[dsem]

        self.wait("vector", dsem, cl)
        self.wait("vector", "t", self.misc.get(("xlnT_done", par), 0))
        self.op("vector", lambda E, xb=xb, par=par, t=t: E.tensor_scalar(
            self.xln[:, par, :], self.x_ring[:, xb, :],
            self.st["mu3"][:, t:t + 1], self.st["r3"][:, t:t + 1],
            ALU.subtract, ALU.mult), inc="v")
        capp = self.cnt["v"]

        ctr = self.transpose8(lambda cc, par=par: self.xln[:, par, cc * 128:(cc + 1) * 128],
                              [("v", capp)])
        self.misc[("xlnT_done", par)] = ctr
        cxT = self.evac_psT(self.xlnT[:, par, :, :],
                            pre_waits=[("t", self.misc.get(("xlnT_rd", par), 0))])

        gelu_done = []
        for grp in range(4):
            self.pe_acquire("psW")
            self.wait("tensor", "v", cxT)
            for half in range(2):
                for cc in range(8):
                    self.op("tensor", lambda E, par=par, cc=cc, grp=grp, half=half: E.matmul(
                        self.psW[:, half * 512:(half + 1) * 512],
                        self.xlnT[:, par, cc, :],
                        self.wfcT[:, cc, grp * 1024 + half * 512:grp * 1024 + (half + 1) * 512],
                        start=(cc == 0), stop=(cc == 7)),
                        inc="t" if (half == 1 and cc == 7) else None)
            self.pe_done("psW")
            if grp == 3:
                self.misc[("xlnT_rd", par)] = self.cnt["t"]
            self.rd_wait("scalar", "psW")
            self.wait("scalar", "t", self.misc.get(("h_rd", par), 0))
            self.wait("scalar", "v", self.misc.get(("h_vrd", par), 0))
            self.op("scalar", lambda E, par=par, grp=grp: E.activation(
                self.h_sb[:, par, grp * 1024:(grp + 1) * 1024], self.psW[:, :],
                AF.Gelu, scale=0.5,
                accum_out=self.hsums[:, par, grp:grp + 1]), inc="s")
            self.rd_done("psW", "s", self.cnt["s"])
            gelu_done.append(self.cnt["s"])
        cgel = gelu_done[-1]

        self.wait("vector", "s", cgel)
        self.op("vector", lambda E, par=par, t=t: E.reduce_sum(
            self.st["sxh"][:, t:t + 1], self.h_sb[:, par, :], AX.X), inc="v")
        cvh_ = self.cnt["v"]
        for grp in range(4):
            self.op("vector", lambda E, par=par, grp=grp: E.scalar_tensor_tensor(
                self.sq_scr[:, :], self.h_sb[:, par, grp * 1024:(grp + 1) * 1024],
                1.0, self.h_sb[:, par, grp * 1024:(grp + 1) * 1024],
                ALU.mult, ALU.mult, accum_out=self.hsq4[:, par, grp:grp + 1]),
                inc="v")
            self.wait("vector", "v", self.cnt["v"])
        self.op("vector", lambda E, par=par, t=t: E.reduce_sum(
            self.st["xxh"][:, t:t + 1], self.hsq4[:, par, :], AX.X), inc="v")
        cvh2 = self.cnt["v"]
        self.misc[("b_hstats", t)] = max(cvh_, cvh2)

    def _b_tail(self, t):
        par = t % 2
        cmuh, crh = self.ln_finalize(self.st, "h", t, HF, self.misc[("b_hstats", t)], None)
        if (t - 1) in self.ln4_pend:
            cv4, cs4 = self.ln4_pend.pop(t - 1)
            cmu4, cr4 = self.ln_finalize(self.st, "4", t - 1, D, cv4, cs4)
            self._plan_ln4_apply(t - 1, cmu4, cr4)

        self.wait("vector", "v", cmuh)
        self.wait("vector", "s", crh)
        self.op("vector", lambda E, par=par, t=t: E.tensor_scalar(
            self.h_sb[:, par, :], self.h_sb[:, par, :],
            self.st["muh"][:, t:t + 1], self.st["rh"][:, t:t + 1],
            ALU.subtract, ALU.mult), inc="v")
        chln = self.cnt["v"]
        self.misc[("h_vrd", par)] = chln

        pb = self.psB[:, :].bitcast(BF16)
        hT_end = []
        for rnd in range(4):
            self.pe_acquire("psB")
            self.wait("tensor", "v", chln)
            for j in range(8):
                self.op("tensor", lambda E, rnd=rnd, j=j, par=par, pb=pb: E.transpose(
                    pb[:, j * 128:(j + 1) * 128],
                    self.h_sb[:, par, (rnd * 8 + j) * 128:(rnd * 8 + j + 1) * 128],
                    self.ident[:, :]),
                    inc="t" if j == 7 else None)
            self.pe_done("psB")
            self.misc[("h_rd", par)] = self.cnt["t"]
            self.rd_wait("vector", "psB")
            self.waits("vector", [("t", self.misc.get("hlnT_rd", 0))])
            self.op("vector", lambda E, rnd=rnd, pb=pb: E.tensor_copy(
                self.hlnT[:, rnd * 8:(rnd + 1) * 8, :],
                pb.rearrange("p (c f) -> p c f", c=8)), inc="v")
            hT_end.append(self.cnt["v"])
            self.rd_done("psB")

        self.pe_acquire("ps0")
        self.pe_acquire("ps1")
        self.wait("tensor", "v", hT_end[-1])
        self.wait("tensor", "v", self.misc.get("wprojT_ready", 0))
        for cc in range(32):
            self.op("tensor", lambda E, cc=cc: E.matmul(
                self.ps0[:, 0:512], self.hlnT[:, cc, :],
                self.wprojT[:, cc, 0:512],
                start=(cc == 0), stop=(cc == 31)))
            self.op("tensor", lambda E, cc=cc: E.matmul(
                self.ps1[:, 0:512], self.hlnT[:, cc, :],
                self.wprojT[:, cc, 512:1024],
                start=(cc == 0), stop=(cc == 31)),
                inc="t" if cc == 31 else None)
        self.pe_done("ps0", "ps1")
        self.misc["hlnT_rd"] = self.cnt["t"]

        self.rd_wait("vector", "ps0")
        self.rd_wait("vector", "ps1")
        self.wait("vector", "v", self.misc.get("c_bc_ready", 0))
        self.wait("vector", "v", self.misc.get(("y4_free", par), 0))
        self.op("vector", lambda E, par=par: E.tensor_add(
            self.y_att[:, par, 0:512], self.ps0[:, 0:512], self.c_bc[:, 0:512]))
        self.op("vector", lambda E, par=par: E.tensor_add(
            self.y_att[:, par, 512:1024], self.ps1[:, 0:512], self.c_bc[:, 512:1024]), inc="v")
        cy4 = self.cnt["v"]
        self.rd_done("ps0")
        self.rd_done("ps1")

        self.wait("vector", "v", cy4)
        self.op("vector", lambda E, par=par, t=t: E.reduce_sum(
            self.st["sx4"][:, t:t + 1], self.y_att[:, par, :], AX.X), inc="v")
        cv4 = self.cnt["v"]
        self.op("vector", lambda E, par=par, t=t: E.scalar_tensor_tensor(
            self.sq_scr[:, :], self.y_att[:, par, :], 1.0, self.y_att[:, par, :],
            ALU.mult, ALU.mult, accum_out=self.st["xx4"][:, t:t + 1]), inc="v")
        cv4b = self.cnt["v"]
        self.ln4_pend[t] = (max(cv4, cv4b), None)


    def _plan_ln4_apply(self, t, cmu4, cr4):
        par = t % 2
        xb = t % self.XR
        self.wait("vector", "v", cmu4)
        self.wait("vector", "s", cr4)
        self.wait("vector", "g", self.misc.get(("ln2t_freeB", par), 0))
        self.op("vector", lambda E, par=par, t=t: E.tensor_scalar(
            self.ln2tmp[:, par, :], self.y_att[:, par, :],
            self.st["mu4"][:, t:t + 1], self.st["r4"][:, t:t + 1],
            ALU.subtract, ALU.mult), inc="v")
        cln4 = self.cnt["v"]
        self.misc[("y4_free", par)] = cln4
        self.wait("gpsimd", "v", cln4)
        self.op("gpsimd", lambda E, xb=xb: E.tensor_add(
            self.x_ring[:, xb, :], self.x_ring[:, xb, :], self.ln2tmp[:, par, :]), inc="g")
        cadd = self.cnt["g"]
        self.misc[("outdone", t)] = cadd
        self.misc[("ln2t_freeB", par)] = cadd
        self.wait("sync", "g", cadd)
        self.op("sync", lambda E, t=t, xb=xb: E.dma_start(
            self.out_d[t * 128:(t + 1) * 128, :], self.x_ring[:, xb, :]), inc16="dout%d" % xb)
        self.misc[("out_tile", t)] = self.cnt["dout%d" % xb]

    def build(self):
        self.init_gamma()
        self.plan_setup()
        self.plan_prefix()
        self.plan_phaseA()
        self.plan_quant()
        self.plan_phaseB()
        return self.nc


_CACHED_NC = None


def _get_nc():
    global _CACHED_NC
    if _CACHED_NC is None:
        _CACHED_NC = Builder().build()
    return _CACHED_NC


def kernel(L0_attn_x, L0_attn_Wq, L0_attn_Wk, L0_attn_Wv, L0_attn_Wo,
           L0_attn_bo, L0_ffn_W_fc, L0_ffn_ln_w, L0_ffn_ln_b, L0_ffn_W_proj):
    X = np.ascontiguousarray(np.asarray(L0_attn_x, dtype=np.float32))
    Wq = np.ascontiguousarray(np.asarray(L0_attn_Wq, dtype=np.float32))
    Wk = np.ascontiguousarray(np.asarray(L0_attn_Wk, dtype=np.float32))
    Wv = np.ascontiguousarray(np.asarray(L0_attn_Wv, dtype=np.float32))
    Wo = np.ascontiguousarray(np.asarray(L0_attn_Wo, dtype=np.float32))
    bo = np.ascontiguousarray(np.asarray(L0_attn_bo, dtype=np.float32)).reshape(1, D)
    Wfc = np.ascontiguousarray(np.asarray(L0_ffn_W_fc, dtype=np.float32))
    lnw = np.ascontiguousarray(np.asarray(L0_ffn_ln_w, dtype=np.float32)).reshape(1, HF)
    lnb = np.ascontiguousarray(np.asarray(L0_ffn_ln_b, dtype=np.float32)).reshape(1, HF)
    Wpj = np.ascontiguousarray(np.asarray(L0_ffn_W_proj, dtype=np.float32))

    nc = _get_nc()
    in_maps = []
    for c in range(8):
        b, s = c // 2, c % 2
        xc = np.ascontiguousarray(X[b, s * NTOK:(s + 1) * NTOK, :])
        xp = np.ascontiguousarray(X[b, 0:NTOK, :])
        mc = np.full((128, 1), 1.0 if s == 1 else 0.0, np.float32)
        mb = np.full((128, 1), 0.0 if s == 1 else -200.0, np.float32)
        in_maps.append({
            "x": xc, "xprev": xp, "mc": mc, "mb": mb,
            "wq": Wq, "wk": Wk, "wv": Wv, "wo": Wo, "bo": bo,
            "wfc": Wfc, "lnw": lnw, "lnb": lnb, "wproj": Wpj,
        })
    res = run_bass_kernel_spmd(nc, in_maps, core_ids=list(range(8)))
    out = np.empty((4, 2, NTOK, D), np.float32)
    for c in range(8):
        out[c // 2, c % 2] = res.results[c]["out"]
    return out.reshape(4, 4096, D)


# version 08
# speedup vs baseline: 1.0761x; 1.0761x over previous
import numpy as np
from contextlib import ExitStack

import concourse.bass as bass
import concourse.mybir as mybir
from concourse.bass_utils import run_bass_kernel_spmd

F32 = mybir.dt.float32
BF16 = mybir.dt.bfloat16
AF = mybir.ActivationFunctionType
ALU = mybir.AluOpType
AX = mybir.AxisListType

D = 1024
HF = 4096
NT = 16
NTOK = NT * 128
LN_EPS = 1e-5
CTX_EPS = 1e-6
ABS_EPS = 1e-6

ENGINES = ("sync", "gpsimd", "vector", "scalar", "tensor")


class Builder:
    def __init__(self):
        nc = bass.Bass("TRN2", target_bir_lowering=False)
        self.nc = nc
        self.es = ExitStack()
        self.cnt = {}
        self.items = None
        self.ps_cons = {}
        self.ps_prod = {}
        self.misc = {}
        self.x_free = {}

        dt = nc.dram_tensor
        self.x_d = dt("x", [NTOK, D], F32, kind="ExternalInput").ap()
        self.xprev_d = dt("xprev", [NTOK, D], F32, kind="ExternalInput").ap()
        self.mc_d = dt("mc", [128, 1], F32, kind="ExternalInput").ap()
        self.mb_d = dt("mb", [128, 1], F32, kind="ExternalInput").ap()
        self.wq_d = dt("wq", [D, D], F32, kind="ExternalInput").ap()
        self.wk_d = dt("wk", [D, D], F32, kind="ExternalInput").ap()
        self.wv_d = dt("wv", [D, D], F32, kind="ExternalInput").ap()
        self.wo_d = dt("wo", [D, D], F32, kind="ExternalInput").ap()
        self.bo_d = dt("bo", [1, D], F32, kind="ExternalInput").ap()
        self.wfc_d = dt("wfc", [HF, D], F32, kind="ExternalInput").ap()
        self.lnw_d = dt("lnw", [1, HF], F32, kind="ExternalInput").ap()
        self.lnb_d = dt("lnb", [1, HF], F32, kind="ExternalInput").ap()
        self.wproj_d = dt("wproj", [D, HF], F32, kind="ExternalInput").ap()
        self.out_d = dt("out", [NTOK, D], F32, kind="ExternalOutput").ap()
        self.x2sp_d = dt("x2spill", [NTOK, D], F32).ap()

        names = ["dx", "dw", "dq0", "dq1", "dx0", "dx1", "dx2",
                 "dsp0", "dsp1", "dout0", "dout1", "dout2", "v", "s", "t", "g"]
        self.sems = {nm: self.es.enter_context(nc.semaphore(nm)) for nm in names}
        for nm in names:
            self.cnt[nm] = 0

        self.psA = self._alloc(self.es, "psA", [128, 512], F32, psum=True)
        self.psB = self._alloc(self.es, "psB", [128, 512], F32, psum=True)
        self.psW = self._alloc(self.es, "psW", [128, 1024], F32, psum=True)
        self.ps0 = self._alloc(self.es, "ps0", [128, 512], F32, psum=True)
        self.ps1 = self._alloc(self.es, "ps1", [128, 512], F32, psum=True)
        self.psT = self._alloc(self.es, "psT", [128, 1024], BF16, psum=True)
        self.psS = self._alloc(self.es, "psS", [128, 512], F32, psum=True)
        self.s_pack_started = set()

        sb = lambda n, s, d=F32: self._alloc(self.es, n, s, d)
        self.ident = sb("ident", [128, 128], BF16)
        self.ident_f = sb("ident_f", [32, 32])
        self.ones_row_b = sb("ones_row_b", [1, 128], BF16)
        self.zrow_b = sb("zrow_b", [1, 512], BF16)
        self.ones_col_f = sb("ones_col_f", [128, 1])
        self.ones_row_f = sb("ones_row_f", [1, 128])
        self.half_sc = sb("half_sc", [1, 1])
        self.mc_sb = sb("mc_sb", [128, 1])
        self.mb_sb = sb("mb_sb", [128, 1])
        self.XR = 3
        self.x_ring = sb("x_ring", [128, self.XR, D])
        self.xln = sb("xln", [128, 2, D], BF16)
        self.xlnT = sb("xlnT", [128, 2, 8, 128], BF16)
        self.y_att = sb("y_att", [128, 2, D])
        self.ln2tmp = sb("ln2tmp", [128, 2, D])
        self.sq_scr = sb("sq_scr", [128, D], BF16)
        self.st = {}
        for nm in ["sx1", "xx1", "mu1", "r1", "tm1",
                   "sx2", "xx2", "mu2", "r2", "tm2",
                   "sx3", "xx3", "mu3", "r3", "tm3",
                   "sx4", "xx4", "mu4", "r4", "tm4",
                   "sxh", "xxh", "muh", "rh", "tmh"]:
            self.st[nm] = sb("st_" + nm, [128, NT])
        self.gam_fc = sb("gam_fc", [128, 1])
        self.gam_pj = sb("gam_pj", [128, 1])
        self.gam_scr = sb("gam_scr", [128, 1])
        self.thr_fc = sb("thr_fc", [128, 1])
        self.thr_fcn = sb("thr_fcn", [128, 1])
        self.thr_pj = sb("thr_pj", [128, 1])
        self.thr_pjn = sb("thr_pjn", [128, 1])

        self.att_scope = ExitStack()
        sba = lambda n, s, d=BF16: self._alloc(self.att_scope, n, s, d)
        self.wq_sb = sba("wq_sb", [128, 8, D])
        self.wk_sb = sba("wk_sb", [128, 8, D])
        self.wv_sb = sba("wv_sb", [128, 8, D])
        self.wo_sb = sba("wo_sb", [128, 8, D])

        self.a_scope = ExitStack()
        sbA = lambda n, s, d=F32: self._alloc(self.a_scope, n, s, d)
        self.bo_row = sbA("bo_row", [1, D])
        self.bo_bc = sbA("bo_bc", [128, D])
        self.mask_g = sbA("mask_g", [128, 4, 256], BF16)
        self.mask_0 = sbA("mask_0", [128, 4, 256], BF16)
        self.qfull = sbA("qfull", [128, 2, D], BF16)
        self.qT = sbA("qT", [128, 2, 8, 128], BF16)
        self.expq = sbA("expq", [128, 768])
        self.zq = sbA("zq", [128, 2, 12])
        self.zqr = sbA("zqr", [128, 2, 12])
        self.kgl = sbA("kgl", [128, 2, 768], BF16)
        self.elu1 = sbA("elu1", [128, 768])
        self.elu2 = sbA("elu2", [128, 768], BF16)
        self.vhat = sbA("vhat", [128, 2, 12, 65], BF16)
        self.kloc = sbA("kloc", [128, 256], BF16)
        self.klocT = sbA("klocT", [128, 2, NT * 128 + 128], BF16)
        self.vloc = sbA("vloc", [128, NT + 1, 4, 65], BF16)
        self.expl = sbA("expl", [128, 4, 256], BF16)
        self.explT = sbA("explT", [128, 8, 128], BF16)
        self.attnl = sbA("attnl", [128, 256], BF16)
        self.zl = sbA("zl", [128, 4])
        self.zlr = sbA("zlr", [128, 4])
        self.ctxp = sbA("ctxp", [128, 2, 2, 192], BF16)
        self.zsn = sbA("zsn", [128, 2, 3])
        self.zsr = sbA("zsr", [128, 2, 3])
        self.attnT = sbA("attnT", [128, 2, 8, 128], BF16)
        self.x2t = sbA("x2t", [128, 2, D])
        self.stp = {}
        for nm in ["sx1p", "xx1p", "mu1p", "r1p", "tm1p"]:
            self.stp[nm] = sbA("st_" + nm, [128, NT])

        self.q_scope = ExitStack()
        sbQ = lambda n, s, d=F32: self._alloc(self.q_scope, n, s, d, side="right")
        self.wg_ring = sbQ("wg_ring", [128, 2, 1024])

    def _alloc(self, scope, name, shape, dtt, psum=False, side=None):
        f = self.nc.psum_tensor if psum else self.nc.sbuf_tensor
        kw = {"side": side} if side else {}
        return scope.enter_context(f(name, shape, dtt, **kw))

    def begin(self):
        self.items = {e: [] for e in ENGINES}

    def op(self, eng, fn, inc=None, inc16=None):
        incs = []
        ret = None
        if inc is not None:
            for nm in ([inc] if isinstance(inc, str) else inc):
                self.cnt[nm] += 1
                incs.append((nm, 1))
                if ret is None:
                    ret = self.cnt[nm]
        if inc16 is not None:
            self.cnt[inc16] += 16
            incs.append((inc16, 16))
            if ret is None:
                ret = self.cnt[inc16]
        self.items[eng].append(("op", fn, incs))
        return ret

    def wait(self, eng, sem, val):
        if val:
            self.items[eng].append(("wait", sem, val))

    def waits(self, eng, lst):
        for sem, val in lst:
            self.wait(eng, sem, val)

    def flush(self):
        items = self.items
        sems = self.sems
        with self.nc.Block() as blk:
            def mk(e):
                def body(E):
                    for it in items[e]:
                        if it[0] == "wait":
                            E.wait_ge(sems[it[1]], it[2])
                        else:
                            inst = it[1](E)
                            for nm, k in it[2]:
                                inst.then_inc(sems[nm], k)
                return body
            for e in ENGINES:
                if items[e]:
                    getattr(blk, e)(mk(e))
        self.items = None

    def pe_acquire(self, name):
        for sem, val in self.ps_cons.pop(name, []):
            self.wait("tensor", sem, val)

    def pe_done(self, *names):
        for name in names:
            self.ps_prod[name] = ("t", self.cnt["t"])

    def rd_wait(self, eng, name):
        p = self.ps_prod.get(name)
        if p:
            self.wait(eng, p[0], p[1])

    def rd_done(self, name, sem="v", val=None):
        if val is None:
            val = self.cnt[sem]
        self.ps_cons.setdefault(name, []).append((sem, val))

    def ln_stats(self, src_ap, st, pfx, col, width):
        self.op("vector", lambda E: E.reduce_sum(st["sx" + pfx][:, col:col + 1], src_ap, AX.X), inc="v")
        cv = self.cnt["v"]
        self.op("scalar", lambda E: E.activation(self.sq_scr[:, 0:width], src_ap, AF.Square,
                                                 accum_out=st["xx" + pfx][:, col:col + 1]), inc="s")
        return cv, self.cnt["s"]

    def ln_stats_vec(self, src_ap, st, pfx, col, scratch):
        self.op("vector", lambda E: E.reduce_sum(st["sx" + pfx][:, col:col + 1], src_ap, AX.X), inc="v")
        self.op("vector", lambda E: E.scalar_tensor_tensor(
            scratch, src_ap, 1.0, src_ap, ALU.mult, ALU.mult,
            accum_out=st["xx" + pfx][:, col:col + 1]), inc="v")
        return self.cnt["v"], None

    def ln_finalize(self, st, pfx, col, nelem, cv, cs):
        mus = st["mu" + pfx]; rs = st["r" + pfx]
        sxs = st["sx" + pfx]; xxs = st["xx" + pfx]; tms = st["tm" + pfx]
        c = slice(col, col + 1)
        self.wait("vector", "v", cv)
        if cs:
            self.wait("vector", "s", cs)
        self.op("vector", lambda E: E.tensor_scalar_mul(mus[:, c], sxs[:, c], 1.0 / nelem), inc="v")
        c1 = self.cnt["v"]
        self.wait("vector", "v", c1)
        self.op("vector", lambda E: E.tensor_mul(tms[:, c], mus[:, c], mus[:, c]), inc="v")
        self.wait("vector", "v", self.cnt["v"])
        self.op("vector", lambda E: E.scalar_tensor_tensor(
            tms[:, c], xxs[:, c], 1.0 / nelem, tms[:, c], ALU.mult, ALU.subtract), inc="v")
        self.wait("vector", "v", self.cnt["v"])
        self.op("vector", lambda E: E.tensor_scalar_add(tms[:, c], tms[:, c], LN_EPS), inc="v")
        c4 = self.cnt["v"]
        self.wait("scalar", "v", c4)
        self.op("scalar", lambda E: E.activation(rs[:, c], tms[:, c], AF.Ln), inc="s")
        self.wait("scalar", "s", self.cnt["s"])
        self.op("scalar", lambda E: E.activation(rs[:, c], rs[:, c], AF.Exp, scale=-0.5), inc="s")
        return c1, self.cnt["s"]

    def transpose8(self, src_getter, waitlist, n=8):
        self.pe_acquire("psT")
        self.waits("tensor", waitlist)
        for cc in range(n):
            self.op("tensor",
                    lambda E, cc=cc: E.transpose(self.psT[:, cc * 128:(cc + 1) * 128],
                                                 src_getter(cc), self.ident[:, :]),
                    inc="t" if cc == n - 1 else None)
        self.pe_done("psT")
        return self.cnt["t"]

    def evac_psT(self, dst_ap, n=8, pre_waits=()):
        self.rd_wait("vector", "psT")
        self.waits("vector", pre_waits)
        self.op("vector", lambda E: E.tensor_copy(
            dst_ap, self.psT[:, 0:n * 128].rearrange("p (c f) -> p c f", c=n)), inc="v")
        self.rd_done("psT")
        return self.cnt["v"]

    def plan_setup(self):
        self.begin()
        for w_sb, w_d in [(self.wq_sb, self.wq_d), (self.wo_sb, self.wo_d)]:
            self.op("gpsimd", lambda E, a=w_sb, b=w_d: E.dma_start(
                a[:, :, :], b.rearrange("(c p) f -> p c f", p=128)), inc16="dw")
        for w_sb, w_d in [(self.wk_sb, self.wk_d), (self.wv_sb, self.wv_d)]:
            self.op("gpsimd", lambda E, a=w_sb, b=w_d: E.dma_start(
                a[:, :, 0:768], b[:, 256:1024].rearrange("(c p) f -> p c f", p=128)), inc16="dw")
            self.op("gpsimd", lambda E, a=w_sb, b=w_d: E.dma_start(
                a[:, :, 768:1024], b[:, 0:256].rearrange("(c p) f -> p c f", p=128)), inc16="dw")
        self.op("sync", lambda E: E.dma_start(self.mc_sb[:, :], self.mc_d), inc16="dx")
        self.op("sync", lambda E: E.dma_start(self.mb_sb[:, :], self.mb_d), inc16="dx")
        self.op("sync", lambda E: E.dma_start(self.bo_row[:, :], self.bo_d), inc16="dx")
        cdx = self.cnt["dx"]

        g = "gpsimd"
        self.op(g, lambda E: E.memset(self.ident[:, :], 0.0), inc="g")
        self.wait(g, "g", self.cnt["g"])
        self.op(g, lambda E: E.affine_select(out=self.ident[:, :], in_=self.ident[:, :],
                                             compare_op=ALU.not_equal, fill=1.0, base=0,
                                             pattern=[[-1, 128]], channel_multiplier=1))
        self.op(g, lambda E: E.memset(self.ident_f[:, :], 0.0), inc="g")
        self.wait(g, "g", self.cnt["g"])
        self.op(g, lambda E: E.affine_select(out=self.ident_f[:, :], in_=self.ident_f[:, :],
                                             compare_op=ALU.not_equal, fill=1.0, base=0,
                                             pattern=[[-1, 32]], channel_multiplier=1))
        self.op(g, lambda E: E.memset(self.ones_col_f[:, :], 1.0))
        self.op(g, lambda E: E.memset(self.ones_row_f[:, :], 1.0))
        self.op(g, lambda E: E.memset(self.half_sc[:, :], 0.5))
        self.op(g, lambda E: E.memset(self.ones_row_b[:, :], 1.0))
        self.op(g, lambda E: E.memset(self.zrow_b[:, :], 0.0))
        self.op(g, lambda E: E.memset(self.vloc[:, :, :, :], 1.0))
        self.op(g, lambda E: E.memset(self.vhat[:, :, :, :], 1.0))
        self.op(g, lambda E: E.memset(self.mask_g[:, :, :], 1.0), inc="g")
        self.wait(g, "g", self.cnt["g"])
        for h in range(4):
            self.op(g, lambda E, h=h: E.affine_select(
                out=self.mask_g[:, h, 128:256], in_=self.mask_g[:, h, 128:256],
                compare_op=ALU.is_ge, fill=0.0, base=0,
                pattern=[[-1, 128]], channel_multiplier=1),
                inc="g" if h == 3 else None)
        cg = self.cnt["g"]

        self.wait("vector", "dx", cdx)
        self.wait("vector", "g", cg)
        self.op("vector", lambda E: E.tensor_copy(self.mask_0[:, :, :], self.mask_g[:, :, :]), inc="v")
        self.wait("vector", "v", self.cnt["v"])
        for h in range(4):
            self.op("vector", lambda E, h=h: E.tensor_scalar_mul(
                self.mask_0[:, h, 0:128], self.mask_0[:, h, 0:128], self.mc_sb[:, 0:1]),
                inc="v" if h == 3 else None)

        self.wait("tensor", "dx", cdx)
        self.wait("tensor", "g", cg)
        for half in range(2):
            self.pe_acquire("psS")
            self.op("tensor", lambda E, half=half: E.matmul(
                self.psS[:, 0:512], self.ones_row_f[:, :],
                self.bo_row[:, half * 512:(half + 1) * 512], start=True, stop=True), inc="t")
            self.pe_done("psS")
            self.rd_wait("vector", "psS")
            self.op("vector", lambda E, half=half: E.tensor_copy(
                self.bo_bc[:, half * 512:(half + 1) * 512], self.psS[:, 0:512]), inc="v")
            self.rd_done("psS")
        self.flush()

    def init_gamma(self):
        tiles = []
        for i in range(32):
            tiles.append(("fc", self.wfc_d[i * 128:(i + 1) * 128, :]))
        for i in range(8):
            for j in range(4):
                tiles.append(("pj", self.wproj_d[i * 128:(i + 1) * 128, j * 1024:(j + 1) * 1024]))
        self.gamma_tiles = tiles
        self.gamma_i = 0
        self.gamma_loads = []
        self.gamma_red = []

    def plan_gamma(self, n):
        for _ in range(n):
            i = self.gamma_i
            if i >= len(self.gamma_tiles):
                return
            kind, ap = self.gamma_tiles[i]
            buf = i % 2
            if i >= 2:
                self.wait("scalar", "v", self.gamma_red[i - 2])
            self.op("scalar", lambda E, ap=ap, buf=buf: E.dma_start(self.wg_ring[:, buf, :], ap), inc16="dq%d" % buf)
            self.gamma_loads.append(self.cnt["dq%d" % buf])
            self.wait("vector", "dq%d" % buf, self.gamma_loads[i])
            tgt = self.gam_fc if kind == "fc" else self.gam_pj
            first = (i == 0) or (kind == "pj" and self.gamma_tiles[i - 1][0] == "fc")
            if first:
                self.op("vector", lambda E, buf=buf, tgt=tgt: E.tensor_reduce(
                    tgt[:, :], self.wg_ring[:, buf, :], AX.X, ALU.add,
                    apply_absolute_value=True), inc="v")
            else:
                self.wait("vector", "v", self.misc.get("gam_add", 0))
                self.op("vector", lambda E, buf=buf: E.tensor_reduce(
                    self.gam_scr[:, :], self.wg_ring[:, buf, :], AX.X, ALU.add,
                    apply_absolute_value=True), inc="v")
                self.wait("vector", "v", self.cnt["v"])
                self.op("vector", lambda E, tgt=tgt: E.tensor_add(tgt[:, :], tgt[:, :], self.gam_scr[:, :]), inc="v")
                self.misc["gam_add"] = self.cnt["v"]
            self.gamma_red.append(self.cnt["v"])
            self.gamma_i = i + 1

    def plan_chunk_S(self, par, r0, waitlist):
        self.waits("tensor", waitlist)
        for name in ("psA", "psB"):
            for sem, val in self.ps_cons.pop(name, []):
                self.wait("tensor", sem, val)
        for h in range(4, 16):
            pack = self.psA if h < 10 else self.psB
            pname = "psA" if h < 10 else "psB"
            s = (h - 4) % 6
            p0 = (s % 2) * 64
            c0 = (s // 2) * 65
            self.op("tensor", lambda E, pack=pack, p0=p0, c0=c0, par=par, r0=r0, h=h: E.matmul(
                pack[p0:p0 + 64, c0:c0 + 65],
                self.kgl[r0:r0 + 64, par, (h - 4) * 64:(h - 3) * 64],
                self.vhat[r0:r0 + 64, par, h - 4, :],
                start=False, stop=False, skip_group_check=True),
                inc="t" if h == 15 else None)
        self.pe_done("psA", "psB")
        return self.cnt["t"]

    def plan_snapshot(self, ring):
        self.wait("vector", "t", self.misc.get(("ctx_rd", ring), 0))
        for ipk, pack in enumerate((self.psA, self.psB)):
            pname = "psA" if ipk == 0 else "psB"
            self.rd_wait("vector", pname)
            self.op("vector", lambda E, pack=pack, ipk=ipk: E.tensor_scalar_add(
                self.zsn[:, ipk, :],
                pack[:, 0:195].rearrange("p (s e) -> p s e", s=3)[:, :, 64], CTX_EPS), inc="v")
            self.wait("vector", "v", self.cnt["v"])
            self.op("vector", lambda E, ipk=ipk: E.reciprocal(
                self.zsr[:, ipk, :], self.zsn[:, ipk, :]), inc="v")
            self.wait("vector", "v", self.cnt["v"])
            self.op("vector", lambda E, pack=pack, ipk=ipk, ring=ring: E.tensor_tensor(
                self.ctxp[:, ring, ipk, :].rearrange("p (s e) -> p s e", s=3),
                pack[:, 0:195].rearrange("p (s e) -> p s e", s=3)[:, :, 0:64],
                self.zsr[:, ipk, :].unsqueeze(2).broadcast_to([128, 3, 64]),
                ALU.mult), inc="v")
            self.rd_done(pname)
        return self.cnt["v"]

    def plan_prefix(self):
        self.begin()
        for pack in (self.psA, self.psB):
            self.op("tensor", lambda E, pack=pack: E.matmul(
                pack[:, 0:512], self.ones_row_b[:, :], self.zrow_b[:, :],
                start=True, stop=False, skip_group_check=True), inc="t")
        self.pe_done("psA", "psB")
        for p in range(NT):
            par = p % 2
            xb = p % self.XR
            if p >= self.XR:
                sem, val = self.x_free.get(p - self.XR, (None, 0))
                if sem:
                    self.wait("sync", sem, val)
            dsem = "dx%d" % xb
            self.op("sync", lambda E, p=p, xb=xb: E.dma_start(
                self.x_ring[:, xb, :], self.xprev_d[p * 128:(p + 1) * 128, :]), inc16=dsem)
            cl = self.cnt[dsem]

            self.wait("vector", dsem, cl)
            self.wait("scalar", dsem, cl)
            cv, cs = self.ln_stats(self.x_ring[:, xb, :], self.stp, "1p", p, D)
            cmu, cr = self.ln_finalize(self.stp, "1p", p, D, cv, cs)
            self.wait("vector", "v", cmu)
            self.wait("vector", "s", cr)
            self.wait("vector", "t", self.misc.get(("xlnT_done", par), 0))
            self.op("vector", lambda E, xb=xb, par=par, p=p: E.tensor_scalar(
                self.xln[:, par, :], self.x_ring[:, xb, :],
                self.stp["mu1p"][:, p:p + 1], self.stp["r1p"][:, p:p + 1],
                ALU.subtract, ALU.mult), inc="v")
            capp = self.cnt["v"]
            self.x_free[p] = ("v", capp)

            ctr = self.transpose8(lambda cc, par=par: self.xln[:, par, cc * 128:(cc + 1) * 128],
                                  [("v", capp)])
            self.misc[("xlnT_done", par)] = ctr
            cxT = self.evac_psT(self.xlnT[:, par, :, :],
                                pre_waits=[("t", self.misc.get(("xlnT_rd", par), 0))])

            self.wait("tensor", "v", cxT)
            self.wait("tensor", "dw", 96)
            self.pe_acquire("ps0")
            self.pe_acquire("ps1")
            for cc in range(8):
                self.op("tensor", lambda E, par=par, cc=cc: E.matmul(
                    self.ps0[:, 0:512], self.xlnT[:, par, cc, :], self.wk_sb[:, cc, 0:512],
                    start=(cc == 0), stop=(cc == 7)))
                self.op("tensor", lambda E, par=par, cc=cc: E.matmul(
                    self.ps1[:, 0:256], self.xlnT[:, par, cc, :], self.wk_sb[:, cc, 512:768],
                    start=(cc == 0), stop=(cc == 7)),
                    inc="t" if cc == 7 else None)
            self.pe_done("ps0", "ps1")
            self.pe_acquire("psW")
            for cc in range(8):
                self.op("tensor", lambda E, par=par, cc=cc: E.matmul(
                    self.psW[:, 0:512], self.xlnT[:, par, cc, :], self.wv_sb[:, cc, 0:512],
                    start=(cc == 0), stop=(cc == 7)))
                self.op("tensor", lambda E, par=par, cc=cc: E.matmul(
                    self.psW[:, 512:768], self.xlnT[:, par, cc, :], self.wv_sb[:, cc, 512:768],
                    start=(cc == 0), stop=(cc == 7)),
                    inc="t" if cc == 7 else None)
            self.pe_done("psW")
            self.misc[("xlnT_rd", par)] = self.cnt["t"]

            self.rd_wait("scalar", "ps0")
            self.rd_wait("scalar", "ps1")
            self.wait("scalar", "v", self.misc.get(("elu2_free", par), 0))
            self.op("scalar", lambda E: E.activation(self.elu1[:, 0:512], self.ps0[:, 0:512], AF.Relu, scale=-1.0))
            self.op("scalar", lambda E: E.activation(self.elu1[:, 512:768], self.ps1[:, 0:256], AF.Relu, scale=-1.0), inc="s")
            ce1 = self.cnt["s"]
            self.wait("scalar", "s", ce1)
            self.op("scalar", lambda E: E.activation(self.elu2[:, :], self.elu1[:, :], AF.Exp,
                                                     scale=-1.0, bias=self.mb_sb[:, 0:1]), inc="s")
            ce2 = self.cnt["s"]
            semv = self.misc.get(("kv_free", par))
            if semv:
                self.wait("vector", semv[0], semv[1])
            self.wait("vector", "s", ce2)
            self.op("vector", lambda E, par=par: E.tensor_scalar(
                self.kgl[:, par, 0:512], self.ps0[:, 0:512], 0.0, self.mc_sb[:, 0:1], ALU.max, ALU.mult))
            self.op("vector", lambda E, par=par: E.tensor_scalar(
                self.kgl[:, par, 512:768], self.ps1[:, 0:256], 0.0, self.mc_sb[:, 0:1], ALU.max, ALU.mult), inc="v")
            ckm = self.cnt["v"]
            self.rd_done("ps0")
            self.rd_done("ps1")
            self.wait("vector", "v", ckm)
            self.op("vector", lambda E, par=par: E.tensor_add(
                self.kgl[:, par, :], self.kgl[:, par, :], self.elu2[:, :]), inc="v")
            ckg = self.cnt["v"]
            self.misc[("elu2_free", par)] = ckg

            self.rd_wait("vector", "psW")
            self.op("vector", lambda E, par=par: E.tensor_copy(
                self.vhat[:, par, :, 0:64],
                self.psW[:, 0:768].rearrange("p (h e) -> p h e", h=12)), inc="v")
            cvh = self.cnt["v"]
            self.rd_done("psW")

            if p == NT - 1:
                self.pe_acquire("ps1")
                self.pe_acquire("psW")
                self.wait("tensor", "v", cxT)
                for cc in range(8):
                    self.op("tensor", lambda E, par=par, cc=cc: E.matmul(
                        self.ps1[:, 256:512], self.xlnT[:, par, cc, :], self.wk_sb[:, cc, 768:1024],
                        start=(cc == 0), stop=(cc == 7)))
                    self.op("tensor", lambda E, par=par, cc=cc: E.matmul(
                        self.psW[:, 768:1024], self.xlnT[:, par, cc, :], self.wv_sb[:, cc, 768:1024],
                        start=(cc == 0), stop=(cc == 7)),
                        inc="t" if cc == 7 else None)
                self.pe_done("ps1", "psW")
                self.misc[("xlnT_rd", par)] = self.cnt["t"]
                self.rd_wait("vector", "ps1")
                self.op("vector", lambda E: E.tensor_copy(self.kloc[:, :], self.ps1[:, 256:512]), inc="v")
                ckl = self.cnt["v"]
                self.rd_done("ps1")
                ctr = self.transpose8(lambda cc: self.kloc[:, cc * 128:(cc + 1) * 128],
                                      [("v", ckl)], n=2)
                self.rd_wait("vector", "psT")
                self.op("vector", lambda E: E.tensor_copy(self.klocT[:, 0, 0:128], self.psT[:, 0:128]))
                self.op("vector", lambda E: E.tensor_copy(self.klocT[:, 1, 0:128], self.psT[:, 128:256]), inc="v")
                self.rd_done("psT")
                self.rd_wait("vector", "psW")
                self.op("vector", lambda E: E.tensor_copy(
                    self.vloc[:, 0, :, 0:64],
                    self.psW[:, 768:1024].rearrange("p (h e) -> p h e", h=4)), inc="v")
                chv = self.cnt["v"]
                self.rd_done("psW")
                self.misc["halo_ready"] = chv

            for half in range(2):
                self.plan_chunk_S(par, half * 64, [("v", ckg), ("v", cvh)])
            self.misc[("kv_free", par)] = ("t", self.cnt["t"])

            self.plan_gamma(2)

        csn = self.plan_snapshot(1)
        self.misc[("ctx_ready", 1)] = csn
        self.flush()

    def plan_phaseA(self, ntiles=NT, upto=99):
        self.begin()
        for t in range(ntiles):
            par = t % 2
            xb = t % self.XR
            if t < self.XR:
                pv = NT - self.XR + t
                sem, val = self.x_free.get(pv, (None, 0))
                if sem:
                    self.wait("sync", sem, val)
            else:
                self.wait("sync", "g", self.misc.get(("xdone", t - self.XR), 0))
            dsem = "dx%d" % xb
            self.op("sync", lambda E, t=t, xb=xb: E.dma_start(
                self.x_ring[:, xb, :], self.x_d[t * 128:(t + 1) * 128, :]), inc16=dsem)
            cl = self.cnt[dsem]

            self.wait("vector", dsem, cl)
            self.wait("scalar", dsem, cl)
            cv, cs = self.ln_stats(self.x_ring[:, xb, :], self.st, "1", t, D)
            cmu, cr = self.ln_finalize(self.st, "1", t, D, cv, cs)
            self.wait("vector", "v", cmu)
            self.wait("vector", "s", cr)
            self.wait("vector", "t", self.misc.get(("xlnT_done", par), 0))
            self.op("vector", lambda E, xb=xb, par=par, t=t: E.tensor_scalar(
                self.xln[:, par, :], self.x_ring[:, xb, :],
                self.st["mu1"][:, t:t + 1], self.st["r1"][:, t:t + 1],
                ALU.subtract, ALU.mult), inc="v")
            capp = self.cnt["v"]

            ctr = self.transpose8(lambda cc, par=par: self.xln[:, par, cc * 128:(cc + 1) * 128],
                                  [("v", capp)])
            self.misc[("xlnT_done", par)] = ctr
            cxT = self.evac_psT(self.xlnT[:, par, :, :],
                                pre_waits=[("t", self.misc.get(("xlnT_rd", par), 0))])

            if upto < 2:
                continue
            self.wait("tensor", "v", cxT)
            self.wait("tensor", "dw", 96)
            self.pe_acquire("psW")
            for cc in range(8):
                self.op("tensor", lambda E, par=par, cc=cc: E.matmul(
                    self.psW[:, 0:512], self.xlnT[:, par, cc, :], self.wq_sb[:, cc, 0:512],
                    start=(cc == 0), stop=(cc == 7)))
                self.op("tensor", lambda E, par=par, cc=cc: E.matmul(
                    self.psW[:, 512:1024], self.xlnT[:, par, cc, :], self.wq_sb[:, cc, 512:1024],
                    start=(cc == 0), stop=(cc == 7)),
                    inc="t" if cc == 7 else None)
            self.pe_done("psW")

            self.pe_acquire("ps0")
            self.pe_acquire("ps1")
            for cc in range(8):
                self.op("tensor", lambda E, par=par, cc=cc: E.matmul(
                    self.ps0[:, 0:512], self.xlnT[:, par, cc, :], self.wk_sb[:, cc, 0:512],
                    start=(cc == 0), stop=(cc == 7)))
                self.op("tensor", lambda E, par=par, cc=cc: E.matmul(
                    self.ps1[:, 0:512], self.xlnT[:, par, cc, :], self.wk_sb[:, cc, 512:1024],
                    start=(cc == 0), stop=(cc == 7)),
                    inc="t" if cc == 7 else None)
            self.pe_done("ps0", "ps1")

            self.rd_wait("scalar", "psW")
            self.wait("scalar", "v", self.misc.get(("expq_free", par), 0))
            self.op("scalar", lambda E: E.activation(self.expq[:, :], self.psW[:, 256:1024], AF.Exp), inc="s")
            cexq = self.cnt["s"]
            self.rd_done("psW", "s", cexq)
            self.wait("vector", "s", cexq)
            self.op("vector", lambda E, par=par: E.tensor_reduce(
                self.zq[:, par, :], self.expq[:, :].rearrange("p (h e) -> p h e", h=12),
                AX.X, ALU.add), inc="v")
            self.wait("vector", "v", self.cnt["v"])
            self.op("vector", lambda E, par=par: E.reciprocal(self.zqr[:, par, :], self.zq[:, par, :]), inc="v")
            czr = self.cnt["v"]
            self.rd_wait("vector", "psW")
            self.wait("vector", "s", cexq)
            self.wait("vector", "t", self.misc.get(("qT_done", par), 0))
            self.op("vector", lambda E, par=par: E.tensor_scalar_mul(
                self.qfull[:, par, 0:256], self.psW[:, 0:256], 0.125), inc="v")
            clq = self.cnt["v"]
            self.rd_done("psW")
            self.wait("vector", "v", czr)
            self.op("vector", lambda E, par=par: E.tensor_tensor(
                self.qfull[:, par, 256:1024].rearrange("p (h e) -> p h e", h=12),
                self.expq[:, :].rearrange("p (h e) -> p h e", h=12),
                self.zqr[:, par, :].unsqueeze(2).broadcast_to([128, 12, 64]),
                ALU.mult), inc="v")
            cqf = self.cnt["v"]
            self.misc[("expq_free", par)] = cqf

            if upto < 3:
                continue
            ctr = self.transpose8(lambda cc, par=par: self.qfull[:, par, cc * 128:(cc + 1) * 128],
                                  [("v", cqf), ("v", clq)])
            self.misc[("qT_done", par)] = ctr
            cqT = self.evac_psT(self.qT[:, par, :, :],
                                pre_waits=[("t", self.misc.get(("qT_rd", par), 0))])

            if upto < 4:
                continue
            self.rd_wait("scalar", "ps0")
            self.rd_wait("scalar", "ps1")
            self.wait("scalar", "v", self.misc.get(("elu2_free", par), 0))
            self.op("scalar", lambda E: E.activation(self.elu1[:, 0:512], self.ps0[:, 0:512], AF.Relu, scale=-1.0))
            self.op("scalar", lambda E: E.activation(self.elu1[:, 512:768], self.ps1[:, 0:256], AF.Relu, scale=-1.0), inc="s")
            ce1 = self.cnt["s"]
            self.wait("scalar", "s", ce1)
            self.op("scalar", lambda E: E.activation(self.elu2[:, :], self.elu1[:, :], AF.Exp, scale=-1.0), inc="s")
            ce2 = self.cnt["s"]
            semv = self.misc.get(("kv_free", par))
            if semv:
                self.wait("vector", semv[0], semv[1])
            self.wait("vector", "s", ce2)
            self.op("vector", lambda E, par=par: E.scalar_tensor_tensor(
                self.kgl[:, par, 0:512], self.ps0[:, 0:512], 0.0, self.elu2[:, 0:512], ALU.max, ALU.add))
            self.op("vector", lambda E, par=par: E.scalar_tensor_tensor(
                self.kgl[:, par, 512:768], self.ps1[:, 0:256], 0.0, self.elu2[:, 512:768], ALU.max, ALU.add), inc="v")
            ckg = self.cnt["v"]
            self.misc[("elu2_free", par)] = ckg
            self.rd_done("ps0")
            self.wait("vector", "t", self.misc.get("kloc_rd", 0))
            self.op("vector", lambda E: E.tensor_copy(self.kloc[:, :], self.ps1[:, 256:512]), inc="v")
            ckl = self.cnt["v"]
            self.rd_done("ps1")
            ctr = self.transpose8(lambda cc: self.kloc[:, cc * 128:(cc + 1) * 128],
                                  [("v", ckl)], n=2)
            self.misc["kloc_rd"] = ctr
            self.rd_wait("vector", "psT")
            col = 128 + t * 128
            self.op("vector", lambda E, col=col: E.tensor_copy(self.klocT[:, 0, col:col + 128], self.psT[:, 0:128]))
            self.op("vector", lambda E, col=col: E.tensor_copy(self.klocT[:, 1, col:col + 128], self.psT[:, 128:256]), inc="v")
            ckT = self.cnt["v"]
            self.rd_done("psT")

            if upto < 5:
                continue
            self.pe_acquire("psW")
            for cc in range(8):
                self.op("tensor", lambda E, par=par, cc=cc: E.matmul(
                    self.psW[:, 0:512], self.xlnT[:, par, cc, :], self.wv_sb[:, cc, 0:512],
                    start=(cc == 0), stop=(cc == 7)))
                self.op("tensor", lambda E, par=par, cc=cc: E.matmul(
                    self.psW[:, 512:1024], self.xlnT[:, par, cc, :], self.wv_sb[:, cc, 512:1024],
                    start=(cc == 0), stop=(cc == 7)),
                    inc="t" if cc == 7 else None)
            self.pe_done("psW")
            self.misc[("xlnT_rd", par)] = self.cnt["t"]
            self.rd_wait("vector", "psW")
            semv = self.misc.get(("kv_free", par))
            if semv:
                self.wait("vector", semv[0], semv[1])
            self.op("vector", lambda E, par=par: E.tensor_copy(
                self.vhat[:, par, :, 0:64],
                self.psW[:, 0:768].rearrange("p (h e) -> p h e", h=12)), inc="v")
            cvh = self.cnt["v"]
            self.op("vector", lambda E, t=t: E.tensor_copy(
                self.vloc[:, t + 1, :, 0:64],
                self.psW[:, 768:1024].rearrange("p (h e) -> p h e", h=4)), inc="v")
            cvl = self.cnt["v"]
            self.rd_done("psW")

            if upto < 6:
                continue
            self.pe_acquire("ps0")
            self.pe_acquire("ps1")
            self.wait("tensor", "v", cqT)
            self.wait("tensor", "v", ckT)
            if t == 0:
                self.wait("tensor", "v", self.misc.get("halo_ready", 0))
            for h in range(4):
                tgt = self.ps0 if h < 2 else self.ps1
                coff = (h % 2) * 256
                p0 = (h % 2) * 64
                if p0 == 0:
                    self.op("tensor", lambda E, h=h, tgt=tgt, coff=coff, par=par, t=t: E.matmul(
                        tgt[:, coff:coff + 256],
                        self.qT[0:64, par, h // 2, :],
                        self.klocT[0:64, h // 2, t * 128:t * 128 + 256],
                        start=True, stop=True))
                else:
                    for mh in range(2):
                        self.op("tensor", lambda E, h=h, tgt=tgt, coff=coff, par=par, t=t, mh=mh: E.matmul(
                            tgt[mh * 64:(mh + 1) * 64, coff:coff + 256],
                            self.qT[64:128, par, h // 2, mh * 64:(mh + 1) * 64],
                            self.klocT[64:128, h // 2, t * 128:t * 128 + 256],
                            start=True, stop=True),
                            inc="t" if (h == 3 and mh == 1) else None)
            self.pe_done("ps0", "ps1")
            if upto < 61:
                continue

            self.rd_wait("scalar", "ps0")
            self.rd_wait("scalar", "ps1")
            self.wait("scalar", "t", self.misc.get("expl_rd", 0))
            self.op("scalar", lambda E: E.activation(
                self.expl[:, 0:2, :].rearrange("p h e -> p (h e)"), self.ps0[:, :], AF.Exp))
            self.op("scalar", lambda E: E.activation(
                self.expl[:, 2:4, :].rearrange("p h e -> p (h e)"), self.ps1[:, :], AF.Exp), inc="s")
            cexl = self.cnt["s"]
            self.rd_done("ps0", "s", cexl)
            self.rd_done("ps1", "s", cexl)
            if upto < 62:
                continue
            self.wait("vector", "s", cexl)
            msk = self.mask_0 if t == 0 else self.mask_g
            self.op("vector", lambda E, msk=msk: E.tensor_tensor(
                self.expl[:, :, :], self.expl[:, :, :], msk[:, :, :], ALU.mult), inc="v")
            cmsk = self.cnt["v"]
            if upto < 63:
                continue

            ctr = self.transpose8(lambda cc: self.expl[:, cc // 2, (cc % 2) * 128:(cc % 2) * 128 + 128],
                                  [("v", cmsk)])
            self.misc["expl_rd"] = ctr
            ceT = self.evac_psT(self.explT[:, :, :],
                                pre_waits=[("t", self.misc.get("explT_rd", 0))])

            if upto < 7:
                continue
            self.pe_acquire("ps1")
            self.wait("tensor", "v", ceT)
            self.wait("tensor", "v", cvl)
            for h in range(4):
                for half in range(2):
                    self.op("tensor", lambda E, h=h, half=half, t=t: E.matmul(
                        self.ps1[:, h * 65:h * 65 + 65],
                        self.explT[:, 2 * h + half, :],
                        self.vloc[:, t + half, h, :],
                        start=(half == 0), stop=(half == 1)),
                        inc="t" if (h == 3 and half == 1) else None)
            self.pe_done("ps1")
            self.misc["explT_rd"] = self.cnt["t"]

            self.rd_wait("vector", "ps1")
            self.op("vector", lambda E: E.tensor_copy(
                self.zl[:, :],
                self.ps1[:, 0:260].rearrange("p (h e) -> p h e", h=4)[:, :, 64]), inc="v")
            self.wait("vector", "v", self.cnt["v"])
            self.op("vector", lambda E: E.reciprocal(self.zlr[:, :], self.zl[:, :]), inc="v")
            self.wait("vector", "v", self.cnt["v"])
            self.wait("vector", "t", self.misc.get("attnl_rd", 0))
            self.op("vector", lambda E: E.tensor_tensor(
                self.attnl[:, :].rearrange("p (h e) -> p h e", h=4),
                self.ps1[:, 0:260].rearrange("p (h e) -> p h e", h=4)[:, :, 0:64],
                self.zlr[:, :].unsqueeze(2).broadcast_to([128, 4, 64]),
                ALU.mult), inc="v")
            cloc = self.cnt["v"]
            self.rd_done("ps1")

            ctr = self.transpose8(lambda cc: self.attnl[:, cc * 128:(cc + 1) * 128],
                                  [("v", cloc)], n=2)
            self.misc["attnl_rd"] = ctr
            self.rd_wait("vector", "psT")
            self.wait("vector", "t", self.misc.get(("wo_done", par), 0))
            self.op("vector", lambda E, par=par: E.tensor_copy(self.attnT[:, par, 0, :], self.psT[:, 0:128]))
            self.op("vector", lambda E, par=par: E.tensor_copy(self.attnT[:, par, 1, :], self.psT[:, 128:256]), inc="v")
            catl = self.cnt["v"]
            self.rd_done("psT")

            if upto < 8:
                continue
            catg = []
            for half in range(2):
                u = 2 * t + half
                ring = u % 2
                prev = 1 - ring
                r0 = half * 64
                self.pe_acquire("ps0")
                self.wait("tensor", "v", self.misc.get(("ctx_ready", prev), 0))
                self.wait("tensor", "v", cqT)
                for h in range(4, 16):
                    ipk = 0 if h < 10 else 1
                    s = (h - 4) % 6
                    pr = (h - 4) // 2
                    self.op("tensor", lambda E, h=h, ipk=ipk, s=s, pr=pr, prev=prev, par=par, r0=r0: E.matmul(
                        self.ps0[((h - 4) % 2) * 64:((h - 4) % 2) * 64 + 64, pr * 64:pr * 64 + 64],
                        self.ctxp[(s % 2) * 64:(s % 2) * 64 + 64, prev, ipk, (s // 2) * 64:(s // 2) * 64 + 64],
                        self.qT[(h % 2) * 64:(h % 2) * 64 + 64, par, h // 2, r0:r0 + 64],
                        start=True, stop=True),
                        inc="t" if h == 15 else None)
                self.pe_done("ps0")
                self.misc[("ctx_rd", prev)] = self.cnt["t"]
                self.rd_wait("vector", "ps0")
                self.op("vector", lambda E, par=par, r0=r0: E.tensor_copy(
                    self.attnT[:, par, 2:8, r0:r0 + 64],
                    self.ps0[:, 0:384].rearrange("p (pr e) -> p pr e", pr=6)), inc="v")
                catg.append(self.cnt["v"])
                self.rd_done("ps0")

                self.plan_chunk_S(par, r0, [("v", ckg), ("v", cvh)])
                csn = self.plan_snapshot(ring)
                self.misc[("ctx_ready", ring)] = csn
            self.misc[("kv_free", par)] = ("t", self.cnt["t"])
            self.misc[("qT_rd", par)] = self.cnt["t"]

            if upto < 9:
                continue
            self.pe_acquire("psW")
            self.wait("tensor", "v", catl)
            for cv_ in catg:
                self.wait("tensor", "v", cv_)
            for cc in range(8):
                self.op("tensor", lambda E, par=par, cc=cc: E.matmul(
                    self.psW[:, 0:512], self.attnT[:, par, cc, :], self.wo_sb[:, cc, 0:512],
                    start=(cc == 0), stop=(cc == 7)))
                self.op("tensor", lambda E, par=par, cc=cc: E.matmul(
                    self.psW[:, 512:1024], self.attnT[:, par, cc, :], self.wo_sb[:, cc, 512:1024],
                    start=(cc == 0), stop=(cc == 7)),
                    inc="t" if cc == 7 else None)
            self.pe_done("psW")
            self.misc[("wo_done", par)] = self.cnt["t"]

            if upto < 10:
                continue
            self.rd_wait("vector", "psW")
            self.wait("vector", "s", self.misc.get(("yatt_sfree", par), 0))
            self.op("vector", lambda E, par=par: E.tensor_add(
                self.y_att[:, par, :], self.psW[:, :], self.bo_bc[:, :]), inc="v")
            cy = self.cnt["v"]
            self.rd_done("psW")
            self.wait("vector", "v", cy)
            self.wait("scalar", "v", cy)
            cv2, cs2 = self.ln_stats(self.y_att[:, par, :], self.st, "2", t, D)
            self.misc[("yatt_sfree", par)] = cs2
            cmu2, cr2 = self.ln_finalize(self.st, "2", t, D, cv2, cs2)
            self.wait("vector", "v", cmu2)
            self.wait("vector", "s", cr2)
            self.wait("vector", "g", self.misc.get(("ln2t_free", par), 0))
            self.op("vector", lambda E, par=par, t=t: E.tensor_scalar(
                self.ln2tmp[:, par, :], self.y_att[:, par, :],
                self.st["mu2"][:, t:t + 1], self.st["r2"][:, t:t + 1],
                ALU.subtract, ALU.mult), inc="v")
            cln2 = self.cnt["v"]
            self.wait("gpsimd", "v", cln2)
            self.wait("gpsimd", "dsp%d" % par, self.misc.get(("x2_spilled", par), 0))
            sv = self.misc.get(("x2_stats", par))
            if sv:
                self.wait("gpsimd", "v", sv[0])
                self.wait("gpsimd", "s", sv[1])
            self.op("gpsimd", lambda E, par=par, xb=xb: E.tensor_add(
                self.x2t[:, par, :], self.x_ring[:, xb, :], self.ln2tmp[:, par, :]), inc="g")
            cx2 = self.cnt["g"]
            self.misc[("xdone", t)] = cx2
            self.misc[("ln2t_free", par)] = cx2

            self.wait("sync", "g", cx2)
            self.op("sync", lambda E, par=par, t=t: E.dma_start(
                self.x2sp_d[t * 128:(t + 1) * 128, :], self.x2t[:, par, :]), inc16="dsp%d" % par)
            self.misc[("x2_spilled", par)] = self.cnt["dsp%d" % par]
            self.wait("vector", "g", cx2)
            self.wait("scalar", "g", cx2)
            cv3, cs3 = self.ln_stats(self.x2t[:, par, :], self.st, "3", t, D)
            self.misc[("x2_stats", par)] = (cv3, cs3)
            self.ln_finalize(self.st, "3", t, D, cv3, cs3)

            self.plan_gamma(2)

        self.flush()

    def plan_quant(self):
        self.a_scope.close()
        self.att_scope.close()
        sb = lambda n, s, d=F32: self._alloc(self.es, n, s, d)
        sbQ = lambda n, s, d=F32: self._alloc(self.q_scope, n, s, d, side="right")
        self.wfcT = sb("wfcT", [128, 8, HF], BF16)
        self.wprojT = sb("wprojT", [128, 32, D], BF16)
        self.c_bc = sb("c_bc", [128, D])
        self.lnwcol = sb("lnwcol", [128, 32])
        self.sg1 = sbQ("sg1", [128, 2, 1024], BF16)
        self.sg2 = sbQ("sg2", [128, 2, 1024], BF16)
        self.wq2 = sbQ("wq2", [128, 2, 1024], BF16)
        self.lnw_sb = sbQ("lnw_sb", [32, 128])
        self.lnb_sb = sbQ("lnb_sb", [32, 128])
        self.c_row = sbQ("c_row", [1, D])
        self.lnbcol = sbQ("lnbcol", [128, 32], BF16)
        self.tval = sbQ("tval", [1, 2])

        self.begin()
        for eng in ENGINES:
            self.wait(eng, "dsp0", self.cnt["dsp0"])
            self.wait(eng, "dsp1", self.cnt["dsp1"])
        self.plan_gamma(len(self.gamma_tiles) - self.gamma_i)
        cv_gam = self.gamma_red[-1]

        self.op("sync", lambda E: E.dma_start(
            self.lnw_sb[:, :], self.lnw_d.rearrange("o (r c) -> (o r) c", c=128)), inc16="dx")
        self.op("sync", lambda E: E.dma_start(
            self.lnb_sb[:, :], self.lnb_d.rearrange("o (r c) -> (o r) c", c=128)), inc16="dx")
        crow = self.cnt["dx"]

        self.pe_acquire("psS")
        self.wait("tensor", "v", cv_gam)
        self.op("tensor", lambda E: E.matmul(self.psS[0:1, 0:1], self.gam_fc[:, :], self.ones_col_f[:, :],
                                             start=True, stop=True))
        self.op("tensor", lambda E: E.matmul(self.psS[0:1, 1:2], self.gam_pj[:, :], self.ones_col_f[:, :],
                                             start=True, stop=True), inc="t")
        self.pe_done("psS")
        self.rd_wait("vector", "psS")
        self.op("vector", lambda E: E.tensor_scalar(
            self.tval[:, 0:2], self.psS[0:1, 0:2], 1.0 / (HF * D), ABS_EPS, ALU.mult, ALU.add), inc="v")
        self.rd_done("psS")
        self.wait("vector", "v", self.cnt["v"])
        self.op("vector", lambda E: E.tensor_scalar_mul(self.tval[:, 0:2], self.tval[:, 0:2], 0.5), inc="v")
        ctv = self.cnt["v"]
        self.pe_acquire("psS")
        self.wait("tensor", "v", ctv)
        self.op("tensor", lambda E: E.matmul(self.psS[:, 0:1], self.ones_row_f[:, :], self.tval[:, 0:1],
                                             start=True, stop=True))
        self.op("tensor", lambda E: E.matmul(self.psS[:, 1:2], self.ones_row_f[:, :], self.tval[:, 1:2],
                                             start=True, stop=True), inc="t")
        self.pe_done("psS")
        self.rd_wait("vector", "psS")
        self.op("vector", lambda E: E.tensor_copy(self.thr_fc[:, :], self.psS[:, 0:1]))
        self.op("vector", lambda E: E.tensor_copy(self.thr_pj[:, :], self.psS[:, 1:2]), inc="v")
        self.rd_done("psS")
        self.wait("vector", "v", self.cnt["v"])
        self.op("vector", lambda E: E.tensor_scalar_mul(self.thr_fcn[:, :], self.thr_fc[:, :], -1.0))
        self.op("vector", lambda E: E.tensor_scalar_mul(self.thr_pjn[:, :], self.thr_pj[:, :], -1.0), inc="v")
        cthr = self.cnt["v"]

        self.pe_acquire("psW")
        self.wait("tensor", "dx", crow)
        self.op("tensor", lambda E: E.transpose(self.psW[:, 0:32], self.lnw_sb[:, :], self.ident_f[:, :]))
        self.op("tensor", lambda E: E.transpose(self.psW[:, 32:64], self.lnb_sb[:, :], self.ident_f[:, :]), inc="t")
        self.pe_done("psW")
        self.rd_wait("vector", "psW")
        self.op("vector", lambda E: E.tensor_scalar_mul(self.lnwcol[:, :], self.psW[:, 0:32], 0.5))
        self.op("vector", lambda E: E.tensor_scalar_mul(self.lnbcol[:, :], self.psW[:, 32:64], 0.5), inc="v")
        ccol = self.cnt["v"]
        self.rd_done("psW")

        qlast = {("add", 0): 0, ("add", 1): 0, ("tr", 0): 0, ("tr", 1): 0}

        def quant_tile(idx, src_ap, thr_p, thr_n, dst_fn):
            buf = idx % 2
            self.op("scalar", lambda E, src_ap=src_ap, buf=buf: E.dma_start(
                self.wg_ring[:, buf, :], src_ap), inc16="dq%d" % buf)
            cld = self.cnt["dq%d" % buf]
            self.wait("scalar", "dq%d" % buf, cld)
            self.wait("scalar", "v", qlast[("add", buf)])
            if idx == 0:
                self.wait("scalar", "v", cthr)
            self.op("scalar", lambda E, buf=buf, thr_n=thr_n: E.activation(
                self.sg1[:, buf, :], self.wg_ring[:, buf, :], AF.Sign, bias=thr_n[:, 0:1]))
            self.op("scalar", lambda E, buf=buf, thr_p=thr_p: E.activation(
                self.sg2[:, buf, :], self.wg_ring[:, buf, :], AF.Sign, bias=thr_p[:, 0:1]), inc="s")
            csg = self.cnt["s"]
            self.wait("vector", "s", csg)
            self.wait("vector", "t", qlast[("tr", buf)])
            self.op("vector", lambda E, buf=buf: E.tensor_add(
                self.wq2[:, buf, :], self.sg1[:, buf, :], self.sg2[:, buf, :]), inc="v")
            cwq = self.cnt["v"]
            qlast[("add", buf)] = cwq
            if buf == 0:
                ctr = self.transpose8(lambda cc, buf=buf: self.wq2[:, buf, cc * 128:(cc + 1) * 128], [("v", cwq)])
                qlast[("tr", buf)] = ctr
                self.evac_psT(dst_fn())
            else:
                pb = self.psB[:, :].bitcast(BF16)
                self.pe_acquire("psB")
                self.wait("tensor", "v", cwq)
                for cc in range(8):
                    self.op("tensor", lambda E, cc=cc, buf=buf, pb=pb: E.transpose(
                        pb[:, cc * 128:(cc + 1) * 128],
                        self.wq2[:, buf, cc * 128:(cc + 1) * 128], self.ident[:, :]),
                        inc="t" if cc == 7 else None)
                self.pe_done("psB")
                qlast[("tr", buf)] = self.cnt["t"]
                self.rd_wait("vector", "psB")
                self.op("vector", lambda E, pb=pb: E.tensor_copy(
                    dst_fn(), pb.rearrange("p (c f) -> p c f", c=8)), inc="v")
                self.rd_done("psB")
            return self.cnt["v"]

        for i in range(32):
            quant_tile(i, self.wfc_d[i * 128:(i + 1) * 128, :], self.thr_fc, self.thr_fcn,
                       (lambda i: lambda: self.wfcT[:, :, i * 128:(i + 1) * 128])(i))
        idx = 32
        for i in range(8):
            for j in range(4):
                quant_tile(idx, self.wproj_d[i * 128:(i + 1) * 128, j * 1024:(j + 1) * 1024],
                           self.thr_pj, self.thr_pjn,
                           (lambda i, j: lambda: self.wprojT[:, j * 8:(j + 1) * 8, i * 128:(i + 1) * 128])(i, j))
                idx += 1
        cwT = self.cnt["v"]

        self.pe_acquire("psW")
        self.wait("tensor", "v", cwT)
        self.wait("tensor", "v", ccol)
        for dhalf in range(2):
            for cc in range(32):
                self.op("tensor", lambda E, cc=cc, dhalf=dhalf: E.matmul(
                    self.psW[0:1, dhalf * 512:(dhalf + 1) * 512],
                    self.lnbcol[:, cc:cc + 1],
                    self.wprojT[:, cc, dhalf * 512:(dhalf + 1) * 512],
                    start=(cc == 0), stop=(cc == 31)),
                    inc="t" if (dhalf == 1 and cc == 31) else None)
        self.pe_done("psW")
        self.rd_wait("vector", "psW")
        self.op("vector", lambda E: E.tensor_copy(self.c_row[:, :], self.psW[0:1, :]), inc="v")
        c_row_done = self.cnt["v"]
        self.rd_done("psW")

        cfold = None
        self.wait("vector", "t", self.cnt["t"])
        for cc in range(32):
            self.op("vector", lambda E, cc=cc: E.tensor_scalar_mul(
                self.wprojT[:, cc, :], self.wprojT[:, cc, :], self.lnwcol[:, cc:cc + 1]),
                inc="v" if cc == 31 else None)
        cfold = self.cnt["v"]
        self.misc["wprojT_ready"] = cfold

        for half in range(2):
            self.pe_acquire("psS")
            self.wait("tensor", "v", c_row_done)
            self.op("tensor", lambda E, half=half: E.matmul(
                self.psS[:, 0:512], self.ones_row_f[:, :],
                self.c_row[:, half * 512:(half + 1) * 512], start=True, stop=True), inc="t")
            self.pe_done("psS")
            self.rd_wait("vector", "psS")
            self.op("vector", lambda E, half=half: E.tensor_copy(
                self.c_bc[:, half * 512:(half + 1) * 512], self.psS[:, 0:512]), inc="v")
            self.rd_done("psS")
        self.misc["c_bc_ready"] = self.cnt["v"]
        self.flush()

    def plan_phaseB(self):
        self.q_scope.close()
        self.h_sb = self._alloc(self.es, "h_sb", [128, 2, HF], BF16, side="right")
        self.hsums = self._alloc(self.es, "hsums", [128, 2, 4], F32, side="right")
        self.hsq4 = self._alloc(self.es, "hsq4", [128, 2, 4], F32, side="right")
        self.hlnT = self._alloc(self.es, "hlnT", [128, 32, 128], BF16, side="right")
        self.begin()
        ln4_pend = {}
        for t in range(NT):
            par = t % 2
            xb = t % self.XR
            dsem = "dx%d" % xb
            self.wait("sync", "dsp%d" % (t % 2), 16 * (t // 2 + 1))
            if t >= self.XR:
                self.wait("sync", "dout%d" % xb, self.misc.get(("out_tile", t - self.XR), 0))
            self.op("sync", lambda E, t=t, xb=xb: E.dma_start(
                self.x_ring[:, xb, :], self.x2sp_d[t * 128:(t + 1) * 128, :]), inc16=dsem)
            cl = self.cnt[dsem]

            self.wait("vector", dsem, cl)
            self.wait("vector", "t", self.misc.get(("xlnT_done", par), 0))
            self.op("vector", lambda E, xb=xb, par=par, t=t: E.tensor_scalar(
                self.xln[:, par, :], self.x_ring[:, xb, :],
                self.st["mu3"][:, t:t + 1], self.st["r3"][:, t:t + 1],
                ALU.subtract, ALU.mult), inc="v")
            capp = self.cnt["v"]

            ctr = self.transpose8(lambda cc, par=par: self.xln[:, par, cc * 128:(cc + 1) * 128],
                                  [("v", capp)])
            self.misc[("xlnT_done", par)] = ctr
            cxT = self.evac_psT(self.xlnT[:, par, :, :],
                                pre_waits=[("t", self.misc.get(("xlnT_rd", par), 0))])

            gelu_done = []
            for grp in range(4):
                use_w = (grp % 2 == 0)
                pnames = ("psW",) if use_w else ("ps0", "ps1")
                for pn in pnames:
                    self.pe_acquire(pn)
                self.wait("tensor", "v", cxT)
                for half in range(2):
                    for cc in range(8):
                        if use_w:
                            self.op("tensor", lambda E, par=par, cc=cc, grp=grp, half=half: E.matmul(
                                self.psW[:, half * 512:(half + 1) * 512],
                                self.xlnT[:, par, cc, :],
                                self.wfcT[:, cc, grp * 1024 + half * 512:grp * 1024 + (half + 1) * 512],
                                start=(cc == 0), stop=(cc == 7)),
                                inc="t" if (half == 1 and cc == 7) else None)
                        else:
                            tgt = self.ps0 if half == 0 else self.ps1
                            self.op("tensor", lambda E, par=par, cc=cc, grp=grp, tgt=tgt, half=half: E.matmul(
                                tgt[:, 0:512],
                                self.xlnT[:, par, cc, :],
                                self.wfcT[:, cc, grp * 1024 + half * 512:grp * 1024 + (half + 1) * 512],
                                start=(cc == 0), stop=(cc == 7)),
                                inc="t" if (half == 1 and cc == 7) else None)
                self.pe_done(*pnames)
                self.misc[("xlnT_rd", par)] = self.cnt["t"]
                for pn in pnames:
                    self.rd_wait("scalar", pn)
                self.wait("scalar", "t", self.misc.get(("h_rd", par), 0))
                self.wait("scalar", "v", self.misc.get(("h_vrd", par), 0))
                if use_w:
                    self.op("scalar", lambda E, par=par, grp=grp: E.activation(
                        self.h_sb[:, par, grp * 1024:(grp + 1) * 1024], self.psW[:, :],
                        AF.Gelu, scale=0.5,
                        accum_out=self.hsums[:, par, grp:grp + 1]), inc="s")
                    self.rd_done("psW", "s", self.cnt["s"])
                else:
                    self.op("scalar", lambda E, par=par, grp=grp: E.activation(
                        self.h_sb[:, par, grp * 1024:grp * 1024 + 512], self.ps0[:, :],
                        AF.Gelu, scale=0.5, accum_out=self.hsums[:, par, grp:grp + 1]))
                    self.op("scalar", lambda E, par=par, grp=grp: E.activation(
                        self.h_sb[:, par, grp * 1024 + 512:(grp + 1) * 1024], self.ps1[:, :],
                        AF.Gelu, scale=0.5), inc="s")
                    self.rd_done("ps0", "s", self.cnt["s"])
                    self.rd_done("ps1", "s", self.cnt["s"])
                gelu_done.append(self.cnt["s"])
            cgel = gelu_done[-1]

            self.wait("vector", "s", cgel)
            self.op("vector", lambda E, par=par, t=t: E.reduce_sum(
                self.st["sxh"][:, t:t + 1], self.h_sb[:, par, :], AX.X), inc="v")
            cvh_ = self.cnt["v"]
            for grp in range(4):
                self.op("vector", lambda E, par=par, grp=grp: E.scalar_tensor_tensor(
                    self.sq_scr[:, :], self.h_sb[:, par, grp * 1024:(grp + 1) * 1024],
                    1.0, self.h_sb[:, par, grp * 1024:(grp + 1) * 1024],
                    ALU.mult, ALU.mult, accum_out=self.hsq4[:, par, grp:grp + 1]),
                    inc="v")
                self.wait("vector", "v", self.cnt["v"])
            self.op("vector", lambda E, par=par, t=t: E.reduce_sum(
                self.st["xxh"][:, t:t + 1], self.hsq4[:, par, :], AX.X), inc="v")
            cvh2 = self.cnt["v"]
            cmuh, crh = self.ln_finalize(self.st, "h", t, HF, max(cvh_, cvh2), None)

            if (t - 1) in ln4_pend:
                cv4, cs4 = ln4_pend.pop(t - 1)
                cmu4, cr4 = self.ln_finalize(self.st, "4", t - 1, D, cv4, cs4)
                self._plan_ln4_apply(t - 1, cmu4, cr4)

            self.wait("vector", "v", cmuh)
            self.wait("vector", "s", crh)
            self.op("vector", lambda E, par=par, t=t: E.tensor_scalar(
                self.h_sb[:, par, :], self.h_sb[:, par, :],
                self.st["muh"][:, t:t + 1], self.st["rh"][:, t:t + 1],
                ALU.subtract, ALU.mult), inc="v")
            chln = self.cnt["v"]
            self.misc[("h_vrd", par)] = chln

            hT_end = []
            for rnd in range(4):
                ctr = self.transpose8(
                    lambda cc, rnd=rnd, par=par: self.h_sb[:, par, (rnd * 8 + cc) * 128:(rnd * 8 + cc + 1) * 128],
                    [("v", chln)])
                self.misc[("h_rd", par)] = ctr
                cev = self.evac_psT(self.hlnT[:, rnd * 8:(rnd + 1) * 8, :],
                                    pre_waits=[("t", self.misc.get("hlnT_rd", 0))])
                hT_end.append(cev)

            self.pe_acquire("psW")
            self.wait("tensor", "v", hT_end[-1])
            self.wait("tensor", "v", self.misc.get("wprojT_ready", 0))
            for dhalf in range(2):
                for cc in range(32):
                    self.op("tensor", lambda E, cc=cc, dhalf=dhalf: E.matmul(
                        self.psW[:, dhalf * 512:(dhalf + 1) * 512],
                        self.hlnT[:, cc, :],
                        self.wprojT[:, cc, dhalf * 512:(dhalf + 1) * 512],
                        start=(cc == 0), stop=(cc == 31)),
                        inc="t" if (dhalf == 1 and cc == 31) else None)
            self.pe_done("psW")
            self.misc["hlnT_rd"] = self.cnt["t"]

            self.rd_wait("vector", "psW")
            self.wait("vector", "v", self.misc.get("c_bc_ready", 0))
            self.wait("vector", "v", self.misc.get(("y4_free", par), 0))
            self.op("vector", lambda E, par=par: E.tensor_add(
                self.y_att[:, par, :], self.psW[:, :], self.c_bc[:, :]), inc="v")
            cy4 = self.cnt["v"]
            self.rd_done("psW")

            self.wait("vector", "v", cy4)
            self.op("vector", lambda E, par=par, t=t: E.reduce_sum(
                self.st["sx4"][:, t:t + 1], self.y_att[:, par, :], AX.X), inc="v")
            cv4 = self.cnt["v"]
            self.op("vector", lambda E, par=par, t=t: E.scalar_tensor_tensor(
                self.sq_scr[:, :], self.y_att[:, par, :], 1.0, self.y_att[:, par, :],
                ALU.mult, ALU.mult, accum_out=self.st["xx4"][:, t:t + 1]), inc="v")
            cv4b = self.cnt["v"]
            ln4_pend[t] = (max(cv4, cv4b), None)

        cv4, cs4 = ln4_pend.pop(NT - 1)
        cmu4, cr4 = self.ln_finalize(self.st, "4", NT - 1, D, cv4, cs4)
        self._plan_ln4_apply(NT - 1, cmu4, cr4)
        for sl in range(self.XR):
            self.wait("sync", "dout%d" % sl, self.cnt["dout%d" % sl])
        self.flush()

    def _plan_ln4_apply(self, t, cmu4, cr4):
        par = t % 2
        xb = t % self.XR
        self.wait("vector", "v", cmu4)
        self.wait("vector", "s", cr4)
        self.wait("vector", "g", self.misc.get(("ln2t_freeB", par), 0))
        self.op("vector", lambda E, par=par, t=t: E.tensor_scalar(
            self.ln2tmp[:, par, :], self.y_att[:, par, :],
            self.st["mu4"][:, t:t + 1], self.st["r4"][:, t:t + 1],
            ALU.subtract, ALU.mult), inc="v")
        cln4 = self.cnt["v"]
        self.misc[("y4_free", par)] = cln4
        self.wait("gpsimd", "v", cln4)
        self.op("gpsimd", lambda E, xb=xb: E.tensor_add(
            self.x_ring[:, xb, :], self.x_ring[:, xb, :], self.ln2tmp[:, par, :]), inc="g")
        cadd = self.cnt["g"]
        self.misc[("outdone", t)] = cadd
        self.misc[("ln2t_freeB", par)] = cadd
        self.wait("sync", "g", cadd)
        self.op("sync", lambda E, t=t, xb=xb: E.dma_start(
            self.out_d[t * 128:(t + 1) * 128, :], self.x_ring[:, xb, :]), inc16="dout%d" % xb)
        self.misc[("out_tile", t)] = self.cnt["dout%d" % xb]

    def build(self):
        self.init_gamma()
        self.plan_setup()
        self.plan_prefix()
        self.plan_phaseA()
        self.plan_quant()
        self.plan_phaseB()
        return self.nc


_CACHED_NC = None


def _get_nc():
    global _CACHED_NC
    if _CACHED_NC is None:
        _CACHED_NC = Builder().build()
    return _CACHED_NC


def kernel(L0_attn_x, L0_attn_Wq, L0_attn_Wk, L0_attn_Wv, L0_attn_Wo,
           L0_attn_bo, L0_ffn_W_fc, L0_ffn_ln_w, L0_ffn_ln_b, L0_ffn_W_proj):
    X = np.ascontiguousarray(np.asarray(L0_attn_x, dtype=np.float32))
    Wq = np.ascontiguousarray(np.asarray(L0_attn_Wq, dtype=np.float32))
    Wk = np.ascontiguousarray(np.asarray(L0_attn_Wk, dtype=np.float32))
    Wv = np.ascontiguousarray(np.asarray(L0_attn_Wv, dtype=np.float32))
    Wo = np.ascontiguousarray(np.asarray(L0_attn_Wo, dtype=np.float32))
    bo = np.ascontiguousarray(np.asarray(L0_attn_bo, dtype=np.float32)).reshape(1, D)
    Wfc = np.ascontiguousarray(np.asarray(L0_ffn_W_fc, dtype=np.float32))
    lnw = np.ascontiguousarray(np.asarray(L0_ffn_ln_w, dtype=np.float32)).reshape(1, HF)
    lnb = np.ascontiguousarray(np.asarray(L0_ffn_ln_b, dtype=np.float32)).reshape(1, HF)
    Wpj = np.ascontiguousarray(np.asarray(L0_ffn_W_proj, dtype=np.float32))

    nc = _get_nc()
    in_maps = []
    for c in range(8):
        b, s = c // 2, c % 2
        xc = np.ascontiguousarray(X[b, s * NTOK:(s + 1) * NTOK, :])
        xp = np.ascontiguousarray(X[b, 0:NTOK, :])
        mc = np.full((128, 1), 1.0 if s == 1 else 0.0, np.float32)
        mb = np.full((128, 1), 0.0 if s == 1 else -200.0, np.float32)
        in_maps.append({
            "x": xc, "xprev": xp, "mc": mc, "mb": mb,
            "wq": Wq, "wk": Wk, "wv": Wv, "wo": Wo, "bo": bo,
            "wfc": Wfc, "lnw": lnw, "lnb": lnb, "wproj": Wpj,
        })
    res = run_bass_kernel_spmd(nc, in_maps, core_ids=list(range(8)))
    out = np.empty((4, 2, NTOK, D), np.float32)
    for c in range(8):
        out[c // 2, c % 2] = res.results[c]["out"]
    return out.reshape(4, 4096, D)


# version 10
# speedup vs baseline: 1.1489x; 1.0677x over previous
import numpy as np
from contextlib import ExitStack

import concourse.bass as bass
import concourse.mybir as mybir
from concourse.bass_utils import run_bass_kernel_spmd

F32 = mybir.dt.float32
BF16 = mybir.dt.bfloat16
AF = mybir.ActivationFunctionType
ALU = mybir.AluOpType
AX = mybir.AxisListType

D = 1024
HF = 4096
NT = 16
NTOK = NT * 128
LN_EPS = 1e-5
CTX_EPS = 1e-6
ABS_EPS = 1e-6

ENGINES = ("sync", "gpsimd", "vector", "scalar", "tensor")


class Builder:
    def __init__(self):
        nc = bass.Bass("TRN2", target_bir_lowering=False)
        self.nc = nc
        self.es = ExitStack()
        self.cnt = {}
        self.items = None
        self.ps_cons = {}
        self.ps_prod = {}
        self.misc = {}
        self.x_free = {}

        dt = nc.dram_tensor
        self.x_d = dt("x", [NTOK, D], F32, kind="ExternalInput").ap()
        self.xprev_d = dt("xprev", [NTOK, D], F32, kind="ExternalInput").ap()
        self.mc_d = dt("mc", [128, 1], F32, kind="ExternalInput").ap()
        self.mb_d = dt("mb", [128, 1], F32, kind="ExternalInput").ap()
        self.wq_d = dt("wq", [D, D], F32, kind="ExternalInput").ap()
        self.wk_d = dt("wk", [D, D], F32, kind="ExternalInput").ap()
        self.wv_d = dt("wv", [D, D], F32, kind="ExternalInput").ap()
        self.wo_d = dt("wo", [D, D], F32, kind="ExternalInput").ap()
        self.bo_d = dt("bo", [1, D], F32, kind="ExternalInput").ap()
        self.wfc_d = dt("wfc", [HF, D], F32, kind="ExternalInput").ap()
        self.lnw_d = dt("lnw", [1, HF], F32, kind="ExternalInput").ap()
        self.lnb_d = dt("lnb", [1, HF], F32, kind="ExternalInput").ap()
        self.wproj_d = dt("wproj", [D, HF], F32, kind="ExternalInput").ap()
        self.out_d = dt("out", [NTOK, D], F32, kind="ExternalOutput").ap()
        self.x2sp_d = dt("x2spill", [NTOK, D], F32).ap()

        names = ["dx", "dw", "dq0", "dq1", "dx0", "dx1", "dx2",
                 "dsp0", "dsp1", "dout0", "dout1", "dout2", "v", "s", "t", "g"]
        self.sems = {nm: self.es.enter_context(nc.semaphore(nm)) for nm in names}
        for nm in names:
            self.cnt[nm] = 0

        self.psA = self._alloc(self.es, "psA", [128, 512], F32, psum=True)
        self.psB = self._alloc(self.es, "psB", [128, 512], F32, psum=True)
        self.psW = self._alloc(self.es, "psW", [128, 1024], F32, psum=True)
        self.ps0 = self._alloc(self.es, "ps0", [128, 512], F32, psum=True)
        self.ps1 = self._alloc(self.es, "ps1", [128, 512], F32, psum=True)
        self.psT = self._alloc(self.es, "psT", [128, 1024], BF16, psum=True)
        self.psS = self._alloc(self.es, "psS", [128, 512], F32, psum=True)
        self.s_pack_started = set()

        sb = lambda n, s, d=F32: self._alloc(self.es, n, s, d)
        self.ident = sb("ident", [128, 128], BF16)
        self.ident_f = sb("ident_f", [32, 32])
        self.ones_row_b = sb("ones_row_b", [1, 128], BF16)
        self.zrow_b = sb("zrow_b", [1, 512], BF16)
        self.ones_col_f = sb("ones_col_f", [128, 1])
        self.ones_row_f = sb("ones_row_f", [1, 128])
        self.half_sc = sb("half_sc", [1, 1])
        self.mc_sb = sb("mc_sb", [128, 1])
        self.mb_sb = sb("mb_sb", [128, 1])
        self.XR = 3
        self.x_ring = sb("x_ring", [128, self.XR, D])
        self.xln = sb("xln", [128, 2, D], BF16)
        self.xlnT = sb("xlnT", [128, 2, 8, 128], BF16)
        self.y_att = sb("y_att", [128, 2, D])
        self.ln2tmp = sb("ln2tmp", [128, 2, D])
        self.sq_scr = sb("sq_scr", [128, D], BF16)
        self.st = {}
        for nm in ["sx1", "xx1", "mu1", "r1", "tm1",
                   "sx2", "xx2", "mu2", "r2", "tm2",
                   "sx3", "xx3", "mu3", "r3", "tm3",
                   "sx4", "xx4", "mu4", "r4", "tm4",
                   "sxh", "xxh", "muh", "rh", "tmh"]:
            self.st[nm] = sb("st_" + nm, [128, NT])
        self.gam_fc = sb("gam_fc", [128, 1])
        self.gam_pj = sb("gam_pj", [128, 1])
        self.gam_scr = sb("gam_scr", [128, 1])
        self.thr_fc = sb("thr_fc", [128, 1])
        self.thr_fcn = sb("thr_fcn", [128, 1])
        self.thr_pj = sb("thr_pj", [128, 1])
        self.thr_pjn = sb("thr_pjn", [128, 1])

        self.att_scope = ExitStack()
        sba = lambda n, s, d=BF16: self._alloc(self.att_scope, n, s, d)
        self.wq_sb = sba("wq_sb", [128, 8, D])
        self.wk_sb = sba("wk_sb", [128, 8, D])
        self.wv_sb = sba("wv_sb", [128, 8, D])
        self.wo_sb = sba("wo_sb", [128, 8, D])

        self.a_scope = ExitStack()
        sbA = lambda n, s, d=F32: self._alloc(self.a_scope, n, s, d)
        self.bo_row = sbA("bo_row", [1, D])
        self.bo_bc = sbA("bo_bc", [128, D])
        self.mask_g = sbA("mask_g", [128, 4, 256], BF16)
        self.mask_0 = sbA("mask_0", [128, 4, 256], BF16)
        self.qfull = sbA("qfull", [128, 2, D], BF16)
        self.qT = sbA("qT", [128, 2, 8, 128], BF16)
        self.expq = sbA("expq", [128, 768])
        self.zq = sbA("zq", [128, 2, 12])
        self.zqr = sbA("zqr", [128, 2, 12])
        self.kgl = sbA("kgl", [128, 2, 768], BF16)
        self.elu1 = sbA("elu1", [128, 768])
        self.elu2 = sbA("elu2", [128, 768], BF16)
        self.vhat = sbA("vhat", [128, 2, 12, 65], BF16)
        self.kloc = sbA("kloc", [128, 256], BF16)
        self.klocT = sbA("klocT", [128, 2, NT * 128 + 128], BF16)
        self.vloc = sbA("vloc", [128, NT + 1, 4, 65], BF16)
        self.expl = sbA("expl", [128, 4, 256], BF16)
        self.explT = sbA("explT", [128, 8, 128], BF16)
        self.attnl = sbA("attnl", [128, 256], BF16)
        self.zl = sbA("zl", [128, 4])
        self.zlr = sbA("zlr", [128, 4])
        self.ctxp = sbA("ctxp", [128, 2, 2, 192], BF16)
        self.zsn = sbA("zsn", [128, 2, 3])
        self.zsr = sbA("zsr", [128, 2, 3])
        self.attnT = sbA("attnT", [128, 2, 8, 128], BF16)
        self.x2t = sbA("x2t", [128, 2, D])
        self.stp = {}
        for nm in ["sx1p", "xx1p", "mu1p", "r1p", "tm1p"]:
            self.stp[nm] = sbA("st_" + nm, [128, NT])

        self.q_scope = ExitStack()
        sbQ = lambda n, s, d=F32: self._alloc(self.q_scope, n, s, d, side="right")
        self.wg_ring = sbQ("wg_ring", [128, 2, 1024])

    def _alloc(self, scope, name, shape, dtt, psum=False, side=None):
        f = self.nc.psum_tensor if psum else self.nc.sbuf_tensor
        kw = {"side": side} if side else {}
        return scope.enter_context(f(name, shape, dtt, **kw))

    def begin(self):
        self.items = {e: [] for e in ENGINES}

    def op(self, eng, fn, inc=None, inc16=None):
        incs = []
        ret = None
        if inc is not None:
            for nm in ([inc] if isinstance(inc, str) else inc):
                self.cnt[nm] += 1
                incs.append((nm, 1))
                if ret is None:
                    ret = self.cnt[nm]
        if inc16 is not None:
            self.cnt[inc16] += 16
            incs.append((inc16, 16))
            if ret is None:
                ret = self.cnt[inc16]
        self.items[eng].append(("op", fn, incs))
        return ret

    def wait(self, eng, sem, val):
        if val:
            self.items[eng].append(("wait", sem, val))

    def waits(self, eng, lst):
        for sem, val in lst:
            self.wait(eng, sem, val)

    def flush(self):
        items = self.items
        sems = self.sems
        with self.nc.Block() as blk:
            def mk(e):
                def body(E):
                    for it in items[e]:
                        if it[0] == "wait":
                            E.wait_ge(sems[it[1]], it[2])
                        else:
                            inst = it[1](E)
                            for nm, k in it[2]:
                                inst.then_inc(sems[nm], k)
                return body
            for e in ENGINES:
                if items[e]:
                    getattr(blk, e)(mk(e))
        self.items = None

    def pe_acquire(self, name):
        for sem, val in self.ps_cons.pop(name, []):
            self.wait("tensor", sem, val)

    def pe_done(self, *names):
        for name in names:
            self.ps_prod[name] = ("t", self.cnt["t"])

    def rd_wait(self, eng, name):
        p = self.ps_prod.get(name)
        if p:
            self.wait(eng, p[0], p[1])

    def rd_done(self, name, sem="v", val=None):
        if val is None:
            val = self.cnt[sem]
        self.ps_cons.setdefault(name, []).append((sem, val))

    def ln_stats(self, src_ap, st, pfx, col, width):
        self.op("vector", lambda E: E.reduce_sum(st["sx" + pfx][:, col:col + 1], src_ap, AX.X), inc="v")
        cv = self.cnt["v"]
        self.op("scalar", lambda E: E.activation(self.sq_scr[:, 0:width], src_ap, AF.Square,
                                                 accum_out=st["xx" + pfx][:, col:col + 1]), inc="s")
        return cv, self.cnt["s"]

    def ln_stats_vec(self, src_ap, st, pfx, col, scratch):
        self.op("vector", lambda E: E.reduce_sum(st["sx" + pfx][:, col:col + 1], src_ap, AX.X), inc="v")
        self.op("vector", lambda E: E.scalar_tensor_tensor(
            scratch, src_ap, 1.0, src_ap, ALU.mult, ALU.mult,
            accum_out=st["xx" + pfx][:, col:col + 1]), inc="v")
        return self.cnt["v"], None

    def ln_finalize(self, st, pfx, col, nelem, cv, cs):
        mus = st["mu" + pfx]; rs = st["r" + pfx]
        sxs = st["sx" + pfx]; xxs = st["xx" + pfx]; tms = st["tm" + pfx]
        c = slice(col, col + 1)
        self.wait("vector", "v", cv)
        if cs:
            self.wait("vector", "s", cs)
        self.op("vector", lambda E: E.tensor_scalar_mul(mus[:, c], sxs[:, c], 1.0 / nelem), inc="v")
        c1 = self.cnt["v"]
        self.wait("vector", "v", c1)
        self.op("vector", lambda E: E.tensor_mul(tms[:, c], mus[:, c], mus[:, c]), inc="v")
        self.wait("vector", "v", self.cnt["v"])
        self.op("vector", lambda E: E.scalar_tensor_tensor(
            tms[:, c], xxs[:, c], 1.0 / nelem, tms[:, c], ALU.mult, ALU.subtract), inc="v")
        self.wait("vector", "v", self.cnt["v"])
        self.op("vector", lambda E: E.tensor_scalar_add(tms[:, c], tms[:, c], LN_EPS), inc="v")
        c4 = self.cnt["v"]
        self.wait("scalar", "v", c4)
        self.op("scalar", lambda E: E.activation(rs[:, c], tms[:, c], AF.Ln), inc="s")
        self.wait("scalar", "s", self.cnt["s"])
        self.op("scalar", lambda E: E.activation(rs[:, c], rs[:, c], AF.Exp, scale=-0.5), inc="s")
        return c1, self.cnt["s"]

    def transpose8(self, src_getter, waitlist, n=8):
        self.pe_acquire("psT")
        self.waits("tensor", waitlist)
        for cc in range(n):
            self.op("tensor",
                    lambda E, cc=cc: E.transpose(self.psT[:, cc * 128:(cc + 1) * 128],
                                                 src_getter(cc), self.ident[:, :]),
                    inc="t" if cc == n - 1 else None)
        self.pe_done("psT")
        return self.cnt["t"]

    def evac_psT(self, dst_ap, n=8, pre_waits=()):
        self.rd_wait("vector", "psT")
        self.waits("vector", pre_waits)
        self.op("vector", lambda E: E.tensor_copy(
            dst_ap, self.psT[:, 0:n * 128].rearrange("p (c f) -> p c f", c=n)), inc="v")
        self.rd_done("psT")
        return self.cnt["v"]

    def plan_setup(self):
        self.begin()
        for w_sb, w_d in [(self.wq_sb, self.wq_d), (self.wo_sb, self.wo_d)]:
            self.op("gpsimd", lambda E, a=w_sb, b=w_d: E.dma_start(
                a[:, :, :], b.rearrange("(c p) f -> p c f", p=128)), inc16="dw")
        for w_sb, w_d in [(self.wk_sb, self.wk_d), (self.wv_sb, self.wv_d)]:
            self.op("gpsimd", lambda E, a=w_sb, b=w_d: E.dma_start(
                a[:, :, 0:768], b[:, 256:1024].rearrange("(c p) f -> p c f", p=128)), inc16="dw")
            self.op("gpsimd", lambda E, a=w_sb, b=w_d: E.dma_start(
                a[:, :, 768:1024], b[:, 0:256].rearrange("(c p) f -> p c f", p=128)), inc16="dw")
        self.op("sync", lambda E: E.dma_start(self.mc_sb[:, :], self.mc_d), inc16="dx")
        self.op("sync", lambda E: E.dma_start(self.mb_sb[:, :], self.mb_d), inc16="dx")
        self.op("sync", lambda E: E.dma_start(self.bo_row[:, :], self.bo_d), inc16="dx")
        cdx = self.cnt["dx"]

        g = "gpsimd"
        self.op(g, lambda E: E.memset(self.ident[:, :], 0.0), inc="g")
        self.wait(g, "g", self.cnt["g"])
        self.op(g, lambda E: E.affine_select(out=self.ident[:, :], in_=self.ident[:, :],
                                             compare_op=ALU.not_equal, fill=1.0, base=0,
                                             pattern=[[-1, 128]], channel_multiplier=1))
        self.op(g, lambda E: E.memset(self.ident_f[:, :], 0.0), inc="g")
        self.wait(g, "g", self.cnt["g"])
        self.op(g, lambda E: E.affine_select(out=self.ident_f[:, :], in_=self.ident_f[:, :],
                                             compare_op=ALU.not_equal, fill=1.0, base=0,
                                             pattern=[[-1, 32]], channel_multiplier=1))
        self.op(g, lambda E: E.memset(self.ones_col_f[:, :], 1.0))
        self.op(g, lambda E: E.memset(self.ones_row_f[:, :], 1.0))
        self.op(g, lambda E: E.memset(self.half_sc[:, :], 0.5))
        self.op(g, lambda E: E.memset(self.ones_row_b[:, :], 1.0))
        self.op(g, lambda E: E.memset(self.zrow_b[:, :], 0.0))
        self.op(g, lambda E: E.memset(self.vloc[:, :, :, :], 1.0))
        self.op(g, lambda E: E.memset(self.vhat[:, :, :, :], 1.0))
        self.op(g, lambda E: E.memset(self.mask_g[:, :, :], 1.0), inc="g")
        self.wait(g, "g", self.cnt["g"])
        for h in range(4):
            self.op(g, lambda E, h=h: E.affine_select(
                out=self.mask_g[:, h, 128:256], in_=self.mask_g[:, h, 128:256],
                compare_op=ALU.is_ge, fill=0.0, base=0,
                pattern=[[-1, 128]], channel_multiplier=1),
                inc="g" if h == 3 else None)
        cg = self.cnt["g"]

        self.wait("vector", "dx", cdx)
        self.wait("vector", "g", cg)
        self.op("vector", lambda E: E.tensor_copy(self.mask_0[:, :, :], self.mask_g[:, :, :]), inc="v")
        self.wait("vector", "v", self.cnt["v"])
        for h in range(4):
            self.op("vector", lambda E, h=h: E.tensor_scalar_mul(
                self.mask_0[:, h, 0:128], self.mask_0[:, h, 0:128], self.mc_sb[:, 0:1]),
                inc="v" if h == 3 else None)

        self.wait("tensor", "dx", cdx)
        self.wait("tensor", "g", cg)
        for half in range(2):
            self.pe_acquire("psS")
            self.op("tensor", lambda E, half=half: E.matmul(
                self.psS[:, 0:512], self.ones_row_f[:, :],
                self.bo_row[:, half * 512:(half + 1) * 512], start=True, stop=True), inc="t")
            self.pe_done("psS")
            self.rd_wait("vector", "psS")
            self.op("vector", lambda E, half=half: E.tensor_copy(
                self.bo_bc[:, half * 512:(half + 1) * 512], self.psS[:, 0:512]), inc="v")
            self.rd_done("psS")
        self.flush()

    def init_gamma(self):
        tiles = []
        for i in range(32):
            tiles.append(("fc", self.wfc_d[i * 128:(i + 1) * 128, :]))
        for i in range(8):
            for j in range(4):
                tiles.append(("pj", self.wproj_d[i * 128:(i + 1) * 128, j * 1024:(j + 1) * 1024]))
        self.gamma_tiles = tiles
        self.gamma_i = 0
        self.gamma_loads = []
        self.gamma_red = []

    def plan_gamma(self, n):
        for _ in range(n):
            i = self.gamma_i
            if i >= len(self.gamma_tiles):
                return
            kind, ap = self.gamma_tiles[i]
            buf = i % 2
            if i >= 2:
                self.wait("scalar", "v", self.gamma_red[i - 2])
            self.op("scalar", lambda E, ap=ap, buf=buf: E.dma_start(self.wg_ring[:, buf, :], ap), inc16="dq%d" % buf)
            self.gamma_loads.append(self.cnt["dq%d" % buf])
            self.wait("vector", "dq%d" % buf, self.gamma_loads[i])
            tgt = self.gam_fc if kind == "fc" else self.gam_pj
            first = (i == 0) or (kind == "pj" and self.gamma_tiles[i - 1][0] == "fc")
            if first:
                self.op("vector", lambda E, buf=buf, tgt=tgt: E.tensor_reduce(
                    tgt[:, :], self.wg_ring[:, buf, :], AX.X, ALU.add,
                    apply_absolute_value=True), inc="v")
            else:
                self.wait("vector", "v", self.misc.get("gam_add", 0))
                self.op("vector", lambda E, buf=buf: E.tensor_reduce(
                    self.gam_scr[:, :], self.wg_ring[:, buf, :], AX.X, ALU.add,
                    apply_absolute_value=True), inc="v")
                self.wait("vector", "v", self.cnt["v"])
                self.op("vector", lambda E, tgt=tgt: E.tensor_add(tgt[:, :], tgt[:, :], self.gam_scr[:, :]), inc="v")
                self.misc["gam_add"] = self.cnt["v"]
            self.gamma_red.append(self.cnt["v"])
            self.gamma_i = i + 1

    def plan_chunk_S(self, par, r0, waitlist):
        self.waits("tensor", waitlist)
        for name in ("psA", "psB"):
            for sem, val in self.ps_cons.pop(name, []):
                self.wait("tensor", sem, val)
        for h in range(4, 16):
            pack = self.psA if h < 10 else self.psB
            pname = "psA" if h < 10 else "psB"
            s = (h - 4) % 6
            p0 = (s % 2) * 64
            c0 = (s // 2) * 65
            self.op("tensor", lambda E, pack=pack, p0=p0, c0=c0, par=par, r0=r0, h=h: E.matmul(
                pack[p0:p0 + 64, c0:c0 + 65],
                self.kgl[r0:r0 + 64, par, (h - 4) * 64:(h - 3) * 64],
                self.vhat[r0:r0 + 64, par, h - 4, :],
                start=False, stop=False, skip_group_check=True),
                inc="t" if h == 15 else None)
        self.pe_done("psA", "psB")
        return self.cnt["t"]

    def plan_snapshot(self, ring):
        self.wait("vector", "t", self.misc.get(("ctx_rd", ring), 0))
        for ipk, pack in enumerate((self.psA, self.psB)):
            pname = "psA" if ipk == 0 else "psB"
            self.rd_wait("vector", pname)
            self.op("vector", lambda E, pack=pack, ipk=ipk: E.tensor_scalar_add(
                self.zsn[:, ipk, :],
                pack[:, 0:195].rearrange("p (s e) -> p s e", s=3)[:, :, 64], CTX_EPS), inc="v")
            self.wait("vector", "v", self.cnt["v"])
            self.op("vector", lambda E, ipk=ipk: E.reciprocal(
                self.zsr[:, ipk, :], self.zsn[:, ipk, :]), inc="v")
            self.wait("vector", "v", self.cnt["v"])
            self.op("vector", lambda E, pack=pack, ipk=ipk, ring=ring: E.tensor_tensor(
                self.ctxp[:, ring, ipk, :].rearrange("p (s e) -> p s e", s=3),
                pack[:, 0:195].rearrange("p (s e) -> p s e", s=3)[:, :, 0:64],
                self.zsr[:, ipk, :].unsqueeze(2).broadcast_to([128, 3, 64]),
                ALU.mult), inc="v")
            self.rd_done(pname)
        return self.cnt["v"]

    def plan_prefix(self):
        self.begin()
        for pack in (self.psA, self.psB):
            self.op("tensor", lambda E, pack=pack: E.matmul(
                pack[:, 0:512], self.ones_row_b[:, :], self.zrow_b[:, :],
                start=True, stop=False, skip_group_check=True), inc="t")
        self.pe_done("psA", "psB")
        for p in range(NT):
            par = p % 2
            xb = p % self.XR
            if p >= self.XR:
                sem, val = self.x_free.get(p - self.XR, (None, 0))
                if sem:
                    self.wait("sync", sem, val)
            dsem = "dx%d" % xb
            self.op("sync", lambda E, p=p, xb=xb: E.dma_start(
                self.x_ring[:, xb, :], self.xprev_d[p * 128:(p + 1) * 128, :]), inc16=dsem)
            cl = self.cnt[dsem]

            self.wait("vector", dsem, cl)
            self.wait("scalar", dsem, cl)
            cv, cs = self.ln_stats(self.x_ring[:, xb, :], self.stp, "1p", p, D)
            cmu, cr = self.ln_finalize(self.stp, "1p", p, D, cv, cs)
            self.wait("vector", "v", cmu)
            self.wait("vector", "s", cr)
            self.wait("vector", "t", self.misc.get(("xlnT_done", par), 0))
            self.op("vector", lambda E, xb=xb, par=par, p=p: E.tensor_scalar(
                self.xln[:, par, :], self.x_ring[:, xb, :],
                self.stp["mu1p"][:, p:p + 1], self.stp["r1p"][:, p:p + 1],
                ALU.subtract, ALU.mult), inc="v")
            capp = self.cnt["v"]
            self.x_free[p] = ("v", capp)

            ctr = self.transpose8(lambda cc, par=par: self.xln[:, par, cc * 128:(cc + 1) * 128],
                                  [("v", capp)])
            self.misc[("xlnT_done", par)] = ctr
            cxT = self.evac_psT(self.xlnT[:, par, :, :],
                                pre_waits=[("t", self.misc.get(("xlnT_rd", par), 0))])

            self.wait("tensor", "v", cxT)
            self.wait("tensor", "dw", 96)
            self.pe_acquire("ps0")
            self.pe_acquire("ps1")
            for cc in range(8):
                self.op("tensor", lambda E, par=par, cc=cc: E.matmul(
                    self.ps0[:, 0:512], self.xlnT[:, par, cc, :], self.wk_sb[:, cc, 0:512],
                    start=(cc == 0), stop=(cc == 7)))
                self.op("tensor", lambda E, par=par, cc=cc: E.matmul(
                    self.ps1[:, 0:256], self.xlnT[:, par, cc, :], self.wk_sb[:, cc, 512:768],
                    start=(cc == 0), stop=(cc == 7)),
                    inc="t" if cc == 7 else None)
            self.pe_done("ps0", "ps1")
            self.pe_acquire("psW")
            for cc in range(8):
                self.op("tensor", lambda E, par=par, cc=cc: E.matmul(
                    self.psW[:, 0:512], self.xlnT[:, par, cc, :], self.wv_sb[:, cc, 0:512],
                    start=(cc == 0), stop=(cc == 7)))
                self.op("tensor", lambda E, par=par, cc=cc: E.matmul(
                    self.psW[:, 512:768], self.xlnT[:, par, cc, :], self.wv_sb[:, cc, 512:768],
                    start=(cc == 0), stop=(cc == 7)),
                    inc="t" if cc == 7 else None)
            self.pe_done("psW")
            self.misc[("xlnT_rd", par)] = self.cnt["t"]

            self.rd_wait("scalar", "ps0")
            self.rd_wait("scalar", "ps1")
            self.wait("scalar", "v", self.misc.get(("elu2_free", par), 0))
            self.op("scalar", lambda E: E.activation(self.elu1[:, 0:512], self.ps0[:, 0:512], AF.Relu, scale=-1.0))
            self.op("scalar", lambda E: E.activation(self.elu1[:, 512:768], self.ps1[:, 0:256], AF.Relu, scale=-1.0), inc="s")
            ce1 = self.cnt["s"]
            self.wait("scalar", "s", ce1)
            self.op("scalar", lambda E: E.activation(self.elu2[:, :], self.elu1[:, :], AF.Exp,
                                                     scale=-1.0, bias=self.mb_sb[:, 0:1]), inc="s")
            ce2 = self.cnt["s"]
            semv = self.misc.get(("kv_free", par))
            if semv:
                self.wait("vector", semv[0], semv[1])
            self.wait("vector", "s", ce2)
            self.op("vector", lambda E, par=par: E.tensor_scalar(
                self.kgl[:, par, 0:512], self.ps0[:, 0:512], 0.0, self.mc_sb[:, 0:1], ALU.max, ALU.mult))
            self.op("vector", lambda E, par=par: E.tensor_scalar(
                self.kgl[:, par, 512:768], self.ps1[:, 0:256], 0.0, self.mc_sb[:, 0:1], ALU.max, ALU.mult), inc="v")
            ckm = self.cnt["v"]
            self.rd_done("ps0")
            self.rd_done("ps1")
            self.wait("vector", "v", ckm)
            self.op("vector", lambda E, par=par: E.tensor_add(
                self.kgl[:, par, :], self.kgl[:, par, :], self.elu2[:, :]), inc="v")
            ckg = self.cnt["v"]
            self.misc[("elu2_free", par)] = ckg

            self.rd_wait("vector", "psW")
            self.op("vector", lambda E, par=par: E.tensor_copy(
                self.vhat[:, par, :, 0:64],
                self.psW[:, 0:768].rearrange("p (h e) -> p h e", h=12)), inc="v")
            cvh = self.cnt["v"]
            self.rd_done("psW")

            if p == NT - 1:
                self.pe_acquire("ps1")
                self.pe_acquire("psW")
                self.wait("tensor", "v", cxT)
                for cc in range(8):
                    self.op("tensor", lambda E, par=par, cc=cc: E.matmul(
                        self.ps1[:, 256:512], self.xlnT[:, par, cc, :], self.wk_sb[:, cc, 768:1024],
                        start=(cc == 0), stop=(cc == 7)))
                    self.op("tensor", lambda E, par=par, cc=cc: E.matmul(
                        self.psW[:, 768:1024], self.xlnT[:, par, cc, :], self.wv_sb[:, cc, 768:1024],
                        start=(cc == 0), stop=(cc == 7)),
                        inc="t" if cc == 7 else None)
                self.pe_done("ps1", "psW")
                self.misc[("xlnT_rd", par)] = self.cnt["t"]
                self.rd_wait("vector", "ps1")
                self.op("vector", lambda E: E.tensor_copy(self.kloc[:, :], self.ps1[:, 256:512]), inc="v")
                ckl = self.cnt["v"]
                self.rd_done("ps1")
                ctr = self.transpose8(lambda cc: self.kloc[:, cc * 128:(cc + 1) * 128],
                                      [("v", ckl)], n=2)
                self.rd_wait("vector", "psT")
                self.op("vector", lambda E: E.tensor_copy(self.klocT[:, 0, 0:128], self.psT[:, 0:128]))
                self.op("vector", lambda E: E.tensor_copy(self.klocT[:, 1, 0:128], self.psT[:, 128:256]), inc="v")
                self.rd_done("psT")
                self.rd_wait("vector", "psW")
                self.op("vector", lambda E: E.tensor_copy(
                    self.vloc[:, 0, :, 0:64],
                    self.psW[:, 768:1024].rearrange("p (h e) -> p h e", h=4)), inc="v")
                chv = self.cnt["v"]
                self.rd_done("psW")
                self.misc["halo_ready"] = chv

            for half in range(2):
                self.plan_chunk_S(par, half * 64, [("v", ckg), ("v", cvh)])
            self.misc[("kv_free", par)] = ("t", self.cnt["t"])

            self.plan_gamma(2)

        csn = self.plan_snapshot(1)
        self.misc[("ctx_ready", 1)] = csn
        self.flush()

    def plan_phaseA(self, ntiles=NT, upto=99):
        self.begin()
        for t in range(ntiles):
            par = t % 2
            xb = t % self.XR
            if t < self.XR:
                pv = NT - self.XR + t
                sem, val = self.x_free.get(pv, (None, 0))
                if sem:
                    self.wait("sync", sem, val)
            else:
                self.wait("sync", "g", self.misc.get(("xdone", t - self.XR), 0))
            dsem = "dx%d" % xb
            self.op("sync", lambda E, t=t, xb=xb: E.dma_start(
                self.x_ring[:, xb, :], self.x_d[t * 128:(t + 1) * 128, :]), inc16=dsem)
            cl = self.cnt[dsem]

            self.wait("vector", dsem, cl)
            self.wait("scalar", dsem, cl)
            cv, cs = self.ln_stats(self.x_ring[:, xb, :], self.st, "1", t, D)
            cmu, cr = self.ln_finalize(self.st, "1", t, D, cv, cs)
            self.wait("vector", "v", cmu)
            self.wait("vector", "s", cr)
            self.wait("vector", "t", self.misc.get(("xlnT_done", par), 0))
            self.op("vector", lambda E, xb=xb, par=par, t=t: E.tensor_scalar(
                self.xln[:, par, :], self.x_ring[:, xb, :],
                self.st["mu1"][:, t:t + 1], self.st["r1"][:, t:t + 1],
                ALU.subtract, ALU.mult), inc="v")
            capp = self.cnt["v"]

            ctr = self.transpose8(lambda cc, par=par: self.xln[:, par, cc * 128:(cc + 1) * 128],
                                  [("v", capp)])
            self.misc[("xlnT_done", par)] = ctr
            cxT = self.evac_psT(self.xlnT[:, par, :, :],
                                pre_waits=[("t", self.misc.get(("xlnT_rd", par), 0))])

            if upto < 2:
                continue
            self.wait("tensor", "v", cxT)
            self.wait("tensor", "dw", 96)
            self.pe_acquire("psW")
            for cc in range(8):
                self.op("tensor", lambda E, par=par, cc=cc: E.matmul(
                    self.psW[:, 0:512], self.xlnT[:, par, cc, :], self.wq_sb[:, cc, 0:512],
                    start=(cc == 0), stop=(cc == 7)))
                self.op("tensor", lambda E, par=par, cc=cc: E.matmul(
                    self.psW[:, 512:1024], self.xlnT[:, par, cc, :], self.wq_sb[:, cc, 512:1024],
                    start=(cc == 0), stop=(cc == 7)),
                    inc="t" if cc == 7 else None)
            self.pe_done("psW")

            self.pe_acquire("ps0")
            self.pe_acquire("ps1")
            for cc in range(8):
                self.op("tensor", lambda E, par=par, cc=cc: E.matmul(
                    self.ps0[:, 0:512], self.xlnT[:, par, cc, :], self.wk_sb[:, cc, 0:512],
                    start=(cc == 0), stop=(cc == 7)))
                self.op("tensor", lambda E, par=par, cc=cc: E.matmul(
                    self.ps1[:, 0:512], self.xlnT[:, par, cc, :], self.wk_sb[:, cc, 512:1024],
                    start=(cc == 0), stop=(cc == 7)),
                    inc="t" if cc == 7 else None)
            self.pe_done("ps0", "ps1")

            self.rd_wait("scalar", "psW")
            self.wait("scalar", "v", self.misc.get(("expq_free", par), 0))
            self.op("scalar", lambda E: E.activation(self.expq[:, :], self.psW[:, 256:1024], AF.Exp), inc="s")
            cexq = self.cnt["s"]
            self.rd_done("psW", "s", cexq)
            self.wait("vector", "s", cexq)
            self.op("vector", lambda E, par=par: E.tensor_reduce(
                self.zq[:, par, :], self.expq[:, :].rearrange("p (h e) -> p h e", h=12),
                AX.X, ALU.add), inc="v")
            self.wait("vector", "v", self.cnt["v"])
            self.op("vector", lambda E, par=par: E.reciprocal(self.zqr[:, par, :], self.zq[:, par, :]), inc="v")
            czr = self.cnt["v"]
            self.rd_wait("vector", "psW")
            self.wait("vector", "s", cexq)
            self.wait("vector", "t", self.misc.get(("qT_done", par), 0))
            self.op("vector", lambda E, par=par: E.tensor_scalar_mul(
                self.qfull[:, par, 0:256], self.psW[:, 0:256], 0.125), inc="v")
            clq = self.cnt["v"]
            self.rd_done("psW")
            self.wait("vector", "v", czr)
            self.op("vector", lambda E, par=par: E.tensor_tensor(
                self.qfull[:, par, 256:1024].rearrange("p (h e) -> p h e", h=12),
                self.expq[:, :].rearrange("p (h e) -> p h e", h=12),
                self.zqr[:, par, :].unsqueeze(2).broadcast_to([128, 12, 64]),
                ALU.mult), inc="v")
            cqf = self.cnt["v"]
            self.misc[("expq_free", par)] = cqf

            if upto < 3:
                continue
            ctr = self.transpose8(lambda cc, par=par: self.qfull[:, par, cc * 128:(cc + 1) * 128],
                                  [("v", cqf), ("v", clq)])
            self.misc[("qT_done", par)] = ctr
            cqT = self.evac_psT(self.qT[:, par, :, :],
                                pre_waits=[("t", self.misc.get(("qT_rd", par), 0))])

            if upto < 4:
                continue
            self.rd_wait("scalar", "ps0")
            self.rd_wait("scalar", "ps1")
            self.wait("scalar", "v", self.misc.get(("elu2_free", par), 0))
            self.op("scalar", lambda E: E.activation(self.elu1[:, 0:512], self.ps0[:, 0:512], AF.Relu, scale=-1.0))
            self.op("scalar", lambda E: E.activation(self.elu1[:, 512:768], self.ps1[:, 0:256], AF.Relu, scale=-1.0), inc="s")
            ce1 = self.cnt["s"]
            self.wait("scalar", "s", ce1)
            self.op("scalar", lambda E: E.activation(self.elu2[:, :], self.elu1[:, :], AF.Exp, scale=-1.0), inc="s")
            ce2 = self.cnt["s"]
            semv = self.misc.get(("kv_free", par))
            if semv:
                self.wait("vector", semv[0], semv[1])
            self.wait("vector", "s", ce2)
            self.op("vector", lambda E, par=par: E.scalar_tensor_tensor(
                self.kgl[:, par, 0:512], self.ps0[:, 0:512], 0.0, self.elu2[:, 0:512], ALU.max, ALU.add))
            self.op("vector", lambda E, par=par: E.scalar_tensor_tensor(
                self.kgl[:, par, 512:768], self.ps1[:, 0:256], 0.0, self.elu2[:, 512:768], ALU.max, ALU.add), inc="v")
            ckg = self.cnt["v"]
            self.misc[("elu2_free", par)] = ckg
            self.rd_done("ps0")
            self.wait("vector", "t", self.misc.get("kloc_rd", 0))
            self.op("vector", lambda E: E.tensor_copy(self.kloc[:, :], self.ps1[:, 256:512]), inc="v")
            ckl = self.cnt["v"]
            self.rd_done("ps1")
            ctr = self.transpose8(lambda cc: self.kloc[:, cc * 128:(cc + 1) * 128],
                                  [("v", ckl)], n=2)
            self.misc["kloc_rd"] = ctr
            self.rd_wait("vector", "psT")
            col = 128 + t * 128
            self.op("vector", lambda E, col=col: E.tensor_copy(self.klocT[:, 0, col:col + 128], self.psT[:, 0:128]))
            self.op("vector", lambda E, col=col: E.tensor_copy(self.klocT[:, 1, col:col + 128], self.psT[:, 128:256]), inc="v")
            ckT = self.cnt["v"]
            self.rd_done("psT")

            if upto < 5:
                continue
            self.pe_acquire("psW")
            for cc in range(8):
                self.op("tensor", lambda E, par=par, cc=cc: E.matmul(
                    self.psW[:, 0:512], self.xlnT[:, par, cc, :], self.wv_sb[:, cc, 0:512],
                    start=(cc == 0), stop=(cc == 7)))
                self.op("tensor", lambda E, par=par, cc=cc: E.matmul(
                    self.psW[:, 512:1024], self.xlnT[:, par, cc, :], self.wv_sb[:, cc, 512:1024],
                    start=(cc == 0), stop=(cc == 7)),
                    inc="t" if cc == 7 else None)
            self.pe_done("psW")
            self.misc[("xlnT_rd", par)] = self.cnt["t"]
            self.rd_wait("vector", "psW")
            semv = self.misc.get(("kv_free", par))
            if semv:
                self.wait("vector", semv[0], semv[1])
            self.op("vector", lambda E, par=par: E.tensor_copy(
                self.vhat[:, par, :, 0:64],
                self.psW[:, 0:768].rearrange("p (h e) -> p h e", h=12)), inc="v")
            cvh = self.cnt["v"]
            self.op("vector", lambda E, t=t: E.tensor_copy(
                self.vloc[:, t + 1, :, 0:64],
                self.psW[:, 768:1024].rearrange("p (h e) -> p h e", h=4)), inc="v")
            cvl = self.cnt["v"]
            self.rd_done("psW")

            if upto < 6:
                continue
            self.pe_acquire("ps0")
            self.pe_acquire("ps1")
            self.wait("tensor", "v", cqT)
            self.wait("tensor", "v", ckT)
            if t == 0:
                self.wait("tensor", "v", self.misc.get("halo_ready", 0))
            for h in range(4):
                tgt = self.ps0 if h < 2 else self.ps1
                coff = (h % 2) * 256
                p0 = (h % 2) * 64
                if p0 == 0:
                    self.op("tensor", lambda E, h=h, tgt=tgt, coff=coff, par=par, t=t: E.matmul(
                        tgt[:, coff:coff + 256],
                        self.qT[0:64, par, h // 2, :],
                        self.klocT[0:64, h // 2, t * 128:t * 128 + 256],
                        start=True, stop=True))
                else:
                    for mh in range(2):
                        self.op("tensor", lambda E, h=h, tgt=tgt, coff=coff, par=par, t=t, mh=mh: E.matmul(
                            tgt[mh * 64:(mh + 1) * 64, coff:coff + 256],
                            self.qT[64:128, par, h // 2, mh * 64:(mh + 1) * 64],
                            self.klocT[64:128, h // 2, t * 128:t * 128 + 256],
                            start=True, stop=True),
                            inc="t" if (h == 3 and mh == 1) else None)
            self.pe_done("ps0", "ps1")
            if upto < 61:
                continue

            self.rd_wait("scalar", "ps0")
            self.rd_wait("scalar", "ps1")
            self.wait("scalar", "t", self.misc.get("expl_rd", 0))
            self.op("scalar", lambda E: E.activation(
                self.expl[:, 0:2, :].rearrange("p h e -> p (h e)"), self.ps0[:, :], AF.Exp))
            self.op("scalar", lambda E: E.activation(
                self.expl[:, 2:4, :].rearrange("p h e -> p (h e)"), self.ps1[:, :], AF.Exp), inc="s")
            cexl = self.cnt["s"]
            self.rd_done("ps0", "s", cexl)
            self.rd_done("ps1", "s", cexl)
            if upto < 62:
                continue
            self.wait("vector", "s", cexl)
            msk = self.mask_0 if t == 0 else self.mask_g
            self.op("vector", lambda E, msk=msk: E.tensor_tensor(
                self.expl[:, :, :], self.expl[:, :, :], msk[:, :, :], ALU.mult), inc="v")
            cmsk = self.cnt["v"]
            if upto < 63:
                continue

            ctr = self.transpose8(lambda cc: self.expl[:, cc // 2, (cc % 2) * 128:(cc % 2) * 128 + 128],
                                  [("v", cmsk)])
            self.misc["expl_rd"] = ctr
            ceT = self.evac_psT(self.explT[:, :, :],
                                pre_waits=[("t", self.misc.get("explT_rd", 0))])

            if upto < 7:
                continue
            self.pe_acquire("ps1")
            self.wait("tensor", "v", ceT)
            self.wait("tensor", "v", cvl)
            for h in range(4):
                for half in range(2):
                    self.op("tensor", lambda E, h=h, half=half, t=t: E.matmul(
                        self.ps1[:, h * 65:h * 65 + 65],
                        self.explT[:, 2 * h + half, :],
                        self.vloc[:, t + half, h, :],
                        start=(half == 0), stop=(half == 1)),
                        inc="t" if (h == 3 and half == 1) else None)
            self.pe_done("ps1")
            self.misc["explT_rd"] = self.cnt["t"]

            self.rd_wait("vector", "ps1")
            self.op("vector", lambda E: E.tensor_copy(
                self.zl[:, :],
                self.ps1[:, 0:260].rearrange("p (h e) -> p h e", h=4)[:, :, 64]), inc="v")
            self.wait("vector", "v", self.cnt["v"])
            self.op("vector", lambda E: E.reciprocal(self.zlr[:, :], self.zl[:, :]), inc="v")
            self.wait("vector", "v", self.cnt["v"])
            self.wait("vector", "t", self.misc.get("attnl_rd", 0))
            self.op("vector", lambda E: E.tensor_tensor(
                self.attnl[:, :].rearrange("p (h e) -> p h e", h=4),
                self.ps1[:, 0:260].rearrange("p (h e) -> p h e", h=4)[:, :, 0:64],
                self.zlr[:, :].unsqueeze(2).broadcast_to([128, 4, 64]),
                ALU.mult), inc="v")
            cloc = self.cnt["v"]
            self.rd_done("ps1")

            ctr = self.transpose8(lambda cc: self.attnl[:, cc * 128:(cc + 1) * 128],
                                  [("v", cloc)], n=2)
            self.misc["attnl_rd"] = ctr
            self.rd_wait("vector", "psT")
            self.wait("vector", "t", self.misc.get(("wo_done", par), 0))
            self.op("vector", lambda E, par=par: E.tensor_copy(self.attnT[:, par, 0, :], self.psT[:, 0:128]))
            self.op("vector", lambda E, par=par: E.tensor_copy(self.attnT[:, par, 1, :], self.psT[:, 128:256]), inc="v")
            catl = self.cnt["v"]
            self.rd_done("psT")

            if upto < 8:
                continue
            catg = []
            for half in range(2):
                u = 2 * t + half
                ring = u % 2
                prev = 1 - ring
                r0 = half * 64
                self.pe_acquire("ps0")
                self.wait("tensor", "v", self.misc.get(("ctx_ready", prev), 0))
                self.wait("tensor", "v", cqT)
                for h in range(4, 16):
                    ipk = 0 if h < 10 else 1
                    s = (h - 4) % 6
                    pr = (h - 4) // 2
                    self.op("tensor", lambda E, h=h, ipk=ipk, s=s, pr=pr, prev=prev, par=par, r0=r0: E.matmul(
                        self.ps0[((h - 4) % 2) * 64:((h - 4) % 2) * 64 + 64, pr * 64:pr * 64 + 64],
                        self.ctxp[(s % 2) * 64:(s % 2) * 64 + 64, prev, ipk, (s // 2) * 64:(s // 2) * 64 + 64],
                        self.qT[(h % 2) * 64:(h % 2) * 64 + 64, par, h // 2, r0:r0 + 64],
                        start=True, stop=True),
                        inc="t" if h == 15 else None)
                self.pe_done("ps0")
                self.misc[("ctx_rd", prev)] = self.cnt["t"]
                self.rd_wait("vector", "ps0")
                self.op("vector", lambda E, par=par, r0=r0: E.tensor_copy(
                    self.attnT[:, par, 2:8, r0:r0 + 64],
                    self.ps0[:, 0:384].rearrange("p (pr e) -> p pr e", pr=6)), inc="v")
                catg.append(self.cnt["v"])
                self.rd_done("ps0")

                self.plan_chunk_S(par, r0, [("v", ckg), ("v", cvh)])
                csn = self.plan_snapshot(ring)
                self.misc[("ctx_ready", ring)] = csn
            self.misc[("kv_free", par)] = ("t", self.cnt["t"])
            self.misc[("qT_rd", par)] = self.cnt["t"]

            if upto < 9:
                continue
            self.pe_acquire("psW")
            self.wait("tensor", "v", catl)
            for cv_ in catg:
                self.wait("tensor", "v", cv_)
            for cc in range(8):
                self.op("tensor", lambda E, par=par, cc=cc: E.matmul(
                    self.psW[:, 0:512], self.attnT[:, par, cc, :], self.wo_sb[:, cc, 0:512],
                    start=(cc == 0), stop=(cc == 7)))
                self.op("tensor", lambda E, par=par, cc=cc: E.matmul(
                    self.psW[:, 512:1024], self.attnT[:, par, cc, :], self.wo_sb[:, cc, 512:1024],
                    start=(cc == 0), stop=(cc == 7)),
                    inc="t" if cc == 7 else None)
            self.pe_done("psW")
            self.misc[("wo_done", par)] = self.cnt["t"]

            if upto < 10:
                continue
            self.rd_wait("vector", "psW")
            self.wait("vector", "s", self.misc.get(("yatt_sfree", par), 0))
            self.op("vector", lambda E, par=par: E.tensor_add(
                self.y_att[:, par, :], self.psW[:, :], self.bo_bc[:, :]), inc="v")
            cy = self.cnt["v"]
            self.rd_done("psW")
            self.wait("vector", "v", cy)
            self.wait("scalar", "v", cy)
            cv2, cs2 = self.ln_stats(self.y_att[:, par, :], self.st, "2", t, D)
            self.misc[("yatt_sfree", par)] = cs2
            cmu2, cr2 = self.ln_finalize(self.st, "2", t, D, cv2, cs2)
            self.wait("vector", "v", cmu2)
            self.wait("vector", "s", cr2)
            self.wait("vector", "g", self.misc.get(("ln2t_free", par), 0))
            self.op("vector", lambda E, par=par, t=t: E.tensor_scalar(
                self.ln2tmp[:, par, :], self.y_att[:, par, :],
                self.st["mu2"][:, t:t + 1], self.st["r2"][:, t:t + 1],
                ALU.subtract, ALU.mult), inc="v")
            cln2 = self.cnt["v"]
            self.wait("gpsimd", "v", cln2)
            self.wait("gpsimd", "dsp%d" % par, self.misc.get(("x2_spilled", par), 0))
            sv = self.misc.get(("x2_stats", par))
            if sv:
                self.wait("gpsimd", "v", sv[0])
                self.wait("gpsimd", "s", sv[1])
            self.op("gpsimd", lambda E, par=par, xb=xb: E.tensor_add(
                self.x2t[:, par, :], self.x_ring[:, xb, :], self.ln2tmp[:, par, :]), inc="g")
            cx2 = self.cnt["g"]
            self.misc[("xdone", t)] = cx2
            self.misc[("ln2t_free", par)] = cx2

            self.wait("sync", "g", cx2)
            self.op("sync", lambda E, par=par, t=t: E.dma_start(
                self.x2sp_d[t * 128:(t + 1) * 128, :], self.x2t[:, par, :]), inc16="dsp%d" % par)
            self.misc[("x2_spilled", par)] = self.cnt["dsp%d" % par]
            self.wait("vector", "g", cx2)
            self.wait("scalar", "g", cx2)
            cv3, cs3 = self.ln_stats(self.x2t[:, par, :], self.st, "3", t, D)
            self.misc[("x2_stats", par)] = (cv3, cs3)
            self.ln_finalize(self.st, "3", t, D, cv3, cs3)

            self.plan_gamma(2)

        self.flush()

    def plan_quant(self):
        self.a_scope.close()
        self.att_scope.close()
        sb = lambda n, s, d=F32: self._alloc(self.es, n, s, d)
        sbQ = lambda n, s, d=F32: self._alloc(self.q_scope, n, s, d, side="right")
        self.wfcT = sb("wfcT", [128, 8, HF], BF16)
        self.wprojT = sb("wprojT", [128, 32, D], BF16)
        self.c_bc = sb("c_bc", [128, D])
        self.lnwcol = sb("lnwcol", [128, 32])
        self.sg1 = sbQ("sg1", [128, 2, 1024], BF16)
        self.sg2 = sbQ("sg2", [128, 2, 1024], BF16)
        self.wq2 = sbQ("wq2", [128, 2, 1024], BF16)
        self.lnw_sb = sbQ("lnw_sb", [32, 128])
        self.lnb_sb = sbQ("lnb_sb", [32, 128])
        self.c_row = sbQ("c_row", [1, D])
        self.lnbcol = sbQ("lnbcol", [128, 32], BF16)
        self.tval = sbQ("tval", [1, 2])

        self.begin()
        for eng in ENGINES:
            self.wait(eng, "dsp0", self.cnt["dsp0"])
            self.wait(eng, "dsp1", self.cnt["dsp1"])
        self.plan_gamma(len(self.gamma_tiles) - self.gamma_i)
        cv_gam = self.gamma_red[-1]

        self.op("sync", lambda E: E.dma_start(
            self.lnw_sb[:, :], self.lnw_d.rearrange("o (r c) -> (o r) c", c=128)), inc16="dx")
        self.op("sync", lambda E: E.dma_start(
            self.lnb_sb[:, :], self.lnb_d.rearrange("o (r c) -> (o r) c", c=128)), inc16="dx")
        crow = self.cnt["dx"]

        self.pe_acquire("psS")
        self.wait("tensor", "v", cv_gam)
        self.op("tensor", lambda E: E.matmul(self.psS[0:1, 0:1], self.gam_fc[:, :], self.ones_col_f[:, :],
                                             start=True, stop=True))
        self.op("tensor", lambda E: E.matmul(self.psS[0:1, 1:2], self.gam_pj[:, :], self.ones_col_f[:, :],
                                             start=True, stop=True), inc="t")
        self.pe_done("psS")
        self.rd_wait("vector", "psS")
        self.op("vector", lambda E: E.tensor_scalar(
            self.tval[:, 0:2], self.psS[0:1, 0:2], 1.0 / (HF * D), ABS_EPS, ALU.mult, ALU.add), inc="v")
        self.rd_done("psS")
        self.wait("vector", "v", self.cnt["v"])
        self.op("vector", lambda E: E.tensor_scalar_mul(self.tval[:, 0:2], self.tval[:, 0:2], 0.5), inc="v")
        ctv = self.cnt["v"]
        self.pe_acquire("psS")
        self.wait("tensor", "v", ctv)
        self.op("tensor", lambda E: E.matmul(self.psS[:, 0:1], self.ones_row_f[:, :], self.tval[:, 0:1],
                                             start=True, stop=True))
        self.op("tensor", lambda E: E.matmul(self.psS[:, 1:2], self.ones_row_f[:, :], self.tval[:, 1:2],
                                             start=True, stop=True), inc="t")
        self.pe_done("psS")
        self.rd_wait("vector", "psS")
        self.op("vector", lambda E: E.tensor_copy(self.thr_fc[:, :], self.psS[:, 0:1]))
        self.op("vector", lambda E: E.tensor_copy(self.thr_pj[:, :], self.psS[:, 1:2]), inc="v")
        self.rd_done("psS")
        self.wait("vector", "v", self.cnt["v"])
        self.op("vector", lambda E: E.tensor_scalar_mul(self.thr_fcn[:, :], self.thr_fc[:, :], -1.0))
        self.op("vector", lambda E: E.tensor_scalar_mul(self.thr_pjn[:, :], self.thr_pj[:, :], -1.0), inc="v")
        cthr = self.cnt["v"]

        self.pe_acquire("psW")
        self.wait("tensor", "dx", crow)
        self.op("tensor", lambda E: E.transpose(self.psW[:, 0:32], self.lnw_sb[:, :], self.ident_f[:, :]))
        self.op("tensor", lambda E: E.transpose(self.psW[:, 32:64], self.lnb_sb[:, :], self.ident_f[:, :]), inc="t")
        self.pe_done("psW")
        self.rd_wait("vector", "psW")
        self.op("vector", lambda E: E.tensor_scalar_mul(self.lnwcol[:, :], self.psW[:, 0:32], 0.5))
        self.op("vector", lambda E: E.tensor_scalar_mul(self.lnbcol[:, :], self.psW[:, 32:64], 0.5), inc="v")
        ccol = self.cnt["v"]
        self.rd_done("psW")

        qlast = {("add", 0): 0, ("add", 1): 0, ("tr", 0): 0, ("tr", 1): 0}

        def quant_tile(idx, src_ap, thr_p, thr_n, dst_fn):
            buf = idx % 2
            self.op("scalar", lambda E, src_ap=src_ap, buf=buf: E.dma_start(
                self.wg_ring[:, buf, :], src_ap), inc16="dq%d" % buf)
            cld = self.cnt["dq%d" % buf]
            self.wait("scalar", "dq%d" % buf, cld)
            self.wait("scalar", "v", qlast[("add", buf)])
            if idx == 0:
                self.wait("scalar", "v", cthr)
            self.op("scalar", lambda E, buf=buf, thr_n=thr_n: E.activation(
                self.sg1[:, buf, :], self.wg_ring[:, buf, :], AF.Sign, bias=thr_n[:, 0:1]))
            self.op("scalar", lambda E, buf=buf, thr_p=thr_p: E.activation(
                self.sg2[:, buf, :], self.wg_ring[:, buf, :], AF.Sign, bias=thr_p[:, 0:1]), inc="s")
            csg = self.cnt["s"]
            self.wait("vector", "s", csg)
            self.wait("vector", "t", qlast[("tr", buf)])
            self.op("vector", lambda E, buf=buf: E.tensor_add(
                self.wq2[:, buf, :], self.sg1[:, buf, :], self.sg2[:, buf, :]), inc="v")
            cwq = self.cnt["v"]
            qlast[("add", buf)] = cwq
            if buf == 0:
                ctr = self.transpose8(lambda cc, buf=buf: self.wq2[:, buf, cc * 128:(cc + 1) * 128], [("v", cwq)])
                qlast[("tr", buf)] = ctr
                self.evac_psT(dst_fn())
            else:
                pb = self.psB[:, :].bitcast(BF16)
                self.pe_acquire("psB")
                self.wait("tensor", "v", cwq)
                for cc in range(8):
                    self.op("tensor", lambda E, cc=cc, buf=buf, pb=pb: E.transpose(
                        pb[:, cc * 128:(cc + 1) * 128],
                        self.wq2[:, buf, cc * 128:(cc + 1) * 128], self.ident[:, :]),
                        inc="t" if cc == 7 else None)
                self.pe_done("psB")
                qlast[("tr", buf)] = self.cnt["t"]
                self.rd_wait("vector", "psB")
                self.op("vector", lambda E, pb=pb: E.tensor_copy(
                    dst_fn(), pb.rearrange("p (c f) -> p c f", c=8)), inc="v")
                self.rd_done("psB")
            return self.cnt["v"]

        for i in range(32):
            quant_tile(i, self.wfc_d[i * 128:(i + 1) * 128, :], self.thr_fc, self.thr_fcn,
                       (lambda i: lambda: self.wfcT[:, :, i * 128:(i + 1) * 128])(i))
        idx = 32
        for i in range(8):
            for j in range(4):
                quant_tile(idx, self.wproj_d[i * 128:(i + 1) * 128, j * 1024:(j + 1) * 1024],
                           self.thr_pj, self.thr_pjn,
                           (lambda i, j: lambda: self.wprojT[:, j * 8:(j + 1) * 8, i * 128:(i + 1) * 128])(i, j))
                idx += 1
        cwT = self.cnt["v"]

        self.pe_acquire("psW")
        self.wait("tensor", "v", cwT)
        self.wait("tensor", "v", ccol)
        for dhalf in range(2):
            for cc in range(32):
                self.op("tensor", lambda E, cc=cc, dhalf=dhalf: E.matmul(
                    self.psW[0:1, dhalf * 512:(dhalf + 1) * 512],
                    self.lnbcol[:, cc:cc + 1],
                    self.wprojT[:, cc, dhalf * 512:(dhalf + 1) * 512],
                    start=(cc == 0), stop=(cc == 31)),
                    inc="t" if (dhalf == 1 and cc == 31) else None)
        self.pe_done("psW")
        self.rd_wait("vector", "psW")
        self.op("vector", lambda E: E.tensor_copy(self.c_row[:, :], self.psW[0:1, :]), inc="v")
        c_row_done = self.cnt["v"]
        self.rd_done("psW")

        cfold = None
        self.wait("vector", "t", self.cnt["t"])
        for cc in range(32):
            self.op("vector", lambda E, cc=cc: E.tensor_scalar_mul(
                self.wprojT[:, cc, :], self.wprojT[:, cc, :], self.lnwcol[:, cc:cc + 1]),
                inc="v" if cc == 31 else None)
        cfold = self.cnt["v"]
        self.misc["wprojT_ready"] = cfold

        for half in range(2):
            self.pe_acquire("psS")
            self.wait("tensor", "v", c_row_done)
            self.op("tensor", lambda E, half=half: E.matmul(
                self.psS[:, 0:512], self.ones_row_f[:, :],
                self.c_row[:, half * 512:(half + 1) * 512], start=True, stop=True), inc="t")
            self.pe_done("psS")
            self.rd_wait("vector", "psS")
            self.op("vector", lambda E, half=half: E.tensor_copy(
                self.c_bc[:, half * 512:(half + 1) * 512], self.psS[:, 0:512]), inc="v")
            self.rd_done("psS")
        self.misc["c_bc_ready"] = self.cnt["v"]
        self.flush()

    def plan_phaseB(self):
        self.q_scope.close()
        self.h_sb = self._alloc(self.es, "h_sb", [128, 2, HF], BF16, side="right")
        self.hsums = self._alloc(self.es, "hsums", [128, 2, 6], F32, side="right")
        self.hsq4 = self._alloc(self.es, "hsq4", [128, 2, 4], F32, side="right")
        self.hlnT = self._alloc(self.es, "hlnT", [128, 32, 128], BF16, side="right")
        self.pool_scr = self._alloc(self.es, "pool_scr", [128, 1024], BF16, side="right")
        self.begin()
        ln4_pend = {}
        for t in range(NT):
            par = t % 2
            xb = t % self.XR
            dsem = "dx%d" % xb
            self.wait("sync", "dsp%d" % (t % 2), 16 * (t // 2 + 1))
            if t >= self.XR:
                self.wait("sync", "dout%d" % xb, self.misc.get(("out_tile", t - self.XR), 0))
            self.op("sync", lambda E, t=t, xb=xb: E.dma_start(
                self.x_ring[:, xb, :], self.x2sp_d[t * 128:(t + 1) * 128, :]), inc16=dsem)
            cl = self.cnt[dsem]

            self.wait("vector", dsem, cl)
            self.wait("vector", "t", self.misc.get(("xlnT_done", par), 0))
            self.op("vector", lambda E, xb=xb, par=par, t=t: E.tensor_scalar(
                self.xln[:, par, :], self.x_ring[:, xb, :],
                self.st["mu3"][:, t:t + 1], self.st["r3"][:, t:t + 1],
                ALU.subtract, ALU.mult), inc="v")
            capp = self.cnt["v"]

            ctr = self.transpose8(lambda cc, par=par: self.xln[:, par, cc * 128:(cc + 1) * 128],
                                  [("v", capp)])
            self.misc[("xlnT_done", par)] = ctr
            cxT = self.evac_psT(self.xlnT[:, par, :, :],
                                pre_waits=[("t", self.misc.get(("xlnT_rd", par), 0))])

            gelu_done = []
            for grp in range(4):
                use_w = (grp % 2 == 0)
                pnames = ("psW",) if use_w else ("ps0", "ps1")
                for pn in pnames:
                    self.pe_acquire(pn)
                self.wait("tensor", "v", cxT)
                for half in range(2):
                    for cc in range(8):
                        if use_w:
                            self.op("tensor", lambda E, par=par, cc=cc, grp=grp, half=half: E.matmul(
                                self.psW[:, half * 512:(half + 1) * 512],
                                self.xlnT[:, par, cc, :],
                                self.wfcT[:, cc, grp * 1024 + half * 512:grp * 1024 + (half + 1) * 512],
                                start=(cc == 0), stop=(cc == 7)),
                                inc="t" if (half == 1 and cc == 7) else None)
                        else:
                            tgt = self.ps0 if half == 0 else self.ps1
                            self.op("tensor", lambda E, par=par, cc=cc, grp=grp, tgt=tgt, half=half: E.matmul(
                                tgt[:, 0:512],
                                self.xlnT[:, par, cc, :],
                                self.wfcT[:, cc, grp * 1024 + half * 512:grp * 1024 + (half + 1) * 512],
                                start=(cc == 0), stop=(cc == 7)),
                                inc="t" if (half == 1 and cc == 7) else None)
                self.pe_done(*pnames)
                self.misc[("xlnT_rd", par)] = self.cnt["t"]
                for pn in pnames:
                    self.rd_wait("scalar", pn)
                self.wait("scalar", "t", self.misc.get(("h_rd", par), 0))
                self.wait("scalar", "v", self.misc.get(("h_vrd", par), 0))
                if use_w:
                    self.op("scalar", lambda E, par=par, grp=grp: E.activation(
                        self.h_sb[:, par, grp * 1024:(grp + 1) * 1024], self.psW[:, :],
                        AF.Gelu, scale=0.5,
                        accum_out=self.hsums[:, par, grp // 2:grp // 2 + 1]), inc="s")
                    self.rd_done("psW", "s", self.cnt["s"])
                else:
                    self.op("scalar", lambda E, par=par, grp=grp: E.activation(
                        self.h_sb[:, par, grp * 1024:grp * 1024 + 512], self.ps0[:, :],
                        AF.Gelu, scale=0.5, accum_out=self.hsums[:, par, grp + 1:grp + 2]))
                    self.op("scalar", lambda E, par=par, grp=grp: E.activation(
                        self.h_sb[:, par, grp * 1024 + 512:(grp + 1) * 1024], self.ps1[:, :],
                        AF.Gelu, scale=0.5, accum_out=self.hsums[:, par, grp + 2:grp + 3]), inc="s")
                    self.rd_done("ps0", "s", self.cnt["s"])
                    self.rd_done("ps1", "s", self.cnt["s"])
                gelu_done.append(self.cnt["s"])
            cgel = gelu_done[-1]

            self.wait("vector", "s", cgel)
            self.op("vector", lambda E, par=par, t=t: E.reduce_sum(
                self.st["sxh"][:, t:t + 1], self.hsums[:, par, :], AX.X), inc="v")
            cvh_ = self.cnt["v"]
            for grp in range(4):
                self.op("vector", lambda E, par=par, grp=grp: E.scalar_tensor_tensor(
                    self.sq_scr[:, :], self.h_sb[:, par, grp * 1024:(grp + 1) * 1024],
                    1.0, self.h_sb[:, par, grp * 1024:(grp + 1) * 1024],
                    ALU.mult, ALU.mult, accum_out=self.hsq4[:, par, grp:grp + 1]),
                    inc="v")
                self.wait("vector", "v", self.cnt["v"])
            self.op("vector", lambda E, par=par, t=t: E.reduce_sum(
                self.st["xxh"][:, t:t + 1], self.hsq4[:, par, :], AX.X), inc="v")
            cvh2 = self.cnt["v"]
            cmuh, crh = self.ln_finalize(self.st, "h", t, HF, max(cvh_, cvh2), None)

            if (t - 1) in ln4_pend:
                cv4, cs4 = ln4_pend.pop(t - 1)
                cmu4, cr4 = self.ln_finalize(self.st, "4", t - 1, D, cv4, cs4)
                self._plan_ln4_apply(t - 1, cmu4, cr4)

            self.wait("vector", "v", cmuh)
            self.wait("vector", "s", crh)
            self.op("vector", lambda E, par=par, t=t: E.tensor_scalar(
                self.h_sb[:, par, :], self.h_sb[:, par, :],
                self.st["muh"][:, t:t + 1], self.st["rh"][:, t:t + 1],
                ALU.subtract, ALU.mult), inc="v")
            chln = self.cnt["v"]
            self.misc[("h_vrd", par)] = chln

            hT_end = []
            pb = self.psB[:, :].bitcast(BF16)
            for rnd in range(4):
                if rnd % 2 == 0:
                    ctr = self.transpose8(
                        lambda cc, rnd=rnd, par=par: self.h_sb[:, par, (rnd * 8 + cc) * 128:(rnd * 8 + cc + 1) * 128],
                        [("v", chln)])
                    self.misc[("h_rd", par)] = ctr
                    cev = self.evac_psT(self.hlnT[:, rnd * 8:(rnd + 1) * 8, :],
                                        pre_waits=[("t", self.misc.get("hlnT_rd", 0))])
                    hT_end.append(cev)
                else:
                    self.pe_acquire("psB")
                    self.wait("tensor", "v", chln)
                    for j in range(8):
                        self.op("tensor", lambda E, rnd=rnd, j=j, par=par, pb=pb: E.transpose(
                            pb[:, j * 128:(j + 1) * 128],
                            self.h_sb[:, par, (rnd * 8 + j) * 128:(rnd * 8 + j + 1) * 128],
                            self.ident[:, :]),
                            inc="t" if j == 7 else None)
                    self.pe_done("psB")
                    self.misc[("h_rd", par)] = self.cnt["t"]
                    self.rd_wait("vector", "psB")
                    self.waits("vector", [("t", self.misc.get("hlnT_rd", 0))])
                    self.op("vector", lambda E, rnd=rnd, pb=pb: E.tensor_copy(
                        self.hlnT[:, rnd * 8:(rnd + 1) * 8, :],
                        pb.rearrange("p (c f) -> p c f", c=8)), inc="v")
                    hT_end.append(self.cnt["v"])
                    self.rd_done("psB")

            self.pe_acquire("psW")
            self.wait("tensor", "v", hT_end[-1])
            self.wait("tensor", "v", self.misc.get("wprojT_ready", 0))
            for dhalf in range(2):
                for cc in range(32):
                    self.op("tensor", lambda E, cc=cc, dhalf=dhalf: E.matmul(
                        self.psW[:, dhalf * 512:(dhalf + 1) * 512],
                        self.hlnT[:, cc, :],
                        self.wprojT[:, cc, dhalf * 512:(dhalf + 1) * 512],
                        start=(cc == 0), stop=(cc == 31)),
                        inc="t" if (dhalf == 1 and cc == 31) else None)
            self.pe_done("psW")
            self.misc["hlnT_rd"] = self.cnt["t"]

            self.rd_wait("vector", "psW")
            self.wait("vector", "v", self.misc.get("c_bc_ready", 0))
            self.wait("vector", "v", self.misc.get(("y4_free", par), 0))
            self.op("vector", lambda E, par=par: E.tensor_add(
                self.y_att[:, par, :], self.psW[:, :], self.c_bc[:, :]), inc="v")
            cy4 = self.cnt["v"]
            self.rd_done("psW")

            self.wait("vector", "v", cy4)
            self.op("vector", lambda E, par=par, t=t: E.reduce_sum(
                self.st["sx4"][:, t:t + 1], self.y_att[:, par, :], AX.X), inc="v")
            cv4 = self.cnt["v"]
            self.op("vector", lambda E, par=par, t=t: E.scalar_tensor_tensor(
                self.sq_scr[:, :], self.y_att[:, par, :], 1.0, self.y_att[:, par, :],
                ALU.mult, ALU.mult, accum_out=self.st["xx4"][:, t:t + 1]), inc="v")
            cv4b = self.cnt["v"]
            ln4_pend[t] = (max(cv4, cv4b), None)

        cv4, cs4 = ln4_pend.pop(NT - 1)
        cmu4, cr4 = self.ln_finalize(self.st, "4", NT - 1, D, cv4, cs4)
        self._plan_ln4_apply(NT - 1, cmu4, cr4)
        for sl in range(self.XR):
            self.wait("sync", "dout%d" % sl, self.cnt["dout%d" % sl])
        self.flush()

    def _plan_ln4_apply(self, t, cmu4, cr4):
        par = t % 2
        xb = t % self.XR
        self.wait("vector", "v", cmu4)
        self.wait("vector", "s", cr4)
        self.wait("vector", "g", self.misc.get(("ln2t_freeB", par), 0))
        self.op("vector", lambda E, par=par, t=t: E.tensor_scalar(
            self.ln2tmp[:, par, :], self.y_att[:, par, :],
            self.st["mu4"][:, t:t + 1], self.st["r4"][:, t:t + 1],
            ALU.subtract, ALU.mult), inc="v")
        cln4 = self.cnt["v"]
        self.misc[("y4_free", par)] = cln4
        self.wait("gpsimd", "v", cln4)
        self.op("gpsimd", lambda E, xb=xb: E.tensor_add(
            self.x_ring[:, xb, :], self.x_ring[:, xb, :], self.ln2tmp[:, par, :]), inc="g")
        cadd = self.cnt["g"]
        self.misc[("outdone", t)] = cadd
        self.misc[("ln2t_freeB", par)] = cadd
        self.wait("sync", "g", cadd)
        self.op("sync", lambda E, t=t, xb=xb: E.dma_start(
            self.out_d[t * 128:(t + 1) * 128, :], self.x_ring[:, xb, :]), inc16="dout%d" % xb)
        self.misc[("out_tile", t)] = self.cnt["dout%d" % xb]

    def build(self):
        self.init_gamma()
        self.plan_setup()
        self.plan_prefix()
        self.plan_phaseA()
        self.plan_quant()
        self.plan_phaseB()
        return self.nc


_CACHED_NC = None


def _get_nc():
    global _CACHED_NC
    if _CACHED_NC is None:
        _CACHED_NC = Builder().build()
    return _CACHED_NC


def kernel(L0_attn_x, L0_attn_Wq, L0_attn_Wk, L0_attn_Wv, L0_attn_Wo,
           L0_attn_bo, L0_ffn_W_fc, L0_ffn_ln_w, L0_ffn_ln_b, L0_ffn_W_proj):
    X = np.ascontiguousarray(np.asarray(L0_attn_x, dtype=np.float32))
    Wq = np.ascontiguousarray(np.asarray(L0_attn_Wq, dtype=np.float32))
    Wk = np.ascontiguousarray(np.asarray(L0_attn_Wk, dtype=np.float32))
    Wv = np.ascontiguousarray(np.asarray(L0_attn_Wv, dtype=np.float32))
    Wo = np.ascontiguousarray(np.asarray(L0_attn_Wo, dtype=np.float32))
    bo = np.ascontiguousarray(np.asarray(L0_attn_bo, dtype=np.float32)).reshape(1, D)
    Wfc = np.ascontiguousarray(np.asarray(L0_ffn_W_fc, dtype=np.float32))
    lnw = np.ascontiguousarray(np.asarray(L0_ffn_ln_w, dtype=np.float32)).reshape(1, HF)
    lnb = np.ascontiguousarray(np.asarray(L0_ffn_ln_b, dtype=np.float32)).reshape(1, HF)
    Wpj = np.ascontiguousarray(np.asarray(L0_ffn_W_proj, dtype=np.float32))

    nc = _get_nc()
    in_maps = []
    for c in range(8):
        b, s = c // 2, c % 2
        xc = np.ascontiguousarray(X[b, s * NTOK:(s + 1) * NTOK, :])
        xp = np.ascontiguousarray(X[b, 0:NTOK, :])
        mc = np.full((128, 1), 1.0 if s == 1 else 0.0, np.float32)
        mb = np.full((128, 1), 0.0 if s == 1 else -200.0, np.float32)
        in_maps.append({
            "x": xc, "xprev": xp, "mc": mc, "mb": mb,
            "wq": Wq, "wk": Wk, "wv": Wv, "wo": Wo, "bo": bo,
            "wfc": Wfc, "lnw": lnw, "lnb": lnb, "wproj": Wpj,
        })
    res = run_bass_kernel_spmd(nc, in_maps, core_ids=list(range(8)))
    out = np.empty((4, 2, NTOK, D), np.float32)
    for c in range(8):
        out[c // 2, c % 2] = res.results[c]["out"]
    return out.reshape(4, 4096, D)
